# version 16
# baseline (speedup 1.0000x reference)
import numpy as np
from contextlib import ExitStack

import concourse.bass as bass
import concourse.mybir as mybir
from concourse.bass_utils import run_bass_kernel_spmd

F32 = mybir.dt.float32
BF16 = mybir.dt.bfloat16
I32 = mybir.dt.int32
AF = mybir.ActivationFunctionType
ALU = mybir.AluOpType

NCORES = 8
EPS = 1e-5
NEG_BIG = -1e30

FULL_CFG = dict(C=1024, DF=4096, NP=4, TW=512, LORA=16)


class Prog:
    ENGINES = ("sync", "tensor", "vector", "scalar", "gpsimd")

    def __init__(self, nc, es):
        self.nc = nc
        self.es = es
        self.events = {e: [] for e in self.ENGINES}
        self.sems = {}
        self.counts = {}

    def sem(self, name):
        self.sems[name] = self.es.enter_context(self.nc.semaphore(name))
        self.counts[name] = 0

    def emit(self, engine, fn, inc=None, amt=1, waits=()):
        waits = tuple(waits)
        post = None
        if inc is not None:
            self.counts[inc] += amt
            post = self.counts[inc]
        sems = self.sems

        def ev(eng, fn=fn, waits=waits, inc=inc, amt=amt):
            for s, v in waits:
                eng.wait_ge(sems[s], v)
            inst = fn(eng)
            if inc is not None:
                inst.then_inc(sems[inc], amt)
        self.events[engine].append(ev)
        return post

    def init_dma_pool(self, engine, n):
        names = []
        for i in range(n):
            nm = f"d_{engine}{i}"
            self.sem(nm)
            names.append(nm)
        self.dma_state = getattr(self, "dma_state", {})
        self.dma_state[engine] = {"sems": names, "rr": 0, "last": [None] * n}

    def dma(self, engine, out, in_, inc=None, waits=()):
        st = self.dma_state[engine]
        n = len(st["sems"])
        i = st["rr"] % n
        st["rr"] += 1
        sem = st["sems"][i]
        w = list(waits)
        if st["last"][i] is not None:
            w.append(st["last"][i])
        v = self.emit(engine, lambda eng, o=out, i_=in_: eng.dma_start(out=o, in_=i_),
                      inc=sem, amt=16, waits=w)
        st["last"][i] = (sem, v)
        return (sem, v)


class Arena:

    def __init__(self, nc, es, name, n_el, dtype):
        self.t = es.enter_context(nc.sbuf_tensor(name, [128, n_el], dtype))
        self.n_el = n_el
        self.off = 0

    def region(self, n):
        off = (self.off + 15) // 16 * 16
        assert off + n <= self.n_el, f"arena {self.t} overflow: {off}+{n}>{self.n_el}"
        self.off = off + n
        return off

    def view(self, off, shape):
        n = int(np.prod(shape))
        ap = self.t[:, off:off + n]
        if len(shape) == 2:
            ap = ap.rearrange("p (a b) -> p a b", a=shape[0], b=shape[1])
        elif len(shape) == 3:
            ap = ap.rearrange("p (a b c) -> p a b c", a=shape[0], b=shape[1], c=shape[2])
        return ap

    def alloc(self, shape):
        return self.view(self.region(int(np.prod(shape))), shape)


def build_program(cfg=FULL_CFG, debug=False):
    C, DF, NP, TW, LORA = cfg["C"], cfg["DF"], cfg["NP"], cfg["TW"], cfg["LORA"]
    JC = C // 128
    JF = DF // 128

    nc = bass.Bass("TRN2", target_bir_lowering=False, detect_race_conditions=False)
    es = ExitStack()
    p = Prog(nc, es)

    def din(name, shape, dt=F32):
        return nc.dram_tensor(name, shape, dt, kind="ExternalInput").ap()

    xin = din("xin", [NP, 128, JC, TW])
    xprev = din("xprev", [128, JC])
    prevvalid = din("prevvalid", [128, 1])
    maskpre = din("maskpre", [128, NCORES])
    maskbig = din("maskbig", [128, NCORES])
    maskrow = din("maskrow", [128, NCORES])

    W_INNER = {"wk": JC * C, "wv": JC * C, "wr": JC * C, "wo": JC * C,
               "fr": JC * C, "fk": JC * DF, "fv": JF * C,
               "wfc": JC * DF, "wproj": JF * C,
               "afc": JC * LORA, "aproj": JF * LORA}
    w_in = {nm: din(nm, [128, inner]) for nm, inner in W_INNER.items()}
    w_in["bfc"] = din("bfc", [LORA, DF])
    w_in["bproj"] = din("bproj", [LORA, C])

    PVN = ["ln1g", "ln1b", "ln2g", "ln2b", "tmk", "tmv", "tmr", "fmk", "fmr",
           "tdecay", "tfirst"]
    pvin = {nm: din(nm, [128, JC]) for nm in PVN}
    bfcv_in = din("bfcv", [128, JF])
    bprojv_in = din("bprojv", [128, JC])

    out = nc.dram_tensor("out", [NP, 128, JC, TW], F32, kind="ExternalOutput").ap()

    def dintt(name, shape, dt=F32, dbg=False):
        kind = "ExternalOutput" if (debug and dbg) else "Internal"
        return nc.dram_tensor(name, shape, dt, kind=kind)

    sp_ek = dintt("sp_ek", [NP, 128, JC, TW], dbg=True).ap()
    sp_beta = dintt("sp_beta", [NP, 128, JC, TW], dbg=True).ap()
    sp_r = dintt("sp_r", [NP, 128, JC, TW], dbg=True).ap()
    sp_xin = dintt("sp_xin", [NP, 128, JC, TW], dbg=True).ap()
    sp_x1 = dintt("sp_x1", [NP, 128, JC, TW], dbg=True).ap()
    sp_h = dintt("sp_h", [NP, 128, JC, TW], dbg=True).ap()
    sp_xk = dintt("sp_xk", [NP, 128, JC, TW], BF16, dbg=True).ap()
    sp_inc = dintt("sp_inc", [NP, 128, 2 * JC], dbg=True).ap()
    sp_acc = dintt("sp_acc", [128, 3 * JC], dbg=True).ap()
    sp_g = dintt("sp_g", [NP, 128, JC, TW], dbg=True).ap()
    sp_xaf = dintt("sp_xaf", [NP, 128, JC, TW], dbg=True).ap()
    sp_sig = dintt("sp_sig", [NP, 128, JC, TW], dbg=True).ap()
    sp_xf = dintt("sp_xf", [NP, 128, JC, TW], BF16, dbg=True).ap()
    sp_h2 = dintt("sp_h2", [NP, 128, JF, TW], BF16, dbg=True).ap()
    sp_k = dintt("sp_k", [NP, 128, JC, TW], dbg=True).ap()
    sp_mk = dintt("sp_mk", [NP, 128, JC], dbg=True).ap()
    sp_nmk = dintt("sp_nmk", [NP, 128, JC], dbg=True).ap()
    SCW = ["wk", "wv", "wr", "wo", "fr", "fk", "fv", "wfc", "wproj"]
    sc_w = {nm: dintt("sc_" + nm, [128, W_INNER[nm]], BF16).ap() for nm in SCW}
    sc_bfc = dintt("sc_bfc", [LORA, DF], BF16).ap()
    sc_bproj = dintt("sc_bproj", [LORA, C], BF16).ap()
    cc_sum_in = dintt("cc_sum_in", [128, 3 * JC])
    cc_sum_out = dintt("cc_sum_out", [NCORES, 128, 3 * JC])
    dbg_sum = dintt("dbg_sum", [128, NCORES, 3 * JC], dbg=True).ap() if debug else None
    cc_row_in = dintt("cc_row_in", [128, JC])
    cc_row_out = dintt("cc_row_out", [NCORES, 128, JC])
    qdram = dintt("qdram", [128, 2])

    for s in ["s_cc", "s_T", "s_V", "s_S", "s_G"]:
        p.sem(s)
    p.init_dma_pool("sync", 10)
    p.init_dma_pool("scalar", 4)
    p.init_dma_pool("gpsimd", 10)

    sbt = lambda name, shape, dt=F32: es.enter_context(nc.sbuf_tensor(name, shape, dt))
    PV = {nm: sbt("t_" + nm, [128, JC]) for nm in PVN}
    t_bfcv = sbt("t_bfcv", [128, JF])
    t_bprojv = sbt("t_bprojv", [128, JC])
    t_xprev = sbt("t_xprev", [128, JC])
    t_pvalid = sbt("t_pvalid", [128, 1])
    t_mpre = sbt("t_mpre", [128, NCORES])
    t_mbig = sbt("t_mbig", [128, NCORES])
    t_mrow = sbt("t_mrow", [128, NCORES])
    t_w = sbt("t_w", [128, JC])
    t_ew = sbt("t_ew", [128, JC])
    t_eu = sbt("t_eu", [128, JC])
    t_tww = sbt("t_tww", [128, JC])
    t_spanw = sbt("t_spanw", [128, JC])
    t_eps = sbt("t_eps", [128, 1])
    t_ones = sbt("t_ones", [128, 128], BF16)
    t_hcol = sbt("t_hcol", [128, JC])
    t_gcol = sbt("t_gcol", [128, JC])
    t_mk = [sbt(f"t_mk{i}", [128, JC]) for i in range(NP)]
    t_negmk = [sbt(f"t_negmk{i}", [128, JC]) for i in range(NP)]
    t_za = sbt("t_za", [128, JC])
    t_zb = sbt("t_zb", [128, JC])
    t_run = [sbt(f"t_run{i}", [128, 3 * JC]) for i in range(NP + 1)]
    t_inc = [sbt(f"t_inc{i}", [128, 2 * JC]) for i in range(NP)]
    t_gath = sbt("t_gath", [128, NCORES, 3 * JC])
    t_rgath = sbt("t_rgath", [128, NCORES, JC])
    t_acc = sbt("t_acc", [128, 3 * JC])
    t_cj1 = sbt("t_cj1", [128, JC])
    t_cj2 = sbt("t_cj2", [128, JC])
    t_cj3 = sbt("t_cj3", [128, JC])
    t_qv = sbt("t_qv", [128, 2])
    t_qrow = sbt("t_qrow", [1, 256])
    t_qs = sbt("t_qs", [128, 2])
    t_qis = sbt("t_qis", [128, 2])
    t_rowbf = sbt("t_rowbf", [128, JC], BF16)
    t_rowbf2 = sbt("t_rowbf2", [128, JC], BF16)
    t_r1 = sbt("t_r1", [128, 1])
    t_r2 = sbt("t_r2", [128, 1])
    t_r3 = sbt("t_r3", [128, 1])
    t_rowa = sbt("t_rowa", [128, JC])
    t_x1c = [sbt(f"t_x1c{i}", [128, JC]) for i in range(NP)]
    t_gc = [sbt(f"t_gc{i}", [128, JC]) for i in range(NP)]
    t_zero = sbt("t_zero", [128, 1])

    afcb = sbt("afcb", [128, JC, LORA], BF16)
    aprojb = sbt("aprojb", [128, JF, LORA], BF16)

    WR_SLOT = 3072
    NSLOT = 3
    wring = sbt("wring", [128, NSLOT * WR_SLOT], BF16)
    wslot = [wring[:, i * WR_SLOT:(i + 1) * WR_SLOT] for i in range(NSLOT)]
    wslot_tok = [[] for _ in range(NSLOT)]
    ring_rr = [0]

    WST_N = 512
    NWST = 4
    WST = [sbt(f"wst{i}", [128, WST_N], F32) for i in range(NWST)]
    WSTI = sbt("wsti", [128, WST_N], I32)
    WSTO2 = [sbt(f"wsto{i}", [128, WST_N], BF16) for i in range(2)]
    WSTO = WSTO2[0]
    wst_free = [[] for _ in range(NWST)]
    wsto_free = [[], []]
    wsti_free = [[]]

    TWJC = JC * TW
    AR = Arena(nc, es, "arena_f", 6 * TWJC + 3 * TW + 64, F32)
    rA0 = AR.region(TWJC)
    rA1 = AR.region(TWJC)
    rB = AR.region(TWJC)
    rC = AR.region(TWJC)
    rD = AR.region(TWJC)
    rE = AR.region(TWJC)
    rS1 = AR.region(TW)
    rS2 = AR.region(TW)
    rS3 = AR.region(TW)
    ABF = Arena(nc, es, "arena_b", 8 * 1024 + 16 * 1024 + 1024 + 4096 + 32, BF16)
    bR1 = ABF.region(8192)
    bR2 = ABF.region(16384)
    bR4 = ABF.region(1024)
    bR5 = ABF.region(4096)
    bfcb = sbt("bfcb", [LORA, DF], BF16)
    bprojb = sbt("bprojb", [LORA, C], BF16)

    pbank = [es.enter_context(nc.psum_tensor(f"pb{i}", [128, 512], F32)) for i in range(8)]
    bank_tok = [[] for _ in range(8)]
    bank_rr = [0]

    def get_bank(extra_waits=()):
        i = bank_rr[0] % 8
        bank_rr[0] += 1
        w = list(bank_tok[i]) + list(extra_waits)
        bank_tok[i] = []
        return i, pbank[i], w

    def bank_done(i, toks):
        bank_tok[i] = list(toks)

    def get_all_banks():
        w = []
        for i in range(8):
            w += bank_tok[i]
            bank_tok[i] = []
        bank_rr[0] = 0
        return pbank, w

    def vi(fn, waits=()):
        return p.emit("vector", fn, inc="s_V", waits=waits)

    def vd(fn, waits=()):
        return p.emit("vector", fn, inc="s_V",
                      waits=list(waits) + [("s_V", p.counts["s_V"])])

    def sd(fn, waits=()):
        return p.emit("scalar", fn, inc="s_S",
                      waits=list(waits) + [("s_S", p.counts["s_S"])])

    def si(fn, waits=()):
        return p.emit("scalar", fn, inc="s_S", waits=waits)

    def ti(fn, waits=()):
        return p.emit("tensor", fn, inc="s_T", waits=waits)

    _ptoks = []
    for nm in PVN:
        _ptoks.append(p.dma("sync", PV[nm][:, :], pvin[nm]))
    _ptoks.append(p.dma("sync", t_bfcv[:, :], bfcv_in))
    _ptoks.append(p.dma("sync", t_bprojv[:, :], bprojv_in))
    _ptoks.append(p.dma("sync", t_xprev[:, :], xprev))
    _ptoks.append(p.dma("sync", t_pvalid[:, :], prevvalid))
    _ptoks.append(p.dma("sync", t_mpre[:, :], maskpre))
    _ptoks.append(p.dma("sync", t_mbig[:, :], maskbig))
    _ptoks.append(p.dma("sync", t_mrow[:, :], maskrow))
    _tp = vi(lambda e: e.memset(t_r3[:, :], 0.0), waits=_ptoks)
    tok_params = ("s_V", _tp)

    vi(lambda e: e.memset(t_ones[:, :], 1.0))
    vi(lambda e: e.memset(t_eps[:, :], EPS))
    vi(lambda e: e.memset(t_zero[:, :], 0.0))
    tok = si(lambda e: e.activation(out=t_w[:, :], in_=PV["tdecay"][:, :], func=AF.Exp,
                                    bias=0.0), waits=[tok_params])
    tok_w = vi(lambda e: e.tensor_scalar(out=t_w[:, :], in0=t_w[:, :], scalar1=-1.0,
                                         scalar2=None, op0=ALU.mult),
               waits=[("s_S", tok)])
    tok_ew = si(lambda e: e.activation(out=t_ew[:, :], in_=t_w[:, :], func=AF.Exp,
                                       bias=0.0), waits=[("s_V", tok_w)])
    tok_eu = si(lambda e: e.activation(out=t_eu[:, :], in_=PV["tfirst"][:, :],
                                       func=AF.Exp, bias=0.0))
    tok_tww = vd(lambda e: e.tensor_scalar(out=t_tww[:, :], in0=t_w[:, :],
                                           scalar1=float(TW), scalar2=None,
                                           op0=ALU.mult), waits=[("s_V", tok_w)])
    tok_spanw = vd(lambda e: e.tensor_scalar(out=t_spanw[:, :], in0=t_w[:, :],
                                             scalar1=float(NP * TW), scalar2=None,
                                             op0=ALU.mult))
    vi(lambda e: e.memset(t_run[0][:, 0:2 * JC], 0.0))
    tok_run = vi(lambda e: e.memset(t_run[0][:, 2 * JC:3 * JC], NEG_BIG))

    def ew_bcast(j, n):
        return bass.AP(t_ew, j, [[JC, 128], [0, n]])

    wcast_done = {}
    _wq = {"buf": 0, "obuf": 0}
    QIDX = {"wfc": 0, "wproj": 1}
    tok_qscale = [None]

    task_q = []
    task_qv = []

    def drain_tasks(n):
        for _ in range(min(n, len(task_q))):
            task_q.pop(0)()

    def drain_vtasks(n):
        for _ in range(min(n, len(task_qv))):
            task_qv.pop(0)()

    def drain_all_tasks():
        while task_qv:
            task_qv.pop(0)()
        while task_q:
            task_q.pop(0)()

    def cast_pieces(nm, quant=False):
        inner = W_INNER[nm]
        srcf = w_in[nm].rearrange("a b -> (a b)")
        dstf = sc_w[nm].rearrange("a b -> (a b)")
        total = 128 * inner
        piece = 128 * WST_N
        parts = (total + piece - 1) // piece
        out = []
        for h in range(parts):
            def body(h=h):
                nw = min(piece, total - h * piece)
                mw = nw // 128
                b = _wq["buf"]; _wq["buf"] = (b + 1) % NWST
                st = WST[b]
                tok_l = p.dma("sync", st[:, 0:mw], srcf[h * piece:h * piece + nw],
                              waits=wst_free[b])
                ob = _wq["obuf"]; _wq["obuf"] ^= 1
                OT = WSTO2[ob]
                if quant:
                    qi = QIDX[nm]
                    assert tok_qscale[0] is not None
                    tok_i = si(lambda e, st=st, mw=mw, qi=qi: e.activation(
                        out=WSTI[:, 0:mw], in_=st[:, 0:mw], func=AF.Identity,
                        bias=0.0, scale=t_qis[:, qi:qi + 1]),
                        waits=[tok_l, ("s_S", tok_qscale[0])] + wsti_free[0])
                    tok_o = si(lambda e, mw=mw, qi=qi, OT=OT: e.activation(
                        out=OT[:, 0:mw], in_=WSTI[:, 0:mw], func=AF.Identity,
                        bias=0.0, scale=t_qs[:, qi:qi + 1]), waits=wsto_free[ob])
                    wsti_free[0] = [("s_S", tok_o)]
                else:
                    tok_o = si(lambda e, st=st, mw=mw, OT=OT: e.activation(
                        out=OT[:, 0:mw], in_=st[:, 0:mw], func=AF.Copy,
                        bias=0.0), waits=[tok_l] + wsto_free[ob])
                wst_free[b] = [("s_S", tok_o)]
                tok_g = p.dma(
                    "scalar", dstf[h * piece:h * piece + nw],
                    OT[:, 0:mw], waits=[("s_S", tok_o)])
                wsto_free[ob] = [tok_g]
                wcast_done[nm] = tok_g
            out.append(body)
        return out

    def emit_cast_tasks(nm, quant=False):
        for f in cast_pieces(nm, quant):
            f()

    def emit_lora_tasks():
        for nm, dst, kcn in [("afc", afcb, JC), ("aproj", aprojb, JF)]:
            n = kcn * LORA
            b = _wq["buf"]; _wq["buf"] = (b + 1) % NWST
            st = WST[b]
            stv = st[:, 0:n].rearrange("p (a c) -> p a c", a=kcn, c=LORA)
            tok_l = p.dma("sync", st[:, 0:n], w_in[nm], waits=wst_free[b])
            tok_o = si(lambda e, stv=stv, dst=dst: e.activation(
                out=dst[:, :, :], in_=stv, func=AF.Copy, bias=0.0),
                waits=[tok_l])
            wst_free[b] = [("s_S", tok_o)]
        for nm, dst, m in [("bfc", sc_bfc, DF), ("bproj", sc_bproj, C)]:
            parts = (m + WST_N - 1) // WST_N
            for h in range(parts):
                mw = min(WST_N, m - h * WST_N)
                b = _wq["buf"]; _wq["buf"] = (b + 1) % NWST
                st = WST[b]
                tok_l = p.dma("sync", st[0:LORA, 0:mw],
                              w_in[nm][:, h * WST_N:h * WST_N + mw],
                              waits=wst_free[b])
                tok_o = si(lambda e, st=st, mw=mw: e.activation(
                    out=WSTO[0:LORA, 0:mw], in_=st[0:LORA, 0:mw], func=AF.Copy,
                    bias=0.0), waits=[tok_l] + wsto_free[0])
                wst_free[b] = [("s_S", tok_o)]
                tok_g = p.dma("scalar", dst[:, h * WST_N:h * WST_N + mw],
                              WSTO[0:LORA, 0:mw], waits=[("s_S", tok_o)])
                wsto_free[0] = [tok_g]
            wcast_done[nm] = tok_g

    def quant_scale_pieces():
        tok_red = {0: None, 1: None}
        qstate = {0: True, 1: True}
        out = []
        for qi, nm in enumerate(["wfc", "wproj"]):
            srcf = w_in[nm].rearrange("a b -> (a b)")
            total = 128 * W_INNER[nm]
            piece = 128 * WST_N
            parts = (total + piece - 1) // piece
            for h in range(parts):
                def body(h=h, qi=qi, srcf=srcf, total=total):
                    nw = min(piece, total - h * piece)
                    mw = nw // 128
                    b = _wq["buf"]; _wq["buf"] = (b + 1) % NWST
                    st = WST[b]
                    tok_l = p.dma("sync", st[:, 0:mw],
                                  srcf[h * piece:h * piece + nw],
                                  waits=wst_free[b])
                    if qstate[qi]:
                        tok_r = vi(lambda e, st=st, mw=mw, qi=qi: e.tensor_reduce(
                            out=t_qv[:, qi:qi + 1], in_=st[:, 0:mw],
                            axis=mybir.AxisListType.X, op=ALU.max,
                            apply_absolute_value=True), waits=[tok_l])
                        qstate[qi] = False
                    else:
                        tok_r = vi(lambda e, st=st, mw=mw: e.tensor_reduce(
                            out=t_r1[:, :], in_=st[:, 0:mw],
                            axis=mybir.AxisListType.X, op=ALU.max,
                            apply_absolute_value=True), waits=[tok_l])
                        tok_r = vi(lambda e, qi=qi: e.tensor_tensor(
                            out=t_qv[:, qi:qi + 1], in0=t_qv[:, qi:qi + 1],
                            in1=t_r1[:, :], op=ALU.max), waits=[("s_V", tok_r)])
                    wst_free[b] = [("s_V", tok_r)]
                    tok_red[qi] = tok_r
                out.append(body)
        out.append(lambda: quant_scale_final(tok_red))
        return out

    def quant_scale_final(tok_red):
        tok_d1 = p.dma("gpsimd", qdram.ap(), t_qv[:, 0:2],
                       waits=[("s_V", tok_red[0]), ("s_V", tok_red[1])])
        tok_d2 = p.dma("gpsimd", t_qrow[:, :], qdram.ap().rearrange("a b -> (a b)"),
                       waits=[tok_d1])
        tok_ms = []
        for qi in range(2):
            qv = bass.AP(t_qrow, qi, [[256, 1], [2, 128]])
            tok_ms.append(vd(lambda e, qv=qv, qi=qi: e.tensor_reduce(
                out=t_qrow[:, 254 + qi:255 + qi], in_=qv,
                axis=mybir.AxisListType.X, op=ALU.max),
                waits=[tok_d2]))
        tok_qq = vd(lambda e: e.tensor_scalar(out=t_qrow[:, 252:254],
                                              in0=t_qrow[:, 254:256],
                                              scalar1=1.0 / 127.0, scalar2=None,
                                              op0=ALU.mult),
                    waits=[("s_V", tok_ms[-1])])
        vd(lambda e: e.reciprocal(out=t_qrow[:, 250:252], in_=t_qrow[:, 252:254]),
           waits=[("s_V", tok_qq)])
        tok_of = vd(lambda e: e.memset(t_qrow[:, 248:249], 1.0))
        ib, bnk, wts = get_bank([("s_V", tok_of)])
        onesrow = bass.AP(t_qrow, 248, [[256, 1], [0, 128]])
        tok_t = ti(lambda e, bnk=bnk: e.matmul(bnk[:, 0:4], onesrow,
                                               t_qrow[:, 250:254],
                                               start=True, stop=True), waits=wts)
        sd(lambda e, bnk=bnk: e.activation(out=t_qis[:, 0:2], in_=bnk[:, 0:2],
                                           func=AF.Identity, bias=0.0, scale=1.0),
           waits=[("s_T", tok_t)])
        tok_qs = sd(lambda e, bnk=bnk: e.activation(out=t_qs[:, 0:2], in_=bnk[:, 2:4],
                                                    func=AF.Identity, bias=0.0,
                                                    scale=1.0))
        bank_done(ib, [("s_S", tok_qs)])
        tok_qscale[0] = tok_qs

    def stream_chunks(src_name, k0, k1, waits=(), m=None):
        src = sc_w[src_name]
        if m is None:
            m = C
        per_slot = WR_SLOT // m
        views = []
        vtoks = []
        slots = []
        kc = k0
        while kc < k1:
            kh = min(k1, kc + per_slot)
            slot = ring_rr[0] % NSLOT
            ring_rr[0] += 1
            wts = list(wslot_tok[slot]) + list(waits) + [wcast_done[src_name]]
            wslot_tok[slot] = []
            v = wslot[slot][:, 0:(kh - kc) * m].rearrange("p (a c) -> p a c",
                                                          a=kh - kc, c=m)
            tok = p.dma("sync", wslot[slot][:, 0:(kh - kc) * m],
                        src[:, kc * m:kh * m], waits=wts)
            for i in range(kh - kc):
                views.append(v[:, i, :])
                vtoks.append(tok)
            slots.append(slot)
            kc = kh
        return views, vtoks, slots

    def stream_cols(src_name, g8, waits=()):
        return stream_chunks(src_name, g8 * JC, g8 * JC + JC, waits=waits, m=1024)

    def slots_release(slots, toks):
        for s in slots:
            wslot_tok[s] = list(wslot_tok[s]) + list(toks)

    m_st = AR.view(rS1, [TW])
    e2_st = AR.view(rS2, [TW])
    msq_st = AR.view(rS3, [TW])

    def ln_stats(x_ap, xb_ap, xsq_ap, waits, stage_free=(), m_free=()):
        si(lambda e: e.activation(out=xb_ap, in_=x_ap, func=AF.Copy, bias=0.0),
           waits=list(waits) + list(stage_free))
        tok_s = si(lambda e: e.activation(out=xsq_ap, in_=x_ap, func=AF.Square,
                                          bias=0.0))
        ib1, b1, w1 = get_bank([("s_S", tok_s)])
        for j in range(JC):
            tok_t1 = ti(lambda e, j=j, b1=b1: e.matmul(
                b1[:, 0:TW], t_ones[:, :], xb_ap[:, j, :],
                start=(j == 0), stop=(j == JC - 1)), waits=w1 if j == 0 else ())
        ib2, b2, w2 = get_bank()
        for j in range(JC):
            tok_t2 = ti(lambda e, j=j, b2=b2: e.matmul(
                b2[:, 0:TW], t_ones[:, :], xsq_ap[:, j, :],
                start=(j == 0), stop=(j == JC - 1)), waits=w2 if j == 0 else ())
        tok_m = si(lambda e, b1=b1: e.activation(out=m_st[:, :], in_=b1[:, 0:TW],
                                                 func=AF.Identity, bias=0.0,
                                                 scale=1.0 / C),
                   waits=[("s_T", tok_t1)] + list(m_free))
        tok_e2 = si(lambda e, b2=b2: e.activation(out=e2_st[:, :], in_=b2[:, 0:TW],
                                                  func=AF.Identity, bias=0.0,
                                                  scale=1.0 / C),
                    waits=[("s_T", tok_t2)])
        bank_done(ib1, [("s_S", tok_m)])
        bank_done(ib2, [("s_S", tok_e2)])
        tok_msq = si(lambda e: e.activation(out=msq_st[:, :], in_=m_st[:, :],
                                            func=AF.Square, bias=0.0))
        tok_var = vi(lambda e: e.tensor_tensor(out=e2_st[:, :], in0=e2_st[:, :],
                                               in1=msq_st[:, :], op=ALU.subtract),
                     waits=[("s_S", tok_msq)])
        si(lambda e: e.activation(out=e2_st[:, :], in_=e2_st[:, :], func=AF.Ln,
                                  bias=t_eps[:, :], scale=1.0),
           waits=[("s_V", tok_var)])
        tok_rstd = si(lambda e: e.activation(out=e2_st[:, :], in_=e2_st[:, :],
                                             func=AF.Exp, bias=0.0, scale=-0.5))
        return tok_rstd, ("s_T", tok_t2)

    def ln_apply(x_ap, out_ap, tok_rstd, g=None, b=None, waits=(), scratch=None):
        sc = scratch if scratch is not None else out_ap
        tok = None
        for j in range(JC):
            tok = vi(lambda e, j=j: e.tensor_tensor(
                out=sc[:, j, :], in0=x_ap[:, j, :], in1=m_st[:, :], op=ALU.subtract),
                waits=(list(waits) + [("s_S", tok_rstd)]) if j == 0 else ())
        for j in range(JC):
            if g is None:
                tok = vi(lambda e, j=j: e.tensor_tensor(
                    out=out_ap[:, j, :], in0=sc[:, j, :], in1=e2_st[:, :],
                    op=ALU.mult))
            else:
                vi(lambda e, j=j: e.tensor_tensor(
                    out=sc[:, j, :], in0=sc[:, j, :], in1=e2_st[:, :], op=ALU.mult))
                tok = vi(lambda e, j=j: e.tensor_scalar(
                    out=out_ap[:, j, :], in0=sc[:, j, :],
                    scalar1=g[:, j:j + 1], scalar2=b[:, j:j + 1],
                    op0=ALU.mult, op1=ALU.add))
        return tok

    def row_ln(x_ap, out_ap, g=None, b=None, waits=()):
        sd(lambda e: e.activation(out=t_rowbf[:, :], in_=x_ap, func=AF.Copy, bias=0.0),
           waits=list(waits) + [("s_T", p.counts["s_T"]), ("s_V", p.counts["s_V"])])
        tok_s = sd(lambda e: e.activation(out=t_rowbf2[:, :], in_=x_ap,
                                          func=AF.Square, bias=0.0))
        ib, bnk, wts = get_bank([("s_S", tok_s)])
        ti(lambda e, bnk=bnk: e.matmul(bnk[:, 0:JC], t_ones[:, :], t_rowbf[:, :],
                                       start=True, stop=True), waits=wts)
        tok_t = ti(lambda e, bnk=bnk: e.matmul(bnk[:, JC:2 * JC], t_ones[:, :],
                                               t_rowbf2[:, :], start=True, stop=True))
        vd(lambda e, bnk=bnk: e.tensor_reduce(
            out=t_r1[:, :], in_=bnk[:, 0:JC], axis=mybir.AxisListType.X, op=ALU.add),
            waits=[("s_T", tok_t)])
        tok_v = vd(lambda e, bnk=bnk: e.tensor_reduce(
            out=t_r2[:, :], in_=bnk[:, JC:2 * JC], axis=mybir.AxisListType.X,
            op=ALU.add))
        bank_done(ib, [("s_V", tok_v)])
        vd(lambda e: e.tensor_scalar(out=t_r1[:, :], in0=t_r1[:, :], scalar1=1.0 / C,
                                     scalar2=None, op0=ALU.mult),
           waits=[("s_V", tok_v)])
        tok_v = vd(lambda e: e.tensor_scalar(out=t_r2[:, :], in0=t_r2[:, :],
                                             scalar1=1.0 / C, scalar2=None,
                                             op0=ALU.mult))
        tok_s = sd(lambda e: e.activation(out=t_r3[:, :], in_=t_r1[:, :],
                                          func=AF.Square, bias=0.0),
                   waits=[("s_V", tok_v)])
        tok_v = vd(lambda e: e.tensor_tensor(out=t_r2[:, :], in0=t_r2[:, :],
                                             in1=t_r3[:, :], op=ALU.subtract),
                   waits=[("s_S", tok_s)])
        sd(lambda e: e.activation(out=t_r2[:, :], in_=t_r2[:, :], func=AF.Ln,
                                  bias=t_eps[:, :], scale=1.0), waits=[("s_V", tok_v)])
        tok_s = sd(lambda e: e.activation(out=t_r2[:, :], in_=t_r2[:, :], func=AF.Exp,
                                          bias=0.0, scale=-0.5))
        tok_v = vd(lambda e: e.tensor_scalar(out=out_ap, in0=x_ap, scalar1=t_r1[:, :],
                                             scalar2=t_r2[:, :], op0=ALU.subtract,
                                             op1=ALU.mult), waits=[("s_S", tok_s)])
        if g is not None:
            vd(lambda e: e.tensor_tensor(out=out_ap, in0=out_ap, in1=g, op=ALU.mult))
            tok_v = vd(lambda e: e.tensor_tensor(out=out_ap, in0=out_ap, in1=b,
                                                 op=ALU.add))
        return tok_v

    x_t = [AR.view(rA0, [JC, TW]), AR.view(rA1, [JC, TW])]
    beta_t = [AR.view(rA0, [JC, TW]), AR.view(rA1, [JC, TW])]
    xin_t = AR.view(rB, [JC, TW])
    h_t = AR.view(rC, [JC, TW])
    ek1_t = AR.view(rC, [JC, TW])
    d_t = AR.view(rD, [JC, TW])
    prod_t = AR.view(rD, [JC, TW])
    r1_t = AR.view(rE, [JC, TW])
    xb_t = ABF.view(bR1, [JC, TW])
    xsq_t = ABF.view(bR1 + TWJC, [JC, TW])
    xk_t = ABF.view(bR2, [JC, TW])
    xv_t = ABF.view(bR2 + TWJC, [JC, TW])
    xr_t = ABF.view(bR2 + 2 * TWJC, [JC, TW])

    prev = {"xfree": [[], []], "xinfree": [], "ekfree": [], "betafree": [[], []],
            "rfree": [], "stagefree": [], "mixfree": [], "mfree": [],
            "applyfree": []}

    def phase1_pass(pp):
        par = pp % 2
        tx = x_t[par]
        tbeta = beta_t[par]
        if pp in xpre:
            tok_x = xpre.pop(pp)
        else:
            tok_x = p.dma("sync", tx, xin[pp], waits=prev["xfree"][par])
        tok_rstd, tok_stg = ln_stats(tx, xb_t, xsq_t, waits=[tok_x],
                                     stage_free=prev["stagefree"],
                                     m_free=prev["mfree"])
        tok_xin = ln_apply(tx, xin_t, tok_rstd, waits=prev["xinfree"])
        if pp == NP - 1 and not debug:
            tok_sp_xin = None
        else:
            tok_sp_xin = p.dma("gpsimd", sp_xin[pp], xin_t,
                               waits=[("s_V", tok_xin)])
        tok_rstd, tok_stg = ln_stats(xin_t, xb_t, xsq_t, waits=[("s_V", tok_xin)])
        tok_h = ln_apply(xin_t, h_t, tok_rstd, g=PV["ln1g"], b=PV["ln1b"],
                         waits=prev["mixfree"])
        prev["xinfree"] = [tok_sp_xin] if tok_sp_xin is not None else []
        prev["stagefree"] = [tok_stg]
        prev["mfree"] = [("s_V", tok_h)]
        for j in range(JC):
            vi(lambda e, j=j: e.tensor_tensor(
                out=d_t[:, j, 1:TW], in0=h_t[:, j, 1:TW], in1=h_t[:, j, 0:TW - 1],
                op=ALU.subtract))
            vd(lambda e, j=j: e.tensor_tensor(
                out=d_t[:, j, 0:1], in0=h_t[:, j, 0:1], in1=t_hcol[:, j:j + 1],
                op=ALU.subtract))
        tok_mix = None
        mix_toks = []
        for tm, xo in ((PV["tmk"], xk_t), (PV["tmv"], xv_t), (PV["tmr"], xr_t)):
            for j in range(JC):
                vi(lambda e, j=j, tm=tm, xo=xo: e.scalar_tensor_tensor(
                    out=xo[:, j, 1:TW], in0=d_t[:, j, 1:TW], scalar=tm[:, j:j + 1],
                    in1=h_t[:, j, 0:TW - 1], op0=ALU.mult, op1=ALU.add))
                tok_mix = vd(lambda e, j=j, tm=tm, xo=xo: e.scalar_tensor_tensor(
                    out=xo[:, j, 0:1], in0=d_t[:, j, 0:1], scalar=tm[:, j:j + 1],
                    in1=t_hcol[:, j:j + 1], op0=ALU.mult, op1=ALU.add))
            mix_toks.append(tok_mix)
        tok_hcol = vi(lambda e: e.tensor_copy(t_hcol[:, :], h_t[:, :, TW - 1:TW]),
                      waits=[("s_V", tok_h)])
        if debug:
            p.dma("gpsimd", sp_h[pp], h_t, waits=[("s_V", tok_hcol)])
            p.dma("gpsimd", sp_xk[pp], xk_t, waits=[("s_V", tok_mix)])
        ek_toks = [None] * JC
        views, vtoks, slots = stream_chunks("wk", 0, JC)
        lastmm = None
        for mc in range(JC):
            ib, bnk, wts = get_bank()
            for kc in range(JC):
                lastmm = ti(lambda e, bnk=bnk, v=views[kc], mc=mc, kc=kc: e.matmul(
                    bnk[:, 0:TW], v[:, mc * 128:(mc + 1) * 128], xk_t[:, kc, :],
                    start=(kc == 0), stop=(kc == JC - 1)),
                    waits=(wts + [vtoks[kc], ("s_V", mix_toks[0])]) if kc == 0
                    else ([vtoks[kc]] if vtoks[kc] is not vtoks[kc - 1] else ()))
            tok_red = vi(lambda e, bnk=bnk, mc=mc, pp=pp: e.tensor_reduce(
                out=t_mk[pp][:, mc:mc + 1], in_=bnk[:, 0:TW],
                axis=mybir.AxisListType.X, op=ALU.max), waits=[("s_T", lastmm)])
            vi(lambda e, mc=mc, pp=pp: e.tensor_scalar(
                out=t_negmk[pp][:, mc:mc + 1], in0=t_mk[pp][:, mc:mc + 1],
                scalar1=-1.0, scalar2=None, op0=ALU.mult),
                waits=[("s_V", tok_red)])
            if debug:
                si(lambda e, bnk=bnk, mc=mc: e.activation(
                    out=d_t[:, mc, :], in_=bnk[:, 0:TW], func=AF.Identity,
                    bias=0.0, scale=1.0), waits=[("s_V", p.counts["s_V"])])
            tok_ek = si(lambda e, bnk=bnk, mc=mc, pp=pp: e.activation(
                out=ek1_t[:, mc, :], in_=bnk[:, 0:TW], func=AF.Exp,
                bias=t_negmk[pp][:, mc:mc + 1], scale=1.0),
                waits=[("s_V", p.counts["s_V"])] + prev["ekfree"] if mc == 0
                else [("s_V", p.counts["s_V"])])
            ek_toks[mc] = tok_ek
            bank_done(ib, [("s_S", tok_ek)])
            drain_vtasks(1)
        slots_release(slots, [("s_T", lastmm)])
        dbg_ktok = []
        if debug:
            tk1 = p.dma("gpsimd", sp_k[pp], d_t, waits=[("s_S", ek_toks[-1])])
            p.dma("gpsimd", sp_mk[pp], t_mk[pp][:, :], waits=[("s_S", ek_toks[-1])])
            p.dma("gpsimd", sp_nmk[pp], t_negmk[pp][:, :], waits=[("s_S", ek_toks[-1])])
            dbg_ktok = [tk1]
        views, vtoks, slots = stream_chunks("wv", 0, JC)
        beta_toks = [None] * JC
        for mc in range(JC):
            ib, bnk, wts = get_bank()
            for kc in range(JC):
                lastmm = ti(lambda e, bnk=bnk, v=views[kc], mc=mc, kc=kc: e.matmul(
                    bnk[:, 0:TW], v[:, mc * 128:(mc + 1) * 128], xv_t[:, kc, :],
                    start=(kc == 0), stop=(kc == JC - 1)),
                    waits=(wts + [vtoks[kc], ("s_V", mix_toks[1])])
                    if (kc == 0 and mc == 0) else (wts + [vtoks[kc]]) if kc == 0
                    else ([vtoks[kc]] if vtoks[kc] is not vtoks[kc - 1] else ()))
            tok_b = vi(lambda e, bnk=bnk, mc=mc, tbeta=tbeta: e.tensor_tensor(
                out=tbeta[:, mc, :], in0=ek1_t[:, mc, :], in1=bnk[:, 0:TW],
                op=ALU.mult),
                waits=[("s_T", lastmm), ("s_S", ek_toks[mc])]
                + (prev["betafree"][par] if mc == 0 else []))
            beta_toks[mc] = tok_b
            bank_done(ib, [("s_V", tok_b)])
            drain_vtasks(1)
        slots_release(slots, [("s_T", lastmm)])
        views, vtoks, slots = stream_chunks("wr", 0, JC)
        r_toks = [None] * JC
        for mc in range(JC):
            ib, bnk, wts = get_bank()
            for kc in range(JC):
                lastmm = ti(lambda e, bnk=bnk, v=views[kc], mc=mc, kc=kc: e.matmul(
                    bnk[:, 0:TW], v[:, mc * 128:(mc + 1) * 128], xr_t[:, kc, :],
                    start=(kc == 0), stop=(kc == JC - 1)),
                    waits=(wts + [vtoks[kc], ("s_V", mix_toks[2])])
                    if (kc == 0 and mc == 0) else (wts + [vtoks[kc]]) if kc == 0
                    else ([vtoks[kc]] if vtoks[kc] is not vtoks[kc - 1] else ()))
            tok_r = si(lambda e, bnk=bnk, mc=mc: e.activation(
                out=r1_t[:, mc, :], in_=bnk[:, 0:TW], func=AF.Identity, bias=0.0,
                scale=1.0),
                waits=[("s_T", lastmm)] + (prev["rfree"] if mc == 0 else []))
            r_toks[mc] = tok_r
            bank_done(ib, [("s_S", tok_r)])
        slots_release(slots, [("s_T", lastmm)])
        tok_sca = None
        for j in range(JC):
            tok_sca = vi(lambda e, j=j, tbeta=tbeta: e.tensor_tensor_scan(
                prod_t[:, j, :], ew_bcast(j, TW), tbeta[:, j, :], 0.0,
                ALU.mult, ALU.add),
                (([("s_V", beta_toks[-1]), ("s_S", tok_ew)] + dbg_ktok)
                 if j == 0 else ()))
        tok_za = vi(lambda e: e.tensor_copy(t_za[:, :], prod_t[:, :, TW - 1:TW]),
                    waits=[("s_V", tok_sca)])
        for j in range(JC):
            tok_sca = vi(lambda e, j=j: e.tensor_tensor_scan(
                prod_t[:, j, :], ew_bcast(j, TW), ek1_t[:, j, :], 0.0,
                ALU.mult, ALU.add),
                waits=[("s_S", ek_toks[-1]), ("s_V", tok_za)] if j == 0 else ())
        tok_zb = vi(lambda e: e.tensor_copy(t_zb[:, :], prod_t[:, :, TW - 1:TW]),
                    waits=[("s_V", tok_sca)])
        ro, rn = t_run[pp], t_run[pp + 1]
        a_o, b_o, m_o = ro[:, 0:JC], ro[:, JC:2 * JC], ro[:, 2 * JC:3 * JC]
        a_n, b_n, m_n = rn[:, 0:JC], rn[:, JC:2 * JC], rn[:, 2 * JC:3 * JC]
        vd(lambda e: e.tensor_tensor(out=t_cj1[:, :], in0=m_o, in1=t_tww[:, :],
                                     op=ALU.add), waits=[("s_V", tok_zb)])
        vd(lambda e: e.tensor_tensor(out=m_n, in0=t_cj1[:, :], in1=t_mk[pp][:, :],
                                     op=ALU.max))
        tok_v = vd(lambda e: e.tensor_tensor(out=t_cj1[:, :], in0=t_cj1[:, :],
                                             in1=m_n, op=ALU.subtract))
        sd(lambda e: e.activation(out=t_cj1[:, :], in_=t_cj1[:, :], func=AF.Exp,
                                  bias=0.0), waits=[("s_V", tok_v)])
        tok_v = vd(lambda e: e.tensor_tensor(out=t_cj2[:, :], in0=t_mk[pp][:, :],
                                             in1=m_n, op=ALU.subtract))
        tok_s = sd(lambda e: e.activation(out=t_cj2[:, :], in_=t_cj2[:, :],
                                          func=AF.Exp, bias=0.0),
                   waits=[("s_V", tok_v)])
        vd(lambda e: e.tensor_tensor(out=a_n, in0=a_o, in1=t_cj1[:, :], op=ALU.mult),
           waits=[("s_S", tok_s)])
        vd(lambda e: e.tensor_tensor(out=t_cj3[:, :], in0=t_za[:, :],
                                     in1=t_cj2[:, :], op=ALU.mult))
        vd(lambda e: e.tensor_tensor(out=a_n, in0=a_n, in1=t_cj3[:, :], op=ALU.add))
        vd(lambda e: e.tensor_tensor(out=b_n, in0=b_o, in1=t_cj1[:, :], op=ALU.mult))
        vd(lambda e: e.tensor_tensor(out=t_cj3[:, :], in0=t_zb[:, :],
                                     in1=t_cj2[:, :], op=ALU.mult))
        tok_run = vd(lambda e: e.tensor_tensor(out=b_n, in0=b_n, in1=t_cj3[:, :],
                                               op=ALU.add))
        if pp == NP - 1 and not debug:
            prev["ekfree"] = []
            prev["betafree"][par] = []
            prev["xfree"][par] = []
            prev["rfree"] = []
        else:
            tok_sp1 = p.dma("gpsimd", sp_ek[pp], ek1_t,
                            waits=[("s_S", ek_toks[-1]), ("s_V", tok_zb)])
            tok_sp2 = p.dma("gpsimd", sp_beta[pp], tbeta,
                            waits=[("s_V", tok_za)])
            tok_sp3 = p.dma("gpsimd", sp_r[pp], r1_t,
                            waits=[("s_S", r_toks[-1])])
            prev["ekfree"] = [tok_sp1]
            prev["betafree"][par] = [tok_sp2]
            prev["xfree"][par] = [tok_sp2]
            prev["rfree"] = [tok_sp3]
        prev["mixfree"] = ([("s_V", tok_mix), ("s_T", lastmm)]
                           + list(prev["ekfree"]))
        return tok_run

    xpre = {0: p.dma("sync", x_t[0], xin[0], waits=[])}
    emit_lora_tasks()
    emit_cast_tasks("wk")
    emit_cast_tasks("wv")
    emit_cast_tasks("wr")
    row_ln(t_xprev[:, :], t_rowa[:, :], waits=[tok_params])
    row_ln(t_rowa[:, :], t_rowa[:, :], g=PV["ln1g"][:, :], b=PV["ln1b"][:, :])
    vi(lambda e: e.tensor_scalar(out=t_hcol[:, :], in0=t_rowa[:, :],
                                 scalar1=t_pvalid[:, :], scalar2=None, op0=ALU.mult))

    after_pass = {0: ["wo"]}
    task_qv.extend(quant_scale_pieces())
    for pp in range(NP):
        tok_run_last = phase1_pass(pp)
        for nm in after_pass.get(pp, []):
            emit_cast_tasks(nm)
        drain_vtasks(16)
    task_q.extend(cast_pieces("fr"))
    task_q.extend(cast_pieces("fk"))
    task_q.extend(cast_pieces("fv"))
    task_q.extend(cast_pieces("wfc", quant=True))
    task_q.extend(cast_pieces("wproj", quant=True))

    tok_pack = p.dma("gpsimd", cc_sum_in.ap(), t_run[NP][:, :],
                     waits=[("s_V", tok_run_last)])
    tok_cc = p.emit("gpsimd", lambda e: e.collective_compute(
        "AllGather", ALU.bypass, replica_groups=[list(range(NCORES))],
        ins=[cc_sum_in.ap().opt()], outs=[cc_sum_out.ap().opt()]),
        inc="s_cc", amt=1, waits=[tok_pack])
    while task_qv:
        task_qv.pop(0)()
    drain_tasks(16)
    gtoks = []
    for r in range(NCORES):
        gtoks.append(p.dma("gpsimd", t_gath[:, r, :], cc_sum_out.ap()[r],
                           waits=[("s_cc", tok_cc)] if r == 0 else ()))
    vd(lambda e: e.memset(t_acc[:, 0:2 * JC], 0.0))
    tok_acc = vd(lambda e: e.memset(t_acc[:, 2 * JC:3 * JC], NEG_BIG))
    aa, ab, am = t_acc[:, 0:JC], t_acc[:, JC:2 * JC], t_acc[:, 2 * JC:3 * JC]
    for r in range(NCORES):
        g_a = t_gath[:, r, 0:JC]
        g_b = t_gath[:, r, JC:2 * JC]
        g_m = t_gath[:, r, 2 * JC:3 * JC]
        vd(lambda e, g_m=g_m, r=r: e.tensor_scalar(
            out=t_cj1[:, :], in0=g_m, scalar1=t_mpre[:, r:r + 1],
            scalar2=t_mbig[:, r:r + 1], op0=ALU.mult, op1=ALU.add),
            waits=[gtoks[r]] + ([("s_V", tok_acc)] if r == 0 else []))
        vd(lambda e, r=r: e.tensor_scalar(out=t_cj2[:, :], in0=t_spanw[:, :],
                                          scalar1=t_mpre[:, r:r + 1], scalar2=None,
                                          op0=ALU.mult))
        vd(lambda e: e.tensor_tensor(out=t_cj2[:, :], in0=t_cj2[:, :], in1=am,
                                     op=ALU.add))
        vd(lambda e: e.tensor_tensor(out=am, in0=t_cj2[:, :], in1=t_cj1[:, :],
                                     op=ALU.max))
        tok_v = vd(lambda e: e.tensor_tensor(out=t_cj2[:, :], in0=t_cj2[:, :], in1=am,
                                             op=ALU.subtract))
        sd(lambda e: e.activation(out=t_cj2[:, :], in_=t_cj2[:, :], func=AF.Exp,
                                  bias=0.0), waits=[("s_V", tok_v)])
        tok_v = vd(lambda e: e.tensor_tensor(out=t_cj1[:, :], in0=t_cj1[:, :], in1=am,
                                             op=ALU.subtract))
        tok_s = sd(lambda e: e.activation(out=t_cj1[:, :], in_=t_cj1[:, :],
                                          func=AF.Exp, bias=0.0),
                   waits=[("s_V", tok_v)])
        vd(lambda e: e.tensor_tensor(out=aa, in0=aa, in1=t_cj2[:, :], op=ALU.mult),
           waits=[("s_S", tok_s)])
        vd(lambda e, g_a=g_a: e.tensor_tensor(out=t_cj3[:, :], in0=g_a,
                                              in1=t_cj1[:, :], op=ALU.mult))
        vd(lambda e: e.tensor_tensor(out=aa, in0=aa, in1=t_cj3[:, :], op=ALU.add))
        vd(lambda e: e.tensor_tensor(out=ab, in0=ab, in1=t_cj2[:, :], op=ALU.mult))
        vd(lambda e, g_b=g_b: e.tensor_tensor(out=t_cj3[:, :], in0=g_b,
                                              in1=t_cj1[:, :], op=ALU.mult))
        tok_acc = vd(lambda e: e.tensor_tensor(out=ab, in0=ab, in1=t_cj3[:, :],
                                               op=ALU.add))
    if debug:
        p.dma("gpsimd", sp_acc, t_acc[:, :], waits=[("s_V", tok_acc)])
        p.dma("gpsimd", dbg_sum, t_gath[:, :, :], waits=[("s_V", tok_acc)])
    tok_inc = [None] * NP
    for pp in range(NP):
        ro = t_run[pp]
        a_o2, b_o2, m_o2 = ro[:, 0:JC], ro[:, JC:2 * JC], ro[:, 2 * JC:3 * JC]
        vd(lambda e, pp=pp: e.tensor_scalar(out=t_cj1[:, :], in0=t_tww[:, :],
                                            scalar1=float(pp), scalar2=None,
                                            op0=ALU.mult), waits=[("s_V", tok_acc)])
        vd(lambda e: e.tensor_tensor(out=t_cj1[:, :], in0=t_cj1[:, :], in1=am,
                                     op=ALU.add))
        vd(lambda e, m_o2=m_o2: e.tensor_tensor(out=t_cj3[:, :], in0=t_cj1[:, :],
                                                in1=m_o2, op=ALU.max))
        tok_v = vd(lambda e: e.tensor_tensor(out=t_cj1[:, :], in0=t_cj1[:, :],
                                             in1=t_cj3[:, :], op=ALU.subtract))
        sd(lambda e: e.activation(out=t_cj1[:, :], in_=t_cj1[:, :], func=AF.Exp,
                                  bias=0.0), waits=[("s_V", tok_v)])
        tok_v = vd(lambda e, m_o2=m_o2: e.tensor_tensor(out=t_cj2[:, :], in0=m_o2,
                                                        in1=t_cj3[:, :],
                                                        op=ALU.subtract))
        tok_s = sd(lambda e: e.activation(out=t_cj2[:, :], in_=t_cj2[:, :],
                                          func=AF.Exp, bias=0.0),
                   waits=[("s_V", tok_v)])
        ia = t_inc[pp][:, 0:JC]
        ibb = t_inc[pp][:, JC:2 * JC]
        vd(lambda e, ia=ia: e.tensor_tensor(out=ia, in0=aa, in1=t_cj1[:, :],
                                            op=ALU.mult), waits=[("s_S", tok_s)])
        vd(lambda e, ia=ia, a_o2=a_o2: e.tensor_tensor(out=t_cj3[:, :], in0=a_o2,
                                                       in1=t_cj2[:, :], op=ALU.mult))
        vd(lambda e, ia=ia: e.tensor_tensor(out=ia, in0=ia, in1=t_cj3[:, :],
                                            op=ALU.add))
        vd(lambda e, ibb=ibb: e.tensor_tensor(out=ibb, in0=ab, in1=t_cj1[:, :],
                                              op=ALU.mult))
        vd(lambda e, ibb=ibb, b_o2=b_o2: e.tensor_tensor(out=t_cj3[:, :], in0=b_o2,
                                                         in1=t_cj2[:, :],
                                                         op=ALU.mult))
        vd(lambda e, ibb=ibb: e.tensor_tensor(out=ibb, in0=ibb, in1=t_cj3[:, :],
                                              op=ALU.add))
        vd(lambda e, pp=pp: e.tensor_scalar(out=t_cj1[:, :], in0=t_tww[:, :],
                                            scalar1=float(pp), scalar2=None,
                                            op0=ALU.mult))
        vd(lambda e: e.tensor_tensor(out=t_cj1[:, :], in0=t_cj1[:, :], in1=am,
                                     op=ALU.add))
        vd(lambda e, m_o2=m_o2: e.tensor_tensor(out=t_cj1[:, :], in0=t_cj1[:, :],
                                                in1=m_o2, op=ALU.max))
        tok_v = vd(lambda e, pp=pp: e.tensor_tensor(out=t_cj1[:, :], in0=t_cj1[:, :],
                                                    in1=t_mk[pp][:, :],
                                                    op=ALU.subtract))
        tok_s = sd(lambda e: e.activation(out=t_cj1[:, :], in_=t_cj1[:, :],
                                          func=AF.Exp, bias=0.0),
                   waits=[("s_V", tok_v)])
        vd(lambda e, ia=ia: e.tensor_tensor(out=ia, in0=ia, in1=t_cj1[:, :],
                                            op=ALU.mult), waits=[("s_S", tok_s)])
        tok_inc[pp] = vd(lambda e, ibb=ibb: e.tensor_tensor(
            out=ibb, in0=ibb, in1=t_cj1[:, :], op=ALU.mult))
        if debug:
            p.dma("gpsimd", sp_inc[pp], t_inc[pp][:, :],
                  waits=[("s_V", tok_inc[pp])])

    srA_t = AR.view(rE, [JC, TW])
    xinA_t = AR.view(rB, [JC, TW])
    ekA_t = AR.view(rC, [JC, TW])
    Aarr = AR.view(rD, [JC, TW])
    rwkv_t = ABF.view(bR1, [JC, TW])
    par_tok = {0: list(prev["betafree"][0]), 1: list(prev["betafree"][1])}
    stA = {"ek": prev["ekfree"], "r": prev["rfree"], "xin": prev["xinfree"],
           "arrD": [], "rwkv": []}
    last_pass = [None]

    def stageA_front(pp, resident):
        par = pp % 2
        betaA_t = AR.view(rA0 if par == 0 else rA1, [JC, TW])
        Barr = AR.view(rA1 if par == 0 else rA0, [JC, TW])
        if resident:
            tok_l1 = tok_l2 = tok_l3 = None
            rel = []
        else:
            tok_l2 = p.dma("gpsimd", betaA_t, sp_beta[pp], waits=par_tok[par])
            tok_l1 = p.dma("gpsimd", ekA_t, sp_ek[pp], waits=stA["ek"])
            tok_l3 = p.dma("gpsimd", srA_t, sp_r[pp], waits=stA["r"])
            rel = [tok_l3]
        tok_sr = si(lambda e, srA_t=srA_t: e.activation(
            out=srA_t, in_=srA_t, func=AF.Sigmoid, bias=0.0), waits=rel)
        tok_sa = None
        for j2 in range(JC):
            w0 = []
            if j2 == 0:
                w0 = [("s_V", tok_inc[pp])] + stA["arrD"]
                if tok_l2 is not None:
                    w0.append(tok_l2)
            tok_sa = vi(lambda e, j=j2, pp=pp, betaA_t=betaA_t: e.tensor_tensor_scan(
                Aarr[:, j, :], ew_bcast(j, TW), betaA_t[:, j, :],
                t_inc[pp][:, j:j + 1], ALU.mult, ALU.add), waits=w0)
        for j2 in range(JC):
            w0 = list(par_tok[1 - par]) if j2 == 0 else []
            if j2 == 0 and tok_l1 is not None:
                w0.append(tok_l1)
            tok_sa = vi(lambda e, j=j2, pp=pp, Barr=Barr: e.tensor_tensor_scan(
                Barr[:, j, :], ew_bcast(j, TW), ekA_t[:, j, :],
                t_inc[pp][:, JC + j:JC + j + 1], ALU.mult, ALU.add), waits=w0)
        for j2 in range(JC):
            vi(lambda e, j=j2, betaA_t=betaA_t: e.scalar_tensor_tensor(
                out=betaA_t[:, j, 1:TW], in0=betaA_t[:, j, 1:TW],
                scalar=t_eu[:, j:j + 1], in1=Aarr[:, j, 0:TW - 1],
                op0=ALU.mult, op1=ALU.add),
                waits=[("s_V", tok_sa), ("s_S", tok_eu)] if j2 == 0 else ())
            vi(lambda e, j=j2, pp=pp, betaA_t=betaA_t: e.scalar_tensor_tensor(
                out=betaA_t[:, j, 0:1], in0=betaA_t[:, j, 0:1],
                scalar=t_eu[:, j:j + 1], in1=t_inc[pp][:, j:j + 1],
                op0=ALU.mult, op1=ALU.add))
        tok_nd = None
        for j2 in range(JC):
            vi(lambda e, j=j2, Barr=Barr: e.scalar_tensor_tensor(
                out=ekA_t[:, j, 1:TW], in0=ekA_t[:, j, 1:TW],
                scalar=t_eu[:, j:j + 1], in1=Barr[:, j, 0:TW - 1],
                op0=ALU.mult, op1=ALU.add))
            tok_nd = vi(lambda e, j=j2, pp=pp: e.scalar_tensor_tensor(
                out=ekA_t[:, j, 0:1], in0=ekA_t[:, j, 0:1],
                scalar=t_eu[:, j:j + 1], in1=t_inc[pp][:, JC + j:JC + j + 1],
                op0=ALU.mult, op1=ALU.add))
        par_tok[1 - par] = [("s_V", tok_nd)]
        stA["arrD"] = [("s_V", tok_nd)]
        ek_flat = ekA_t.rearrange("p a b -> p (a b)")
        si(lambda e: e.activation(out=ek_flat, in_=ek_flat, func=AF.Ln,
                                  bias=0.0), waits=[("s_V", tok_nd)])
        tok_rc = sd(lambda e: e.activation(out=ek_flat, in_=ek_flat, func=AF.Exp,
                                           bias=0.0, scale=-1.0))
        tok_y = vi(lambda e, betaA_t=betaA_t: e.tensor_tensor(
            out=betaA_t.rearrange("p a b -> p (a b)"),
            in0=betaA_t.rearrange("p a b -> p (a b)"), in1=ek_flat, op=ALU.mult),
            waits=[("s_S", tok_rc)])
        stA["ek"] = [("s_V", tok_y)]
        drain_tasks(24)
        return dict(pp=pp, par=par, betaA_t=betaA_t, tok_sr=tok_sr)

    def stageA_back(ctx, front_next):
        pp, par, betaA_t = ctx["pp"], ctx["par"], ctx["betaA_t"]
        tok_rwkv = vi(lambda e, betaA_t=betaA_t: e.tensor_tensor(
            out=rwkv_t.rearrange("p a b -> p (a b)"),
            in0=betaA_t.rearrange("p a b -> p (a b)"),
            in1=srA_t.rearrange("p a b -> p (a b)"), op=ALU.mult),
            waits=[("s_S", ctx["tok_sr"])] + stA["rwkv"])
        par_tok[par] = [("s_V", tok_rwkv)]
        stA["r"] = [("s_V", tok_rwkv)]
        views, vtoks = wo_res["views"], wo_res["vtoks"]
        mminfo = []
        lastmm = None
        for mc in range(JC):
            ib, bnk, wts = get_bank()
            for kc in range(JC):
                lastmm = ti(lambda e, bnk=bnk, v=views[kc], mc=mc, kc=kc: e.matmul(
                    bnk[:, 0:TW], v[:, mc * 128:(mc + 1) * 128], rwkv_t[:, kc, :],
                    start=(kc == 0), stop=(kc == JC - 1)),
                    waits=(wts + [vtoks[kc], ("s_V", tok_rwkv)]) if kc == 0
                    else ([vtoks[kc]] if vtoks[kc] is not vtoks[kc - 1] else ()))
            mminfo.append((ib, bnk, lastmm))
        stA["rwkv"] = [("s_T", lastmm)]
        wo_res["last"] = lastmm
        ctx_next = front_next() if front_next is not None else None
        tok_x1 = None
        for mc, (ib, bnk, mm) in enumerate(mminfo):
            tok_x1 = vi(lambda e, bnk=bnk, mc=mc: e.tensor_tensor(
                out=xinA_t[:, mc, :], in0=xinA_t[:, mc, :], in1=bnk[:, 0:TW],
                op=ALU.add),
                waits=[("s_T", mm)] + (stA["xinload"] if mc == 0 else []))
            bank_done(ib, [("s_V", tok_x1)])
            drain_tasks(5)
        tok_col = vi(lambda e, pp=pp: e.tensor_copy(t_x1c[pp][:, :],
                                                    xinA_t[:, :, TW - 1:TW]),
                     waits=[("s_V", tok_x1)])
        tok_spx1 = p.dma("gpsimd", sp_x1[pp], xinA_t, waits=[("s_V", tok_col)])
        stA["xin"] = [tok_spx1]
        return tok_spx1, tok_col, ctx_next

    orderA = [NP - 1] + list(range(NP - 1))
    tok_sp_x1 = {}
    stA["xinload"] = []
    wo_res = {"views": [], "vtoks": [], "last": None, "slots": []}
    _kc = 0
    _slot = 0
    while _kc < JC:
        _kh = min(JC, _kc + WR_SLOT // C)
        wts = list(wslot_tok[_slot]) + [wcast_done["wo"]]
        wslot_tok[_slot] = []
        _v = wslot[_slot][:, 0:(_kh - _kc) * C].rearrange(
            "p (a c) -> p a c", a=_kh - _kc, c=C)
        _tok = p.dma("sync", wslot[_slot][:, 0:(_kh - _kc) * C],
                     sc_w["wo"][:, _kc * C:_kh * C], waits=wts)
        for _i in range(_kh - _kc):
            wo_res["views"].append(_v[:, _i, :])
            wo_res["vtoks"].append(_tok)
        wo_res["slots"].append(_slot)
        _slot += 1
        _kc = _kh
    first_res = not debug
    ctxA = stageA_front(orderA[0], resident=first_res)
    if not first_res:
        stA["xinload"] = [p.dma("gpsimd", xinA_t, sp_xin[orderA[0]],
                                waits=stA["xin"])]
    for idx, pp in enumerate(orderA):
        nxt = orderA[idx + 1] if idx + 1 < NP else None
        fe = (lambda nxt=nxt: stageA_front(nxt, resident=False)) \
            if nxt is not None else None
        tok_spx1, tok_col, ctxA_next = stageA_back(ctxA, fe)
        tok_sp_x1[pp] = tok_spx1
        if nxt is not None:
            stA["xinload"] = [p.dma("gpsimd", xinA_t, sp_xin[nxt],
                                    waits=[tok_spx1])]
            ctxA = ctxA_next
        if nxt is None:
            slots_release(wo_res["slots"], [("s_T", wo_res["last"])])
            ring_rr[0] = 0
        if pp == NP - 1:
            tok_pub = p.dma("gpsimd", cc_row_in.ap(), t_x1c[NP - 1][:, :],
                            waits=[("s_V", tok_col)])
            tok_cc2 = p.emit("gpsimd", lambda e: e.collective_compute(
                "AllGather", ALU.bypass, replica_groups=[list(range(NCORES))],
                ins=[cc_row_in.ap().opt()], outs=[cc_row_out.ap().opt()]),
                inc="s_cc", amt=1, waits=[tok_pub])
            rgtoks = []
            for r in range(NCORES):
                rgtoks.append(p.dma("gpsimd", t_rgath[:, r, :], cc_row_out.ap()[r],
                              waits=[("s_cc", tok_cc2)] if r == 0 else ()))
        drain_tasks(6)
    drain_all_tasks()
    tok_sel = vd(lambda e: e.tensor_scalar(out=t_rowa[:, :], in0=t_rgath[:, 0, :],
                                           scalar1=t_mrow[:, 0:1], scalar2=None,
                                           op0=ALU.mult), waits=[rgtoks[0]])
    for r in range(1, NCORES):
        tok_sel = vd(lambda e, r=r: e.scalar_tensor_tensor(
            out=t_rowa[:, :], in0=t_rgath[:, r, :], scalar=t_mrow[:, r:r + 1],
            in1=t_rowa[:, :], op0=ALU.mult, op1=ALU.add), waits=[rgtoks[r]])
    row_ln(t_rowa[:, :], t_rowa[:, :], g=PV["ln2g"][:, :], b=PV["ln2b"][:, :],
           waits=[("s_V", tok_sel)])
    tok_gcol0 = vd(lambda e: e.tensor_scalar(out=t_gcol[:, :], in0=t_rowa[:, :],
                                             scalar1=t_pvalid[:, :], scalar2=None,
                                             op0=ALU.mult))

    x1B_par = [AR.view(rA0, [JC, TW]), AR.view(rA1, [JC, TW])]
    gB_t = AR.view(rB, [JC, TW])
    d2B_t = AR.view(rD, [JC, TW])
    sigB_t = AR.view(rC, [JC, TW])
    xB_t = AR.view(rE, [JC, TW])
    outB_t = AR.view(rE, [JC, TW])
    relu_st = AR.view(rS3, [TW])
    ffn_st = AR.view(rS3, [TW])
    gk_t = ABF.view(bR1, [JC, TW])
    gr_t = ABF.view(bR1 + TWJC, [JC, TW])
    kf_t = ABF.view(bR2, [JF, TW])
    h2_t = ABF.view(bR2, [JF, TW])
    stg3a = ABF.view(bR2, [JC, TW])
    stg3b = ABF.view(bR2 + TWJC, [JC, TW])
    xf_t = ABF.view(bR5, [JC, TW])
    xa_t = ABF.view(bR4, [TW])
    xa2_t = ABF.view(bR4 + TW, [TW])

    stB = {"x1p0": [], "x1p1": [], "g": [], "d2": [], "sig": [], "x": [],
           "out": [], "gk": [], "kf": [], "xf": [], "bfc": [], "bproj": [],
           "stage": [], "mfree": []}

    def stageB_pass(pp):
        x1B_t = x1B_par[pp % 2]
        gcol = t_gcol if pp == 0 else t_gc[pp - 1]
        gcol_tok = [("s_V", tok_gcol0)] if pp == 0 else stB[f"gcol{pp - 1}"]
        tok_lx1 = p.dma("gpsimd", x1B_t, sp_x1[pp],
                        waits=[tok_sp_x1[pp]] + stB[f"x1p{pp % 2}"])
        tok_rstd, _ = ln_stats(x1B_t, stg3a, stg3b, waits=[tok_lx1],
                               stage_free=stB["stage"], m_free=stB["mfree"])
        tok_g = ln_apply(x1B_t, gB_t, tok_rstd, g=PV["ln2g"], b=PV["ln2b"],
                         waits=stB["g"])
        tok_gc = vi(lambda e, pp=pp: e.tensor_copy(t_gc[pp][:, :],
                                                   gB_t[:, :, TW - 1:TW]),
                    waits=[("s_V", tok_g)])
        stB[f"gcol{pp}"] = [("s_V", tok_gc)]
        if debug:
            p.dma("gpsimd", sp_g[pp], gB_t, waits=[("s_V", tok_gc)])
        for j in range(JC):
            vi(lambda e, j=j: e.tensor_tensor(
                out=d2B_t[:, j, 1:TW], in0=gB_t[:, j, 1:TW], in1=gB_t[:, j, 0:TW - 1],
                op=ALU.subtract), waits=stB["d2"] if j == 0 else ())
            vd(lambda e, j=j, gcol=gcol: e.tensor_tensor(
                out=d2B_t[:, j, 0:1], in0=gB_t[:, j, 0:1], in1=gcol[:, j:j + 1],
                op=ALU.subtract), waits=gcol_tok if j == 0 else ())
        tok_mix = None
        for fm, xo in ((PV["fmk"], gk_t), (PV["fmr"], gr_t)):
            for j in range(JC):
                vi(lambda e, j=j, fm=fm, xo=xo: e.scalar_tensor_tensor(
                    out=xo[:, j, 1:TW], in0=d2B_t[:, j, 1:TW], scalar=fm[:, j:j + 1],
                    in1=gB_t[:, j, 0:TW - 1], op0=ALU.mult, op1=ALU.add),
                    waits=stB["gk"] if (j == 0 and xo is gk_t) else ())
                tok_mix = vd(lambda e, j=j, fm=fm, xo=xo, gcol=gcol:
                             e.scalar_tensor_tensor(
                    out=xo[:, j, 0:1], in0=d2B_t[:, j, 0:1], scalar=fm[:, j:j + 1],
                    in1=gcol[:, j:j + 1], op0=ALU.mult, op1=ALU.add))
        views, vtoks, slots = stream_chunks("fr", 0, JC)
        sig_toks = [None] * JC
        lastmm = None
        for mc in range(JC):
            ib, bnk, wts = get_bank()
            for kc in range(JC):
                lastmm = ti(lambda e, bnk=bnk, v=views[kc], mc=mc, kc=kc: e.matmul(
                    bnk[:, 0:TW], v[:, mc * 128:(mc + 1) * 128], gr_t[:, kc, :],
                    start=(kc == 0), stop=(kc == JC - 1)),
                    waits=(wts + [vtoks[kc], ("s_V", tok_mix)]) if kc == 0
                    else ([vtoks[kc]] if vtoks[kc] is not vtoks[kc - 1] else ()))
            tok_sg = si(lambda e, bnk=bnk, mc=mc: e.activation(
                out=sigB_t[:, mc, :], in_=bnk[:, 0:TW], func=AF.Sigmoid, bias=0.0),
                waits=[("s_T", lastmm)] + (stB["sig"] if mc == 0 else []))
            sig_toks[mc] = tok_sg
            bank_done(ib, [("s_S", tok_sg)])
        slots_release(slots, [("s_T", lastmm)])
        kf_toks = []
        NG = JF // 8
        for g8 in range(NG):
            views, vtoks, slots = stream_cols("fk", g8)
            for jfo in range(8):
                jf = g8 * 8 + jfo
                ib, bnk, wts = get_bank()
                for kc in range(JC):
                    lastmm = ti(lambda e, bnk=bnk, v=views[kc], jfo=jfo, kc=kc:
                                e.matmul(
                        bnk[:, 0:TW], v[:, jfo * 128:(jfo + 1) * 128],
                        gk_t[:, kc, :], start=(kc == 0), stop=(kc == JC - 1)),
                        waits=(wts + [vtoks[kc]]) if (kc == 0 and jfo == 0)
                        else ([vtoks[kc]] if jfo == 0 else ()))
                tok_rl = si(lambda e, bnk=bnk: e.activation(
                    out=relu_st[:, :], in_=bnk[:, 0:TW], func=AF.Relu, bias=0.0),
                    waits=[("s_T", lastmm)] + (stB["kf"] if jf == 0 else []))
                tok_kf = si(lambda e, jf=jf: e.activation(
                    out=kf_t[:, jf, :], in_=relu_st[:, :], func=AF.Square, bias=0.0))
                kf_toks.append(tok_kf)
                bank_done(ib, [("s_S", tok_kf)])
            slots_release(slots, [("s_T", lastmm)])
        tok_after_fk = ("s_T", p.counts["s_T"])
        banks, bwaits = get_all_banks()
        x2_toks = [None] * JC
        for g8 in range(NG):
            views, vtoks, slots = stream_chunks("fv", g8 * 8, g8 * 8 + 8)
            for kci in range(8):
                kc = g8 * 8 + kci
                for mc in range(JC):
                    w8 = []
                    if g8 == 0 and kci == 0 and mc == 0:
                        w8 = bwaits + [vtoks[kci], ("s_S", kf_toks[-1])]
                    elif mc == 0 and (kci == 0 or vtoks[kci] is not vtoks[kci - 1]):
                        w8 = [vtoks[kci]]
                    lastmm = ti(lambda e, v=views[kci], mc=mc, kc=kc: e.matmul(
                        pbank[mc][:, 0:TW], v[:, mc * 128:(mc + 1) * 128],
                        kf_t[:, kc, :], start=(kc == 0), stop=(kc == JF - 1)),
                        waits=w8)
            slots_release(slots, [("s_T", lastmm)])
        for mc in range(JC):
            vi(lambda e, mc=mc: e.tensor_tensor(
                out=sigB_t[:, mc, :], in0=sigB_t[:, mc, :], in1=pbank[mc][:, 0:TW],
                op=ALU.mult), waits=[("s_T", lastmm)] if mc == 0 else ())
            tok_x2 = vi(lambda e, mc=mc: e.tensor_tensor(
                out=x1B_t[:, mc, :], in0=x1B_t[:, mc, :], in1=sigB_t[:, mc, :],
                op=ALU.add))
            x2_toks[mc] = tok_x2
            bank_done(mc, [("s_V", tok_x2)])
        tok_lx = p.dma("gpsimd", xB_t, xin[pp], waits=stB["x"])
        tok_xa = vi(lambda e: e.tensor_tensor(
            out=x1B_t.rearrange("p a b -> p (a b)"),
            in0=x1B_t.rearrange("p a b -> p (a b)"),
            in1=xB_t.rearrange("p a b -> p (a b)"), op=ALU.add),
            waits=[tok_lx, ("s_V", x2_toks[-1])])
        if debug:
            p.dma("gpsimd", sp_xaf[pp], x1B_t, waits=[("s_V", tok_xa)])
            p.dma("gpsimd", sp_sig[pp], sigB_t, waits=[("s_V", x2_toks[-1])])
        tok_rstd, tok_stg3 = ln_stats(x1B_t, stg3a, stg3b,
                                      waits=[("s_V", tok_xa),
                                             ("s_S", kf_toks[-1])])
        tok_xf = ln_apply(x1B_t, xf_t, tok_rstd, waits=stB["xf"], scratch=d2B_t)
        ib, bnk, wts = get_bank()
        for kc in range(JC):
            lastmm = ti(lambda e, bnk=bnk, kc=kc: e.matmul(
                bnk[0:LORA, 0:TW], afcb[:, kc, :], xf_t[:, kc, :],
                start=(kc == 0), stop=(kc == JC - 1)),
                waits=(wts + [("s_V", tok_xf)]) if kc == 0 else ())
        tok_xa1 = si(lambda e, bnk=bnk: e.activation(
            out=xa_t[0:LORA, :], in_=bnk[0:LORA, 0:TW], func=AF.Identity,
            bias=0.0, scale=2.0), waits=[("s_T", lastmm)] + stB["bfc"])
        bank_done(ib, [("s_S", tok_xa1)])
        tok_bfc = p.dma("sync", bfcb[:, :], sc_bfc,
                        waits=[wcast_done["bfc"]] + stB["bfc"])
        h2_toks = []
        for g8 in range(NG):
            views, vtoks, slots = stream_cols("wfc", g8)
            for jfo in range(8):
                jf = g8 * 8 + jfo
                ib, bnk, wts = get_bank()
                for kc in range(JC):
                    ti(lambda e, bnk=bnk, v=views[kc], jfo=jfo, kc=kc: e.matmul(
                        bnk[:, 0:TW], v[:, jfo * 128:(jfo + 1) * 128],
                        xf_t[:, kc, :], start=(kc == 0), stop=False),
                        waits=(wts + [vtoks[kc]]) if (kc == 0 and jfo == 0)
                        else ([vtoks[kc]] if jfo == 0 else ()))
                lastmm = ti(lambda e, bnk=bnk, jf=jf: e.matmul(
                    bnk[:, 0:TW], bfcb[:, jf * 128:(jf + 1) * 128], xa_t[0:LORA, :],
                    start=False, stop=True),
                    waits=[tok_bfc, ("s_S", tok_xa1)]
                    if (g8 == 0 and jfo == 0) else ())
                tok_h2 = si(lambda e, bnk=bnk, jf=jf: e.activation(
                    out=h2_t[:, jf, :], in_=bnk[:, 0:TW], func=AF.Gelu,
                    bias=t_bfcv[:, jf:jf + 1], scale=1.0),
                    waits=[("s_T", lastmm)])
                h2_toks.append(tok_h2)
                bank_done(ib, [("s_S", tok_h2)])
            slots_release(slots, [("s_T", lastmm)])
        tok_after_wfc = ("s_T", p.counts["s_T"])
        if debug:
            p.dma("gpsimd", sp_xf[pp], xf_t, waits=[("s_V", tok_xf)])
            p.dma("gpsimd", sp_h2[pp], h2_t, waits=[("s_S", h2_toks[-1])])
        ib, bnk, wts = get_bank()
        for kc in range(JF):
            lastmm = ti(lambda e, bnk=bnk, kc=kc: e.matmul(
                bnk[0:LORA, 0:TW], aprojb[:, kc, :], h2_t[:, kc, :],
                start=(kc == 0), stop=(kc == JF - 1)),
                waits=(wts + [("s_S", h2_toks[-1])]) if kc == 0 else ())
        tok_xa2 = si(lambda e, bnk=bnk: e.activation(
            out=xa2_t[0:LORA, :], in_=bnk[0:LORA, 0:TW], func=AF.Identity,
            bias=0.0, scale=2.0), waits=[("s_T", lastmm)])
        bank_done(ib, [("s_S", tok_xa2)])
        tok_bproj = p.dma("sync", bprojb[:, :], sc_bproj,
                          waits=[wcast_done["bproj"]] + stB["bproj"])
        banks, bwaits = get_all_banks()
        for g8 in range(NG):
            views, vtoks, slots = stream_chunks("wproj", g8 * 8, g8 * 8 + 8)
            for kci in range(8):
                kc = g8 * 8 + kci
                for mc in range(JC):
                    w8 = []
                    if g8 == 0 and kci == 0 and mc == 0:
                        w8 = bwaits + [vtoks[kci]]
                    elif mc == 0 and (kci == 0 or vtoks[kci] is not vtoks[kci - 1]):
                        w8 = [vtoks[kci]]
                    lastmm = ti(lambda e, v=views[kci], mc=mc, kc=kc: e.matmul(
                        pbank[mc][:, 0:TW], v[:, mc * 128:(mc + 1) * 128],
                        h2_t[:, kc, :], start=(kc == 0), stop=False),
                        waits=w8)
            slots_release(slots, [("s_T", lastmm)])
        for mc in range(JC):
            lastmm = ti(lambda e, mc=mc: e.matmul(
                pbank[mc][:, 0:TW], bprojb[:, mc * 128:(mc + 1) * 128],
                xa2_t[0:LORA, :], start=False, stop=True),
                waits=[tok_bproj, ("s_S", tok_xa2)] if mc == 0 else ())
        tok_out = None
        prev_out = [None, None]
        fst = [ffn_st, AR.view(rS1, [TW])]
        for mc in range(JC):
            fb = fst[mc % 2]
            w9 = [("s_T", lastmm)] + (stB["out"] if mc == 0 else [])
            if prev_out[mc % 2] is not None:
                w9.append(("s_V", prev_out[mc % 2]))
            tok_f = si(lambda e, mc=mc, fb=fb: e.activation(
                out=fb[:, :], in_=pbank[mc][:, 0:TW], func=AF.Identity,
                bias=t_bprojv[:, mc:mc + 1], scale=1.0), waits=w9)
            tok_out = vi(lambda e, mc=mc, fb=fb: e.tensor_tensor(
                out=outB_t[:, mc, :], in0=x1B_t[:, mc, :], in1=fb[:, :],
                op=ALU.add), waits=[("s_S", tok_f)])
            prev_out[mc % 2] = tok_out
            bank_done(mc, [("s_V", tok_out)])
        tok_o = p.dma("sync", out[pp], outB_t, waits=[("s_V", tok_out)])
        stB[f"x1p{pp % 2}"] = [("s_V", tok_out)]
        stB["g"] = [("s_V", tok_mix)]
        stB["d2"] = [("s_V", tok_xf)]
        stB["sig"] = [("s_V", x2_toks[-1])]
        stB["x"] = [("s_V", tok_xa), tok_o]
        stB["out"] = [tok_o]
        stB["gk"] = [tok_after_fk]
        stB["kf"] = [("s_S", h2_toks[-1] if h2_toks else kf_toks[-1])]
        stB["xf"] = [tok_after_wfc]
        stB["bfc"] = [("s_T", p.counts["s_T"])]
        stB["bproj"] = [("s_T", p.counts["s_T"])]
        stB["stage"] = [tok_stg3]
        stB["mfree"] = [("s_V", tok_xf)]
        return tok_o

    vnow = [("s_V", p.counts["s_V"])]
    snow = [("s_S", p.counts["s_S"])]
    tnow = [("s_T", p.counts["s_T"])]
    stB["x1p0"] = vnow
    stB["x1p1"] = vnow

    stB["g"] = vnow + [tok_sp_x1[pp2] for pp2 in range(NP)]
    stB["d2"] = vnow
    stB["sig"] = vnow + snow
    stB["x"] = vnow
    stB["out"] = vnow
    stB["kf"] = vnow
    stB["gk"] = vnow + tnow

    tok_os = []
    for pp in range(NP):
        tok_os.append(stageB_pass(pp))

    p.emit("sync", lambda e: None, waits=tok_os)

    with nc.Block() as block:
        @block.sync
        def _(eng):
            for ev in p.events["sync"]:
                ev(eng)

        @block.tensor
        def _(eng):
            for ev in p.events["tensor"]:
                ev(eng)

        @block.vector
        def _(eng):
            for ev in p.events["vector"]:
                ev(eng)

        @block.scalar
        def _(eng):
            for ev in p.events["scalar"]:
                ev(eng)

        @block.gpsimd
        def _(eng):
            for ev in p.events["gpsimd"]:
                ev(eng)

    es.close()
    return nc


def _featmaj_vec(v, j):
    return np.ascontiguousarray(np.asarray(v, dtype=np.float32).reshape(j, 128).T)


def _prep_inputs(inputs, cfg):
    C, DF, NP, TW, LORA = cfg["C"], cfg["DF"], cfg["NP"], cfg["TW"], cfg["LORA"]
    JC, JF = C // 128, DF // 128
    TC = NP * TW
    QPS = NCORES // 2
    g = lambda k: np.asarray(inputs[k], dtype=np.float32)

    x = g("L0_attn_x")
    B, T, _ = x.shape
    assert T == QPS * TC and B == 2

    shared = {}
    wT = lambda a: np.asarray(a, np.float32).T

    def pmaj(a, kcn, m):
        return np.ascontiguousarray(
            a.reshape(kcn, 128, m).transpose(1, 0, 2).reshape(128, kcn * m))

    def pmaj_grp(a, kcn, m, gw=1024):
        ng = m // gw
        return np.ascontiguousarray(
            a.reshape(kcn, 128, ng, gw).transpose(1, 2, 0, 3).reshape(
                128, ng * kcn * gw))

    shared["wk"] = pmaj(wT(g("L0_attn_Wk")), JC, C)
    shared["wv"] = pmaj(wT(g("L0_attn_Wv")), JC, C)
    shared["wr"] = pmaj(wT(g("L0_attn_Wr")), JC, C)
    shared["wo"] = pmaj(wT(g("L0_attn_Wo")), JC, C)
    shared["fr"] = pmaj(wT(g("L0_attn_Fr")), JC, C)
    shared["fk"] = pmaj_grp(wT(g("L0_attn_Fk")), JC, DF)
    shared["fv"] = pmaj(wT(g("L0_attn_Fv")), JF, C)
    shared["wfc"] = pmaj_grp(wT(g("L0_ffn_W_fc")), JC, DF)
    shared["wproj"] = pmaj(wT(g("L0_ffn_W_proj")), JF, C)
    shared["afc"] = pmaj(wT(g("L0_ffn_A_fc")), JC, LORA)
    shared["aproj"] = pmaj(wT(g("L0_ffn_A_proj")), JF, LORA)
    shared["bfc"] = np.ascontiguousarray(wT(g("L0_ffn_B_fc")))
    shared["bproj"] = np.ascontiguousarray(wT(g("L0_ffn_B_proj")))
    shared["ln1g"] = _featmaj_vec(g("L0_attn_ln1_g"), JC)
    shared["ln1b"] = _featmaj_vec(g("L0_attn_ln1_b"), JC)
    shared["ln2g"] = _featmaj_vec(g("L0_attn_ln2_g"), JC)
    shared["ln2b"] = _featmaj_vec(g("L0_attn_ln2_b"), JC)
    shared["tmk"] = _featmaj_vec(g("L0_attn_tm_k"), JC)
    shared["tmv"] = _featmaj_vec(g("L0_attn_tm_v"), JC)
    shared["tmr"] = _featmaj_vec(g("L0_attn_tm_r"), JC)
    shared["fmk"] = _featmaj_vec(g("L0_attn_fm_k"), JC)
    shared["fmr"] = _featmaj_vec(g("L0_attn_fm_r"), JC)
    shared["tdecay"] = _featmaj_vec(g("L0_attn_time_decay"), JC)
    shared["tfirst"] = _featmaj_vec(g("L0_attn_time_first"), JC)
    shared["bfcv"] = _featmaj_vec(g("L0_ffn_b_fc"), JF)
    shared["bprojv"] = _featmaj_vec(g("L0_ffn_b_proj"), JC)

    in_maps = []
    for core in range(NCORES):
        s, q = core // QPS, core % QPS
        xs = x[s, q * TC:(q + 1) * TC, :]
        xs = np.ascontiguousarray(
            xs.reshape(NP, TW, JC, 128).transpose(0, 3, 2, 1))
        if q > 0:
            xp = np.ascontiguousarray(x[s, q * TC - 1, :].reshape(JC, 128).T)
            pvalid = np.ones((128, 1), np.float32)
        else:
            xp = np.zeros((128, JC), np.float32)
            pvalid = np.zeros((128, 1), np.float32)
        mpre = np.zeros(NCORES, np.float32)
        for r in range(NCORES):
            if r // QPS == s and r % QPS < q:
                mpre[r] = 1.0
        mrow = np.zeros(NCORES, np.float32)
        if q > 0:
            mrow[core - 1] = 1.0
        m = dict(shared)
        m["xin"] = xs
        m["xprev"] = xp
        m["prevvalid"] = pvalid
        m["maskpre"] = np.ascontiguousarray(np.broadcast_to(mpre, (128, NCORES)))
        m["maskbig"] = np.ascontiguousarray(
            np.broadcast_to((mpre - 1.0) * 1e30, (128, NCORES)).astype(np.float32))
        m["maskrow"] = np.ascontiguousarray(np.broadcast_to(mrow, (128, NCORES)))
        in_maps.append(m)
    return in_maps


def run(inputs, cfg=FULL_CFG, debug=False):
    C, NP, TW = cfg["C"], cfg["NP"], cfg["TW"]
    JC = C // 128
    TC = NP * TW
    QPS = NCORES // 2
    nc = build_program(cfg, debug=debug)
    in_maps = _prep_inputs(inputs, cfg)
    res = run_bass_kernel_spmd(nc, in_maps, core_ids=list(range(NCORES)))
    T = QPS * TC
    outx = np.empty((2, T, C), dtype=np.float32)
    for core in range(NCORES):
        s, q = core // QPS, core % QPS
        o = np.asarray(res.results[core]["out"])
        outx[s, q * TC:(q + 1) * TC, :] = o.transpose(0, 3, 2, 1).reshape(TC, C)
    return outx, res


def kernel(**inputs):
    outx, _ = run(inputs, FULL_CFG, debug=False)
    return outx


# version 17
# speedup vs baseline: 1.0102x; 1.0102x over previous
import numpy as np
from contextlib import ExitStack

import concourse.bass as bass
import concourse.mybir as mybir
from concourse.bass_utils import run_bass_kernel_spmd

F32 = mybir.dt.float32
BF16 = mybir.dt.bfloat16
I32 = mybir.dt.int32
AF = mybir.ActivationFunctionType
ALU = mybir.AluOpType

NCORES = 8
EPS = 1e-5
NEG_BIG = -1e30

FULL_CFG = dict(C=1024, DF=4096, NP=4, TW=512, LORA=16)


class Prog:
    ENGINES = ("sync", "tensor", "vector", "scalar", "gpsimd")

    def __init__(self, nc, es):
        self.nc = nc
        self.es = es
        self.events = {e: [] for e in self.ENGINES}
        self.sems = {}
        self.counts = {}

    def sem(self, name):
        self.sems[name] = self.es.enter_context(self.nc.semaphore(name))
        self.counts[name] = 0

    def emit(self, engine, fn, inc=None, amt=1, waits=()):
        waits = tuple(waits)
        post = None
        if inc is not None:
            self.counts[inc] += amt
            post = self.counts[inc]
        sems = self.sems

        def ev(eng, fn=fn, waits=waits, inc=inc, amt=amt):
            for s, v in waits:
                eng.wait_ge(sems[s], v)
            inst = fn(eng)
            if inc is not None:
                inst.then_inc(sems[inc], amt)
        self.events[engine].append(ev)
        return post

    def init_dma_pool(self, engine, n):
        names = []
        for i in range(n):
            nm = f"d_{engine}{i}"
            self.sem(nm)
            names.append(nm)
        self.dma_state = getattr(self, "dma_state", {})
        self.dma_state[engine] = {"sems": names, "rr": 0, "last": [None] * n}

    def dma(self, engine, out, in_, inc=None, waits=()):
        st = self.dma_state[engine]
        n = len(st["sems"])
        i = st["rr"] % n
        st["rr"] += 1
        sem = st["sems"][i]
        w = list(waits)
        if st["last"][i] is not None:
            w.append(st["last"][i])
        v = self.emit(engine, lambda eng, o=out, i_=in_: eng.dma_start(out=o, in_=i_),
                      inc=sem, amt=16, waits=w)
        st["last"][i] = (sem, v)
        return (sem, v)


class Arena:

    def __init__(self, nc, es, name, n_el, dtype):
        self.t = es.enter_context(nc.sbuf_tensor(name, [128, n_el], dtype))
        self.n_el = n_el
        self.off = 0

    def region(self, n):
        off = (self.off + 15) // 16 * 16
        assert off + n <= self.n_el, f"arena {self.t} overflow: {off}+{n}>{self.n_el}"
        self.off = off + n
        return off

    def view(self, off, shape):
        n = int(np.prod(shape))
        ap = self.t[:, off:off + n]
        if len(shape) == 2:
            ap = ap.rearrange("p (a b) -> p a b", a=shape[0], b=shape[1])
        elif len(shape) == 3:
            ap = ap.rearrange("p (a b c) -> p a b c", a=shape[0], b=shape[1], c=shape[2])
        return ap

    def alloc(self, shape):
        return self.view(self.region(int(np.prod(shape))), shape)


def build_program(cfg=FULL_CFG, debug=False):
    C, DF, NP, TW, LORA = cfg["C"], cfg["DF"], cfg["NP"], cfg["TW"], cfg["LORA"]
    JC = C // 128
    JF = DF // 128

    nc = bass.Bass("TRN2", target_bir_lowering=False, detect_race_conditions=False)
    es = ExitStack()
    p = Prog(nc, es)

    def din(name, shape, dt=F32):
        return nc.dram_tensor(name, shape, dt, kind="ExternalInput").ap()

    xin = din("xin", [NP, 128, JC, TW])
    xprev = din("xprev", [128, JC])
    prevvalid = din("prevvalid", [128, 1])
    maskpre = din("maskpre", [128, NCORES])
    maskbig = din("maskbig", [128, NCORES])
    maskrow = din("maskrow", [128, NCORES])

    W_INNER = {"wk": JC * C, "wv": JC * C, "wr": JC * C, "wo": JC * C,
               "fr": JC * C, "fk": JC * DF, "fv": JF * C,
               "wfc": JC * DF, "wproj": JF * C,
               "afc": JC * LORA, "aproj": JF * LORA}
    w_in = {nm: din(nm, [128, inner]) for nm, inner in W_INNER.items()}
    w_in["bfc"] = din("bfc", [LORA, DF])
    w_in["bproj"] = din("bproj", [LORA, C])

    PVN = ["ln1g", "ln1b", "ln2g", "ln2b", "tmk", "tmv", "tmr", "fmk", "fmr",
           "tdecay", "tfirst"]
    pvin = {nm: din(nm, [128, JC]) for nm in PVN}
    bfcv_in = din("bfcv", [128, JF])
    bprojv_in = din("bprojv", [128, JC])

    out = nc.dram_tensor("out", [NP, 128, JC, TW], F32, kind="ExternalOutput").ap()

    def dintt(name, shape, dt=F32, dbg=False):
        kind = "ExternalOutput" if (debug and dbg) else "Internal"
        return nc.dram_tensor(name, shape, dt, kind=kind)

    sp_ek = dintt("sp_ek", [NP, 128, JC, TW], dbg=True).ap()
    sp_beta = dintt("sp_beta", [NP, 128, JC, TW], dbg=True).ap()
    sp_r = dintt("sp_r", [NP, 128, JC, TW], dbg=True).ap()
    sp_xin = dintt("sp_xin", [NP, 128, JC, TW], dbg=True).ap()
    sp_x1 = dintt("sp_x1", [NP, 128, JC, TW], dbg=True).ap()
    sp_h = dintt("sp_h", [NP, 128, JC, TW], dbg=True).ap()
    sp_xk = dintt("sp_xk", [NP, 128, JC, TW], BF16, dbg=True).ap()
    sp_inc = dintt("sp_inc", [NP, 128, 2 * JC], dbg=True).ap()
    sp_acc = dintt("sp_acc", [128, 3 * JC], dbg=True).ap()
    sp_g = dintt("sp_g", [NP, 128, JC, TW], dbg=True).ap()
    sp_xaf = dintt("sp_xaf", [NP, 128, JC, TW], dbg=True).ap()
    sp_sig = dintt("sp_sig", [NP, 128, JC, TW], dbg=True).ap()
    sp_xf = dintt("sp_xf", [NP, 128, JC, TW], BF16, dbg=True).ap()
    sp_h2 = dintt("sp_h2", [NP, 128, JF, TW], BF16, dbg=True).ap()
    sp_k = dintt("sp_k", [NP, 128, JC, TW], dbg=True).ap()
    sp_mk = dintt("sp_mk", [NP, 128, JC], dbg=True).ap()
    sp_nmk = dintt("sp_nmk", [NP, 128, JC], dbg=True).ap()
    SCW = ["wk", "wv", "wr", "wo", "fr", "fk", "fv", "wfc", "wproj"]
    sc_w = {nm: dintt("sc_" + nm, [128, W_INNER[nm]], BF16).ap() for nm in SCW}
    sc_bfc = dintt("sc_bfc", [LORA, DF], BF16).ap()
    sc_bproj = dintt("sc_bproj", [LORA, C], BF16).ap()
    cc_sum_in = dintt("cc_sum_in", [128, 3 * JC])
    cc_sum_out = dintt("cc_sum_out", [NCORES, 128, 3 * JC])
    dbg_sum = dintt("dbg_sum", [128, NCORES, 3 * JC], dbg=True).ap() if debug else None
    cc_row_in = dintt("cc_row_in", [128, JC])
    cc_row_out = dintt("cc_row_out", [NCORES, 128, JC])
    qdram = dintt("qdram", [128, 2])

    for s in ["s_cc", "s_T", "s_V", "s_S", "s_G"]:
        p.sem(s)
    p.init_dma_pool("sync", 10)
    p.init_dma_pool("scalar", 4)
    p.init_dma_pool("gpsimd", 10)

    sbt = lambda name, shape, dt=F32: es.enter_context(nc.sbuf_tensor(name, shape, dt))
    PV = {nm: sbt("t_" + nm, [128, JC]) for nm in PVN}
    t_bfcv = sbt("t_bfcv", [128, JF])
    t_bprojv = sbt("t_bprojv", [128, JC])
    t_xprev = sbt("t_xprev", [128, JC])
    t_pvalid = sbt("t_pvalid", [128, 1])
    t_mpre = sbt("t_mpre", [128, NCORES])
    t_mbig = sbt("t_mbig", [128, NCORES])
    t_mrow = sbt("t_mrow", [128, NCORES])
    t_w = sbt("t_w", [128, JC])
    t_ew = sbt("t_ew", [128, JC])
    t_eu = sbt("t_eu", [128, JC])
    t_tww = sbt("t_tww", [128, JC])
    t_spanw = sbt("t_spanw", [128, JC])
    t_eps = sbt("t_eps", [128, 1])
    t_ones = sbt("t_ones", [128, 128], BF16)
    t_hcol = sbt("t_hcol", [128, JC])
    t_gcol = sbt("t_gcol", [128, JC])
    t_mk = [sbt(f"t_mk{i}", [128, JC]) for i in range(NP)]
    t_negmk = [sbt(f"t_negmk{i}", [128, JC]) for i in range(NP)]
    t_za = sbt("t_za", [128, JC])
    t_zb = sbt("t_zb", [128, JC])
    t_run = [sbt(f"t_run{i}", [128, 3 * JC]) for i in range(NP + 1)]
    t_inc = [sbt(f"t_inc{i}", [128, 2 * JC]) for i in range(NP)]
    t_gath = sbt("t_gath", [128, NCORES, 3 * JC])
    t_rgath = sbt("t_rgath", [128, NCORES, JC])
    t_acc = sbt("t_acc", [128, 3 * JC])
    t_cj1 = sbt("t_cj1", [128, JC])
    t_cj2 = sbt("t_cj2", [128, JC])
    t_cj3 = sbt("t_cj3", [128, JC])
    t_qv = sbt("t_qv", [128, 2])
    t_qrow = sbt("t_qrow", [1, 256])
    t_qs = sbt("t_qs", [128, 2])
    t_qis = sbt("t_qis", [128, 2])
    t_rowbf = sbt("t_rowbf", [128, JC], BF16)
    t_rowbf2 = sbt("t_rowbf2", [128, JC], BF16)
    t_r1 = sbt("t_r1", [128, 1])
    t_r2 = sbt("t_r2", [128, 1])
    t_r3 = sbt("t_r3", [128, 1])
    t_rowa = sbt("t_rowa", [128, JC])
    t_x1c = [sbt(f"t_x1c{i}", [128, JC]) for i in range(NP)]
    t_gc = [sbt(f"t_gc{i}", [128, JC]) for i in range(NP)]
    t_zero = sbt("t_zero", [128, 1])

    afcb = sbt("afcb", [128, JC, LORA], BF16)
    aprojb = sbt("aprojb", [128, JF, LORA], BF16)

    WR_SLOT = 3072
    NSLOT = 3
    wring = sbt("wring", [128, NSLOT * WR_SLOT], BF16)
    wslot = [wring[:, i * WR_SLOT:(i + 1) * WR_SLOT] for i in range(NSLOT)]
    wslot_tok = [[] for _ in range(NSLOT)]
    ring_rr = [0]

    WST_N = 512
    NWST = 4
    WST = [sbt(f"wst{i}", [128, WST_N], F32) for i in range(NWST)]
    WSTI = sbt("wsti", [128, WST_N], I32)
    WSTO2 = [sbt(f"wsto{i}", [128, WST_N], BF16) for i in range(2)]
    WSTO = WSTO2[0]
    wst_free = [[] for _ in range(NWST)]
    wsto_free = [[], []]
    wsti_free = [[]]

    TWJC = JC * TW
    AR = Arena(nc, es, "arena_f", 6 * TWJC + 3 * TW + 64, F32)
    rA0 = AR.region(TWJC)
    rA1 = AR.region(TWJC)
    rB = AR.region(TWJC)
    rC = AR.region(TWJC)
    rD = AR.region(TWJC)
    rE = AR.region(TWJC)
    rS1 = AR.region(TW)
    rS2 = AR.region(TW)
    rS3 = AR.region(TW)
    ABF = Arena(nc, es, "arena_b", 8 * 1024 + 16 * 1024 + 1024 + 4096 + 32, BF16)
    bR1 = ABF.region(8192)
    bR2 = ABF.region(16384)
    bR4 = ABF.region(1024)
    bR5 = ABF.region(4096)
    bfcb = sbt("bfcb", [LORA, DF], BF16)
    bprojb = sbt("bprojb", [LORA, C], BF16)

    pbank = [es.enter_context(nc.psum_tensor(f"pb{i}", [128, 512], F32)) for i in range(8)]
    bank_tok = [[] for _ in range(8)]
    bank_rr = [0]

    def get_bank(extra_waits=()):
        i = bank_rr[0] % 8
        bank_rr[0] += 1
        w = list(bank_tok[i]) + list(extra_waits)
        bank_tok[i] = []
        return i, pbank[i], w

    def bank_done(i, toks):
        bank_tok[i] = list(toks)

    def get_all_banks():
        w = []
        for i in range(8):
            w += bank_tok[i]
            bank_tok[i] = []
        bank_rr[0] = 0
        return pbank, w

    def vi(fn, waits=()):
        return p.emit("vector", fn, inc="s_V", waits=waits)

    def vd(fn, waits=()):
        return p.emit("vector", fn, inc="s_V",
                      waits=list(waits) + [("s_V", p.counts["s_V"])])

    def sd(fn, waits=()):
        return p.emit("scalar", fn, inc="s_S",
                      waits=list(waits) + [("s_S", p.counts["s_S"])])

    def si(fn, waits=()):
        return p.emit("scalar", fn, inc="s_S", waits=waits)

    def ti(fn, waits=()):
        return p.emit("tensor", fn, inc="s_T", waits=waits)

    _ptoks = []
    for nm in PVN:
        _ptoks.append(p.dma("sync", PV[nm][:, :], pvin[nm]))
    _ptoks.append(p.dma("sync", t_bfcv[:, :], bfcv_in))
    _ptoks.append(p.dma("sync", t_bprojv[:, :], bprojv_in))
    _ptoks.append(p.dma("sync", t_xprev[:, :], xprev))
    _ptoks.append(p.dma("sync", t_pvalid[:, :], prevvalid))
    _ptoks.append(p.dma("sync", t_mpre[:, :], maskpre))
    _ptoks.append(p.dma("sync", t_mbig[:, :], maskbig))
    _ptoks.append(p.dma("sync", t_mrow[:, :], maskrow))
    _tp = vi(lambda e: e.memset(t_r3[:, :], 0.0), waits=_ptoks)
    tok_params = ("s_V", _tp)

    vi(lambda e: e.memset(t_ones[:, :], 1.0))
    vi(lambda e: e.memset(t_eps[:, :], EPS))
    vi(lambda e: e.memset(t_zero[:, :], 0.0))
    tok = si(lambda e: e.activation(out=t_w[:, :], in_=PV["tdecay"][:, :], func=AF.Exp,
                                    bias=0.0), waits=[tok_params])
    tok_w = vi(lambda e: e.tensor_scalar(out=t_w[:, :], in0=t_w[:, :], scalar1=-1.0,
                                         scalar2=None, op0=ALU.mult),
               waits=[("s_S", tok)])
    tok_ew = si(lambda e: e.activation(out=t_ew[:, :], in_=t_w[:, :], func=AF.Exp,
                                       bias=0.0), waits=[("s_V", tok_w)])
    tok_eu = si(lambda e: e.activation(out=t_eu[:, :], in_=PV["tfirst"][:, :],
                                       func=AF.Exp, bias=0.0))
    tok_tww = vd(lambda e: e.tensor_scalar(out=t_tww[:, :], in0=t_w[:, :],
                                           scalar1=float(TW), scalar2=None,
                                           op0=ALU.mult), waits=[("s_V", tok_w)])
    tok_spanw = vd(lambda e: e.tensor_scalar(out=t_spanw[:, :], in0=t_w[:, :],
                                             scalar1=float(NP * TW), scalar2=None,
                                             op0=ALU.mult))
    vi(lambda e: e.memset(t_run[0][:, 0:2 * JC], 0.0))
    tok_run = vi(lambda e: e.memset(t_run[0][:, 2 * JC:3 * JC], NEG_BIG))

    def ew_bcast(j, n):
        return bass.AP(t_ew, j, [[JC, 128], [0, n]])

    wcast_done = {}
    _wq = {"buf": 0, "obuf": 0}
    QIDX = {"wfc": 0, "wproj": 1}
    tok_qscale = [None]

    task_q = []
    task_qv = []

    def drain_tasks(n):
        for _ in range(min(n, len(task_q))):
            task_q.pop(0)()

    def drain_vtasks(n):
        for _ in range(min(n, len(task_qv))):
            task_qv.pop(0)()

    def drain_all_tasks():
        while task_qv:
            task_qv.pop(0)()
        while task_q:
            task_q.pop(0)()

    def cast_pieces(nm, quant=False):
        inner = W_INNER[nm]
        srcf = w_in[nm].rearrange("a b -> (a b)")
        dstf = sc_w[nm].rearrange("a b -> (a b)")
        total = 128 * inner
        piece = 128 * WST_N
        parts = (total + piece - 1) // piece
        out = []
        for h in range(parts):
            def body(h=h):
                nw = min(piece, total - h * piece)
                mw = nw // 128
                b = _wq["buf"]; _wq["buf"] = (b + 1) % NWST
                st = WST[b]
                tok_l = p.dma("sync", st[:, 0:mw], srcf[h * piece:h * piece + nw],
                              waits=wst_free[b])
                ob = _wq["obuf"]; _wq["obuf"] ^= 1
                OT = WSTO2[ob]
                if quant:
                    qi = QIDX[nm]
                    assert tok_qscale[0] is not None
                    tok_i = si(lambda e, st=st, mw=mw, qi=qi: e.activation(
                        out=WSTI[:, 0:mw], in_=st[:, 0:mw], func=AF.Identity,
                        bias=0.0, scale=t_qis[:, qi:qi + 1]),
                        waits=[tok_l, ("s_S", tok_qscale[0])] + wsti_free[0])
                    tok_o = si(lambda e, mw=mw, qi=qi, OT=OT: e.activation(
                        out=OT[:, 0:mw], in_=WSTI[:, 0:mw], func=AF.Identity,
                        bias=0.0, scale=t_qs[:, qi:qi + 1]), waits=wsto_free[ob])
                    wsti_free[0] = [("s_S", tok_o)]
                else:
                    tok_o = si(lambda e, st=st, mw=mw, OT=OT: e.activation(
                        out=OT[:, 0:mw], in_=st[:, 0:mw], func=AF.Copy,
                        bias=0.0), waits=[tok_l] + wsto_free[ob])
                wst_free[b] = [("s_S", tok_o)]
                tok_g = p.dma(
                    "scalar", dstf[h * piece:h * piece + nw],
                    OT[:, 0:mw], waits=[("s_S", tok_o)])
                wsto_free[ob] = [tok_g]
                wcast_done[nm] = tok_g
            out.append(body)
        return out

    def emit_cast_tasks(nm, quant=False):
        for f in cast_pieces(nm, quant):
            f()

    def emit_lora_tasks():
        for nm, dst, kcn in [("afc", afcb, JC), ("aproj", aprojb, JF)]:
            n = kcn * LORA
            b = _wq["buf"]; _wq["buf"] = (b + 1) % NWST
            st = WST[b]
            stv = st[:, 0:n].rearrange("p (a c) -> p a c", a=kcn, c=LORA)
            tok_l = p.dma("sync", st[:, 0:n], w_in[nm], waits=wst_free[b])
            tok_o = si(lambda e, stv=stv, dst=dst: e.activation(
                out=dst[:, :, :], in_=stv, func=AF.Copy, bias=0.0),
                waits=[tok_l])
            wst_free[b] = [("s_S", tok_o)]
        for nm, dst, m in [("bfc", sc_bfc, DF), ("bproj", sc_bproj, C)]:
            parts = (m + WST_N - 1) // WST_N
            for h in range(parts):
                mw = min(WST_N, m - h * WST_N)
                b = _wq["buf"]; _wq["buf"] = (b + 1) % NWST
                st = WST[b]
                tok_l = p.dma("sync", st[0:LORA, 0:mw],
                              w_in[nm][:, h * WST_N:h * WST_N + mw],
                              waits=wst_free[b])
                tok_o = si(lambda e, st=st, mw=mw: e.activation(
                    out=WSTO[0:LORA, 0:mw], in_=st[0:LORA, 0:mw], func=AF.Copy,
                    bias=0.0), waits=[tok_l] + wsto_free[0])
                wst_free[b] = [("s_S", tok_o)]
                tok_g = p.dma("scalar", dst[:, h * WST_N:h * WST_N + mw],
                              WSTO[0:LORA, 0:mw], waits=[("s_S", tok_o)])
                wsto_free[0] = [tok_g]
            wcast_done[nm] = tok_g

    def quant_scale_pieces():
        tok_red = {0: None, 1: None}
        qstate = {0: True, 1: True}
        out = []
        for qi, nm in enumerate(["wfc", "wproj"]):
            srcf = w_in[nm].rearrange("a b -> (a b)")
            total = 128 * W_INNER[nm]
            piece = 128 * WST_N
            parts = (total + piece - 1) // piece
            for h in range(parts):
                def body(h=h, qi=qi, srcf=srcf, total=total):
                    nw = min(piece, total - h * piece)
                    mw = nw // 128
                    b = _wq["buf"]; _wq["buf"] = (b + 1) % NWST
                    st = WST[b]
                    tok_l = p.dma("sync", st[:, 0:mw],
                                  srcf[h * piece:h * piece + nw],
                                  waits=wst_free[b])
                    if qstate[qi]:
                        tok_r = vi(lambda e, st=st, mw=mw, qi=qi: e.tensor_reduce(
                            out=t_qv[:, qi:qi + 1], in_=st[:, 0:mw],
                            axis=mybir.AxisListType.X, op=ALU.max,
                            apply_absolute_value=True), waits=[tok_l])
                        qstate[qi] = False
                    else:
                        tok_r = vi(lambda e, st=st, mw=mw: e.tensor_reduce(
                            out=t_r1[:, :], in_=st[:, 0:mw],
                            axis=mybir.AxisListType.X, op=ALU.max,
                            apply_absolute_value=True), waits=[tok_l])
                        tok_r = vi(lambda e, qi=qi: e.tensor_tensor(
                            out=t_qv[:, qi:qi + 1], in0=t_qv[:, qi:qi + 1],
                            in1=t_r1[:, :], op=ALU.max), waits=[("s_V", tok_r)])
                    wst_free[b] = [("s_V", tok_r)]
                    tok_red[qi] = tok_r
                out.append(body)
        out.append(lambda: quant_scale_final(tok_red))
        return out

    def quant_scale_final(tok_red):
        tok_d1 = p.dma("gpsimd", qdram.ap(), t_qv[:, 0:2],
                       waits=[("s_V", tok_red[0]), ("s_V", tok_red[1])])
        tok_d2 = p.dma("gpsimd", t_qrow[:, :], qdram.ap().rearrange("a b -> (a b)"),
                       waits=[tok_d1])
        tok_ms = []
        for qi in range(2):
            qv = bass.AP(t_qrow, qi, [[256, 1], [2, 128]])
            tok_ms.append(vd(lambda e, qv=qv, qi=qi: e.tensor_reduce(
                out=t_qrow[:, 254 + qi:255 + qi], in_=qv,
                axis=mybir.AxisListType.X, op=ALU.max),
                waits=[tok_d2]))
        tok_qq = vd(lambda e: e.tensor_scalar(out=t_qrow[:, 252:254],
                                              in0=t_qrow[:, 254:256],
                                              scalar1=1.0 / 127.0, scalar2=None,
                                              op0=ALU.mult),
                    waits=[("s_V", tok_ms[-1])])
        vd(lambda e: e.reciprocal(out=t_qrow[:, 250:252], in_=t_qrow[:, 252:254]),
           waits=[("s_V", tok_qq)])
        tok_of = vd(lambda e: e.memset(t_qrow[:, 248:249], 1.0))
        ib, bnk, wts = get_bank([("s_V", tok_of)])
        onesrow = bass.AP(t_qrow, 248, [[256, 1], [0, 128]])
        tok_t = ti(lambda e, bnk=bnk: e.matmul(bnk[:, 0:4], onesrow,
                                               t_qrow[:, 250:254],
                                               start=True, stop=True), waits=wts)
        sd(lambda e, bnk=bnk: e.activation(out=t_qis[:, 0:2], in_=bnk[:, 0:2],
                                           func=AF.Identity, bias=0.0, scale=1.0),
           waits=[("s_T", tok_t)])
        tok_qs = sd(lambda e, bnk=bnk: e.activation(out=t_qs[:, 0:2], in_=bnk[:, 2:4],
                                                    func=AF.Identity, bias=0.0,
                                                    scale=1.0))
        bank_done(ib, [("s_S", tok_qs)])
        tok_qscale[0] = tok_qs

    def stream_chunks(src_name, k0, k1, waits=(), m=None):
        src = sc_w[src_name]
        if m is None:
            m = C
        per_slot = WR_SLOT // m
        views = []
        vtoks = []
        slots = []
        kc = k0
        while kc < k1:
            kh = min(k1, kc + per_slot)
            slot = ring_rr[0] % NSLOT
            ring_rr[0] += 1
            wts = list(wslot_tok[slot]) + list(waits) + [wcast_done[src_name]]
            wslot_tok[slot] = []
            v = wslot[slot][:, 0:(kh - kc) * m].rearrange("p (a c) -> p a c",
                                                          a=kh - kc, c=m)
            tok = p.dma("sync", wslot[slot][:, 0:(kh - kc) * m],
                        src[:, kc * m:kh * m], waits=wts)
            for i in range(kh - kc):
                views.append(v[:, i, :])
                vtoks.append(tok)
            slots.append(slot)
            kc = kh
        return views, vtoks, slots

    def stream_cols(src_name, g8, waits=()):
        return stream_chunks(src_name, g8 * JC, g8 * JC + JC, waits=waits, m=1024)

    def slots_release(slots, toks):
        for s in slots:
            wslot_tok[s] = list(wslot_tok[s]) + list(toks)

    m_st = AR.view(rS1, [TW])
    e2_st = AR.view(rS2, [TW])
    msq_st = AR.view(rS3, [TW])

    def ln_stats(x_ap, xb_ap, xsq_ap, waits, stage_free=(), m_free=()):
        si(lambda e: e.activation(out=xb_ap, in_=x_ap, func=AF.Copy, bias=0.0),
           waits=list(waits) + list(stage_free))
        tok_s = si(lambda e: e.activation(out=xsq_ap, in_=x_ap, func=AF.Square,
                                          bias=0.0))
        ib1, b1, w1 = get_bank([("s_S", tok_s)])
        for j in range(JC):
            tok_t1 = ti(lambda e, j=j, b1=b1: e.matmul(
                b1[:, 0:TW], t_ones[:, :], xb_ap[:, j, :],
                start=(j == 0), stop=(j == JC - 1)), waits=w1 if j == 0 else ())
        ib2, b2, w2 = get_bank()
        for j in range(JC):
            tok_t2 = ti(lambda e, j=j, b2=b2: e.matmul(
                b2[:, 0:TW], t_ones[:, :], xsq_ap[:, j, :],
                start=(j == 0), stop=(j == JC - 1)), waits=w2 if j == 0 else ())
        tok_m = si(lambda e, b1=b1: e.activation(out=m_st[:, :], in_=b1[:, 0:TW],
                                                 func=AF.Identity, bias=0.0,
                                                 scale=1.0 / C),
                   waits=[("s_T", tok_t1)] + list(m_free))
        tok_e2 = si(lambda e, b2=b2: e.activation(out=e2_st[:, :], in_=b2[:, 0:TW],
                                                  func=AF.Identity, bias=0.0,
                                                  scale=1.0 / C),
                    waits=[("s_T", tok_t2)])
        bank_done(ib1, [("s_S", tok_m)])
        bank_done(ib2, [("s_S", tok_e2)])
        tok_msq = si(lambda e: e.activation(out=msq_st[:, :], in_=m_st[:, :],
                                            func=AF.Square, bias=0.0))
        tok_var = vi(lambda e: e.tensor_tensor(out=e2_st[:, :], in0=e2_st[:, :],
                                               in1=msq_st[:, :], op=ALU.subtract),
                     waits=[("s_S", tok_msq)])
        si(lambda e: e.activation(out=e2_st[:, :], in_=e2_st[:, :], func=AF.Ln,
                                  bias=t_eps[:, :], scale=1.0),
           waits=[("s_V", tok_var)])
        tok_rstd = si(lambda e: e.activation(out=e2_st[:, :], in_=e2_st[:, :],
                                             func=AF.Exp, bias=0.0, scale=-0.5))
        return tok_rstd, ("s_T", tok_t2)

    def ln_apply(x_ap, out_ap, tok_rstd, g=None, b=None, waits=(), scratch=None):
        sc = scratch if scratch is not None else out_ap
        tok = None
        for j in range(JC):
            tok = vi(lambda e, j=j: e.tensor_tensor(
                out=sc[:, j, :], in0=x_ap[:, j, :], in1=m_st[:, :], op=ALU.subtract),
                waits=(list(waits) + [("s_S", tok_rstd)]) if j == 0 else ())
        for j in range(JC):
            if g is None:
                tok = vi(lambda e, j=j: e.tensor_tensor(
                    out=out_ap[:, j, :], in0=sc[:, j, :], in1=e2_st[:, :],
                    op=ALU.mult))
            else:
                vi(lambda e, j=j: e.tensor_tensor(
                    out=sc[:, j, :], in0=sc[:, j, :], in1=e2_st[:, :], op=ALU.mult))
                tok = vi(lambda e, j=j: e.tensor_scalar(
                    out=out_ap[:, j, :], in0=sc[:, j, :],
                    scalar1=g[:, j:j + 1], scalar2=b[:, j:j + 1],
                    op0=ALU.mult, op1=ALU.add))
        return tok

    def row_ln(x_ap, out_ap, g=None, b=None, waits=()):
        sd(lambda e: e.activation(out=t_rowbf[:, :], in_=x_ap, func=AF.Copy, bias=0.0),
           waits=list(waits) + [("s_T", p.counts["s_T"]), ("s_V", p.counts["s_V"])])
        tok_s = sd(lambda e: e.activation(out=t_rowbf2[:, :], in_=x_ap,
                                          func=AF.Square, bias=0.0))
        ib, bnk, wts = get_bank([("s_S", tok_s)])
        ti(lambda e, bnk=bnk: e.matmul(bnk[:, 0:JC], t_ones[:, :], t_rowbf[:, :],
                                       start=True, stop=True), waits=wts)
        tok_t = ti(lambda e, bnk=bnk: e.matmul(bnk[:, JC:2 * JC], t_ones[:, :],
                                               t_rowbf2[:, :], start=True, stop=True))
        vd(lambda e, bnk=bnk: e.tensor_reduce(
            out=t_r1[:, :], in_=bnk[:, 0:JC], axis=mybir.AxisListType.X, op=ALU.add),
            waits=[("s_T", tok_t)])
        tok_v = vd(lambda e, bnk=bnk: e.tensor_reduce(
            out=t_r2[:, :], in_=bnk[:, JC:2 * JC], axis=mybir.AxisListType.X,
            op=ALU.add))
        bank_done(ib, [("s_V", tok_v)])
        vd(lambda e: e.tensor_scalar(out=t_r1[:, :], in0=t_r1[:, :], scalar1=1.0 / C,
                                     scalar2=None, op0=ALU.mult),
           waits=[("s_V", tok_v)])
        tok_v = vd(lambda e: e.tensor_scalar(out=t_r2[:, :], in0=t_r2[:, :],
                                             scalar1=1.0 / C, scalar2=None,
                                             op0=ALU.mult))
        tok_s = sd(lambda e: e.activation(out=t_r3[:, :], in_=t_r1[:, :],
                                          func=AF.Square, bias=0.0),
                   waits=[("s_V", tok_v)])
        tok_v = vd(lambda e: e.tensor_tensor(out=t_r2[:, :], in0=t_r2[:, :],
                                             in1=t_r3[:, :], op=ALU.subtract),
                   waits=[("s_S", tok_s)])
        sd(lambda e: e.activation(out=t_r2[:, :], in_=t_r2[:, :], func=AF.Ln,
                                  bias=t_eps[:, :], scale=1.0), waits=[("s_V", tok_v)])
        tok_s = sd(lambda e: e.activation(out=t_r2[:, :], in_=t_r2[:, :], func=AF.Exp,
                                          bias=0.0, scale=-0.5))
        tok_v = vd(lambda e: e.tensor_scalar(out=out_ap, in0=x_ap, scalar1=t_r1[:, :],
                                             scalar2=t_r2[:, :], op0=ALU.subtract,
                                             op1=ALU.mult), waits=[("s_S", tok_s)])
        if g is not None:
            vd(lambda e: e.tensor_tensor(out=out_ap, in0=out_ap, in1=g, op=ALU.mult))
            tok_v = vd(lambda e: e.tensor_tensor(out=out_ap, in0=out_ap, in1=b,
                                                 op=ALU.add))
        return tok_v

    x_t = [AR.view(rA0, [JC, TW]), AR.view(rA1, [JC, TW])]
    beta_t = [AR.view(rA0, [JC, TW]), AR.view(rA1, [JC, TW])]
    xin_t = AR.view(rB, [JC, TW])
    h_t = AR.view(rC, [JC, TW])
    ek1_t = AR.view(rC, [JC, TW])
    d_t = AR.view(rD, [JC, TW])
    prod_t = AR.view(rD, [JC, TW])
    r1_t = AR.view(rE, [JC, TW])
    xb_t = ABF.view(bR1, [JC, TW])
    xsq_t = ABF.view(bR1 + TWJC, [JC, TW])
    xk_t = ABF.view(bR2, [JC, TW])
    xv_t = ABF.view(bR2 + TWJC, [JC, TW])
    xr_t = ABF.view(bR2 + 2 * TWJC, [JC, TW])

    prev = {"xfree": [[], []], "xinfree": [], "ekfree": [], "betafree": [[], []],
            "rfree": [], "stagefree": [], "mixfree": [], "mfree": [],
            "applyfree": []}

    def phase1_pass(pp):
        par = pp % 2
        tx = x_t[par]
        tbeta = beta_t[par]
        if pp in xpre:
            tok_x = xpre.pop(pp)
        else:
            tok_x = p.dma("sync", tx, xin[pp], waits=prev["xfree"][par])
        tok_rstd, tok_stg = ln_stats(tx, xb_t, xsq_t, waits=[tok_x],
                                     stage_free=prev["stagefree"],
                                     m_free=prev["mfree"])
        tok_xin = ln_apply(tx, xin_t, tok_rstd, waits=prev["xinfree"])
        if pp == NP - 1 and not debug:
            tok_sp_xin = None
        else:
            tok_sp_xin = p.dma("gpsimd", sp_xin[pp], xin_t,
                               waits=[("s_V", tok_xin)])
        tok_rstd, tok_stg = ln_stats(xin_t, xb_t, xsq_t, waits=[("s_V", tok_xin)])
        tok_h = ln_apply(xin_t, h_t, tok_rstd, g=PV["ln1g"], b=PV["ln1b"],
                         waits=prev["mixfree"])
        prev["xinfree"] = [tok_sp_xin] if tok_sp_xin is not None else []
        prev["stagefree"] = [tok_stg]
        prev["mfree"] = [("s_V", tok_h)]
        for j in range(JC):
            vi(lambda e, j=j: e.tensor_tensor(
                out=d_t[:, j, 1:TW], in0=h_t[:, j, 1:TW], in1=h_t[:, j, 0:TW - 1],
                op=ALU.subtract))
            vd(lambda e, j=j: e.tensor_tensor(
                out=d_t[:, j, 0:1], in0=h_t[:, j, 0:1], in1=t_hcol[:, j:j + 1],
                op=ALU.subtract))
        tok_mix = None
        mix_toks = []
        for tm, xo in ((PV["tmk"], xk_t), (PV["tmv"], xv_t), (PV["tmr"], xr_t)):
            for j in range(JC):
                vi(lambda e, j=j, tm=tm, xo=xo: e.scalar_tensor_tensor(
                    out=xo[:, j, 1:TW], in0=d_t[:, j, 1:TW], scalar=tm[:, j:j + 1],
                    in1=h_t[:, j, 0:TW - 1], op0=ALU.mult, op1=ALU.add))
                tok_mix = vd(lambda e, j=j, tm=tm, xo=xo: e.scalar_tensor_tensor(
                    out=xo[:, j, 0:1], in0=d_t[:, j, 0:1], scalar=tm[:, j:j + 1],
                    in1=t_hcol[:, j:j + 1], op0=ALU.mult, op1=ALU.add))
            mix_toks.append(tok_mix)
        tok_hcol = vi(lambda e: e.tensor_copy(t_hcol[:, :], h_t[:, :, TW - 1:TW]),
                      waits=[("s_V", tok_h)])
        if debug:
            p.dma("gpsimd", sp_h[pp], h_t, waits=[("s_V", tok_hcol)])
            p.dma("gpsimd", sp_xk[pp], xk_t, waits=[("s_V", tok_mix)])
        ek_toks = [None] * JC
        views, vtoks, slots = stream_chunks("wk", 0, JC)
        lastmm = None
        for mc in range(JC):
            ib, bnk, wts = get_bank()
            for kc in range(JC):
                lastmm = ti(lambda e, bnk=bnk, v=views[kc], mc=mc, kc=kc: e.matmul(
                    bnk[:, 0:TW], v[:, mc * 128:(mc + 1) * 128], xk_t[:, kc, :],
                    start=(kc == 0), stop=(kc == JC - 1)),
                    waits=(wts + [vtoks[kc], ("s_V", mix_toks[0])]) if kc == 0
                    else ([vtoks[kc]] if vtoks[kc] is not vtoks[kc - 1] else ()))
            tok_red = vi(lambda e, bnk=bnk, mc=mc, pp=pp: e.tensor_reduce(
                out=t_mk[pp][:, mc:mc + 1], in_=bnk[:, 0:TW],
                axis=mybir.AxisListType.X, op=ALU.max), waits=[("s_T", lastmm)])
            vi(lambda e, mc=mc, pp=pp: e.tensor_scalar(
                out=t_negmk[pp][:, mc:mc + 1], in0=t_mk[pp][:, mc:mc + 1],
                scalar1=-1.0, scalar2=None, op0=ALU.mult),
                waits=[("s_V", tok_red)])
            if debug:
                si(lambda e, bnk=bnk, mc=mc: e.activation(
                    out=d_t[:, mc, :], in_=bnk[:, 0:TW], func=AF.Identity,
                    bias=0.0, scale=1.0), waits=[("s_V", p.counts["s_V"])])
            tok_ek = si(lambda e, bnk=bnk, mc=mc, pp=pp: e.activation(
                out=ek1_t[:, mc, :], in_=bnk[:, 0:TW], func=AF.Exp,
                bias=t_negmk[pp][:, mc:mc + 1], scale=1.0),
                waits=[("s_V", p.counts["s_V"])] + prev["ekfree"] if mc == 0
                else [("s_V", p.counts["s_V"])])
            ek_toks[mc] = tok_ek
            bank_done(ib, [("s_S", tok_ek)])
            drain_vtasks(1)
        slots_release(slots, [("s_T", lastmm)])
        dbg_ktok = []
        if debug:
            tk1 = p.dma("gpsimd", sp_k[pp], d_t, waits=[("s_S", ek_toks[-1])])
            p.dma("gpsimd", sp_mk[pp], t_mk[pp][:, :], waits=[("s_S", ek_toks[-1])])
            p.dma("gpsimd", sp_nmk[pp], t_negmk[pp][:, :], waits=[("s_S", ek_toks[-1])])
            dbg_ktok = [tk1]
        views, vtoks, slots = stream_chunks("wv", 0, JC)
        beta_toks = [None] * JC
        for mc in range(JC):
            ib, bnk, wts = get_bank()
            for kc in range(JC):
                lastmm = ti(lambda e, bnk=bnk, v=views[kc], mc=mc, kc=kc: e.matmul(
                    bnk[:, 0:TW], v[:, mc * 128:(mc + 1) * 128], xv_t[:, kc, :],
                    start=(kc == 0), stop=(kc == JC - 1)),
                    waits=(wts + [vtoks[kc], ("s_V", mix_toks[1])])
                    if (kc == 0 and mc == 0) else (wts + [vtoks[kc]]) if kc == 0
                    else ([vtoks[kc]] if vtoks[kc] is not vtoks[kc - 1] else ()))
            tok_b = vi(lambda e, bnk=bnk, mc=mc, tbeta=tbeta: e.tensor_tensor(
                out=tbeta[:, mc, :], in0=ek1_t[:, mc, :], in1=bnk[:, 0:TW],
                op=ALU.mult),
                waits=[("s_T", lastmm), ("s_S", ek_toks[mc])]
                + (prev["betafree"][par] if mc == 0 else []))
            beta_toks[mc] = tok_b
            bank_done(ib, [("s_V", tok_b)])
            drain_vtasks(1)
        slots_release(slots, [("s_T", lastmm)])
        views, vtoks, slots = stream_chunks("wr", 0, JC)
        r_toks = [None] * JC
        for mc in range(JC):
            ib, bnk, wts = get_bank()
            for kc in range(JC):
                lastmm = ti(lambda e, bnk=bnk, v=views[kc], mc=mc, kc=kc: e.matmul(
                    bnk[:, 0:TW], v[:, mc * 128:(mc + 1) * 128], xr_t[:, kc, :],
                    start=(kc == 0), stop=(kc == JC - 1)),
                    waits=(wts + [vtoks[kc], ("s_V", mix_toks[2])])
                    if (kc == 0 and mc == 0) else (wts + [vtoks[kc]]) if kc == 0
                    else ([vtoks[kc]] if vtoks[kc] is not vtoks[kc - 1] else ()))
            tok_r = si(lambda e, bnk=bnk, mc=mc: e.activation(
                out=r1_t[:, mc, :], in_=bnk[:, 0:TW], func=AF.Sigmoid, bias=0.0,
                scale=1.0),
                waits=[("s_T", lastmm)] + (prev["rfree"] if mc == 0 else []))
            r_toks[mc] = tok_r
            bank_done(ib, [("s_S", tok_r)])
        slots_release(slots, [("s_T", lastmm)])
        tok_sca = None
        for j in range(JC):
            tok_sca = vi(lambda e, j=j, tbeta=tbeta: e.tensor_tensor_scan(
                prod_t[:, j, :], ew_bcast(j, TW), tbeta[:, j, :], 0.0,
                ALU.mult, ALU.add),
                (([("s_V", beta_toks[-1]), ("s_S", tok_ew)] + dbg_ktok)
                 if j == 0 else ()))
        tok_za = vi(lambda e: e.tensor_copy(t_za[:, :], prod_t[:, :, TW - 1:TW]),
                    waits=[("s_V", tok_sca)])
        for j in range(JC):
            tok_sca = vi(lambda e, j=j: e.tensor_tensor_scan(
                prod_t[:, j, :], ew_bcast(j, TW), ek1_t[:, j, :], 0.0,
                ALU.mult, ALU.add),
                waits=[("s_S", ek_toks[-1]), ("s_V", tok_za)] if j == 0 else ())
        tok_zb = vi(lambda e: e.tensor_copy(t_zb[:, :], prod_t[:, :, TW - 1:TW]),
                    waits=[("s_V", tok_sca)])
        ro, rn = t_run[pp], t_run[pp + 1]
        a_o, b_o, m_o = ro[:, 0:JC], ro[:, JC:2 * JC], ro[:, 2 * JC:3 * JC]
        a_n, b_n, m_n = rn[:, 0:JC], rn[:, JC:2 * JC], rn[:, 2 * JC:3 * JC]
        vd(lambda e: e.tensor_tensor(out=t_cj1[:, :], in0=m_o, in1=t_tww[:, :],
                                     op=ALU.add), waits=[("s_V", tok_zb)])
        vd(lambda e: e.tensor_tensor(out=m_n, in0=t_cj1[:, :], in1=t_mk[pp][:, :],
                                     op=ALU.max))
        tok_v = vd(lambda e: e.tensor_tensor(out=t_cj1[:, :], in0=t_cj1[:, :],
                                             in1=m_n, op=ALU.subtract))
        sd(lambda e: e.activation(out=t_cj1[:, :], in_=t_cj1[:, :], func=AF.Exp,
                                  bias=0.0), waits=[("s_V", tok_v)])
        tok_v = vd(lambda e: e.tensor_tensor(out=t_cj2[:, :], in0=t_mk[pp][:, :],
                                             in1=m_n, op=ALU.subtract))
        tok_s = sd(lambda e: e.activation(out=t_cj2[:, :], in_=t_cj2[:, :],
                                          func=AF.Exp, bias=0.0),
                   waits=[("s_V", tok_v)])
        vd(lambda e: e.tensor_tensor(out=a_n, in0=a_o, in1=t_cj1[:, :], op=ALU.mult),
           waits=[("s_S", tok_s)])
        vd(lambda e: e.tensor_tensor(out=t_cj3[:, :], in0=t_za[:, :],
                                     in1=t_cj2[:, :], op=ALU.mult))
        vd(lambda e: e.tensor_tensor(out=a_n, in0=a_n, in1=t_cj3[:, :], op=ALU.add))
        vd(lambda e: e.tensor_tensor(out=b_n, in0=b_o, in1=t_cj1[:, :], op=ALU.mult))
        vd(lambda e: e.tensor_tensor(out=t_cj3[:, :], in0=t_zb[:, :],
                                     in1=t_cj2[:, :], op=ALU.mult))
        tok_run = vd(lambda e: e.tensor_tensor(out=b_n, in0=b_n, in1=t_cj3[:, :],
                                               op=ALU.add))
        if pp == NP - 1 and not debug:
            prev["ekfree"] = []
            prev["betafree"][par] = []
            prev["xfree"][par] = []
            prev["rfree"] = []
        else:
            tok_sp1 = p.dma("gpsimd", sp_ek[pp], ek1_t,
                            waits=[("s_S", ek_toks[-1]), ("s_V", tok_zb)])
            tok_sp2 = p.dma("gpsimd", sp_beta[pp], tbeta,
                            waits=[("s_V", tok_za)])
            tok_sp3 = p.dma("gpsimd", sp_r[pp], r1_t,
                            waits=[("s_S", r_toks[-1])])
            prev["ekfree"] = [tok_sp1]
            prev["betafree"][par] = [tok_sp2]
            prev["xfree"][par] = [tok_sp2]
            prev["rfree"] = [tok_sp3]
        prev["mixfree"] = ([("s_V", tok_mix), ("s_T", lastmm)]
                           + list(prev["ekfree"]))
        return tok_run

    xpre = {0: p.dma("sync", x_t[0], xin[0], waits=[])}
    emit_lora_tasks()
    emit_cast_tasks("wk")
    emit_cast_tasks("wv")
    emit_cast_tasks("wr")
    row_ln(t_xprev[:, :], t_rowa[:, :], waits=[tok_params])
    row_ln(t_rowa[:, :], t_rowa[:, :], g=PV["ln1g"][:, :], b=PV["ln1b"][:, :])
    vi(lambda e: e.tensor_scalar(out=t_hcol[:, :], in0=t_rowa[:, :],
                                 scalar1=t_pvalid[:, :], scalar2=None, op0=ALU.mult))

    after_pass = {0: ["wo"]}
    task_qv.extend(quant_scale_pieces())
    for pp in range(NP):
        tok_run_last = phase1_pass(pp)
        for nm in after_pass.get(pp, []):
            emit_cast_tasks(nm)
        drain_vtasks(16)
    task_q.extend(cast_pieces("fr"))
    task_q.extend(cast_pieces("fk"))
    task_q.extend(cast_pieces("fv"))
    task_q.extend(cast_pieces("wfc", quant=True))
    task_q.extend(cast_pieces("wproj", quant=True))

    tok_pack = p.dma("gpsimd", cc_sum_in.ap(), t_run[NP][:, :],
                     waits=[("s_V", tok_run_last)])
    tok_cc = p.emit("gpsimd", lambda e: e.collective_compute(
        "AllGather", ALU.bypass, replica_groups=[list(range(NCORES))],
        ins=[cc_sum_in.ap().opt()], outs=[cc_sum_out.ap().opt()]),
        inc="s_cc", amt=1, waits=[tok_pack])
    while task_qv:
        task_qv.pop(0)()
    drain_tasks(16)
    gtoks = []
    for r in range(NCORES):
        gtoks.append(p.dma("gpsimd", t_gath[:, r, :], cc_sum_out.ap()[r],
                           waits=[("s_cc", tok_cc)] if r == 0 else ()))
    vd(lambda e: e.memset(t_acc[:, 0:2 * JC], 0.0))
    tok_acc = vd(lambda e: e.memset(t_acc[:, 2 * JC:3 * JC], NEG_BIG))
    aa, ab, am = t_acc[:, 0:JC], t_acc[:, JC:2 * JC], t_acc[:, 2 * JC:3 * JC]
    for r in range(NCORES):
        g_a = t_gath[:, r, 0:JC]
        g_b = t_gath[:, r, JC:2 * JC]
        g_m = t_gath[:, r, 2 * JC:3 * JC]
        vd(lambda e, g_m=g_m, r=r: e.tensor_scalar(
            out=t_cj1[:, :], in0=g_m, scalar1=t_mpre[:, r:r + 1],
            scalar2=t_mbig[:, r:r + 1], op0=ALU.mult, op1=ALU.add),
            waits=[gtoks[r]] + ([("s_V", tok_acc)] if r == 0 else []))
        vd(lambda e, r=r: e.tensor_scalar(out=t_cj2[:, :], in0=t_spanw[:, :],
                                          scalar1=t_mpre[:, r:r + 1], scalar2=None,
                                          op0=ALU.mult))
        vd(lambda e: e.tensor_tensor(out=t_cj2[:, :], in0=t_cj2[:, :], in1=am,
                                     op=ALU.add))
        vd(lambda e: e.tensor_tensor(out=am, in0=t_cj2[:, :], in1=t_cj1[:, :],
                                     op=ALU.max))
        tok_v = vd(lambda e: e.tensor_tensor(out=t_cj2[:, :], in0=t_cj2[:, :], in1=am,
                                             op=ALU.subtract))
        sd(lambda e: e.activation(out=t_cj2[:, :], in_=t_cj2[:, :], func=AF.Exp,
                                  bias=0.0), waits=[("s_V", tok_v)])
        tok_v = vd(lambda e: e.tensor_tensor(out=t_cj1[:, :], in0=t_cj1[:, :], in1=am,
                                             op=ALU.subtract))
        tok_s = sd(lambda e: e.activation(out=t_cj1[:, :], in_=t_cj1[:, :],
                                          func=AF.Exp, bias=0.0),
                   waits=[("s_V", tok_v)])
        vd(lambda e: e.tensor_tensor(out=aa, in0=aa, in1=t_cj2[:, :], op=ALU.mult),
           waits=[("s_S", tok_s)])
        vd(lambda e, g_a=g_a: e.tensor_tensor(out=t_cj3[:, :], in0=g_a,
                                              in1=t_cj1[:, :], op=ALU.mult))
        vd(lambda e: e.tensor_tensor(out=aa, in0=aa, in1=t_cj3[:, :], op=ALU.add))
        vd(lambda e: e.tensor_tensor(out=ab, in0=ab, in1=t_cj2[:, :], op=ALU.mult))
        vd(lambda e, g_b=g_b: e.tensor_tensor(out=t_cj3[:, :], in0=g_b,
                                              in1=t_cj1[:, :], op=ALU.mult))
        tok_acc = vd(lambda e: e.tensor_tensor(out=ab, in0=ab, in1=t_cj3[:, :],
                                               op=ALU.add))
    if debug:
        p.dma("gpsimd", sp_acc, t_acc[:, :], waits=[("s_V", tok_acc)])
        p.dma("gpsimd", dbg_sum, t_gath[:, :, :], waits=[("s_V", tok_acc)])
    tok_inc = [None] * NP
    for pp in range(NP):
        ro = t_run[pp]
        a_o2, b_o2, m_o2 = ro[:, 0:JC], ro[:, JC:2 * JC], ro[:, 2 * JC:3 * JC]
        vd(lambda e, pp=pp: e.tensor_scalar(out=t_cj1[:, :], in0=t_tww[:, :],
                                            scalar1=float(pp), scalar2=None,
                                            op0=ALU.mult), waits=[("s_V", tok_acc)])
        vd(lambda e: e.tensor_tensor(out=t_cj1[:, :], in0=t_cj1[:, :], in1=am,
                                     op=ALU.add))
        vd(lambda e, m_o2=m_o2: e.tensor_tensor(out=t_cj3[:, :], in0=t_cj1[:, :],
                                                in1=m_o2, op=ALU.max))
        tok_v = vd(lambda e: e.tensor_tensor(out=t_cj1[:, :], in0=t_cj1[:, :],
                                             in1=t_cj3[:, :], op=ALU.subtract))
        sd(lambda e: e.activation(out=t_cj1[:, :], in_=t_cj1[:, :], func=AF.Exp,
                                  bias=0.0), waits=[("s_V", tok_v)])
        tok_v = vd(lambda e, m_o2=m_o2: e.tensor_tensor(out=t_cj2[:, :], in0=m_o2,
                                                        in1=t_cj3[:, :],
                                                        op=ALU.subtract))
        tok_s = sd(lambda e: e.activation(out=t_cj2[:, :], in_=t_cj2[:, :],
                                          func=AF.Exp, bias=0.0),
                   waits=[("s_V", tok_v)])
        ia = t_inc[pp][:, 0:JC]
        ibb = t_inc[pp][:, JC:2 * JC]
        vd(lambda e, ia=ia: e.tensor_tensor(out=ia, in0=aa, in1=t_cj1[:, :],
                                            op=ALU.mult), waits=[("s_S", tok_s)])
        vd(lambda e, ia=ia, a_o2=a_o2: e.tensor_tensor(out=t_cj3[:, :], in0=a_o2,
                                                       in1=t_cj2[:, :], op=ALU.mult))
        vd(lambda e, ia=ia: e.tensor_tensor(out=ia, in0=ia, in1=t_cj3[:, :],
                                            op=ALU.add))
        vd(lambda e, ibb=ibb: e.tensor_tensor(out=ibb, in0=ab, in1=t_cj1[:, :],
                                              op=ALU.mult))
        vd(lambda e, ibb=ibb, b_o2=b_o2: e.tensor_tensor(out=t_cj3[:, :], in0=b_o2,
                                                         in1=t_cj2[:, :],
                                                         op=ALU.mult))
        vd(lambda e, ibb=ibb: e.tensor_tensor(out=ibb, in0=ibb, in1=t_cj3[:, :],
                                              op=ALU.add))
        vd(lambda e, pp=pp: e.tensor_scalar(out=t_cj1[:, :], in0=t_tww[:, :],
                                            scalar1=float(pp), scalar2=None,
                                            op0=ALU.mult))
        vd(lambda e: e.tensor_tensor(out=t_cj1[:, :], in0=t_cj1[:, :], in1=am,
                                     op=ALU.add))
        vd(lambda e, m_o2=m_o2: e.tensor_tensor(out=t_cj1[:, :], in0=t_cj1[:, :],
                                                in1=m_o2, op=ALU.max))
        tok_v = vd(lambda e, pp=pp: e.tensor_tensor(out=t_cj1[:, :], in0=t_cj1[:, :],
                                                    in1=t_mk[pp][:, :],
                                                    op=ALU.subtract))
        tok_s = sd(lambda e: e.activation(out=t_cj1[:, :], in_=t_cj1[:, :],
                                          func=AF.Exp, bias=0.0),
                   waits=[("s_V", tok_v)])
        vd(lambda e, ia=ia: e.tensor_tensor(out=ia, in0=ia, in1=t_cj1[:, :],
                                            op=ALU.mult), waits=[("s_S", tok_s)])
        tok_inc[pp] = vd(lambda e, ibb=ibb: e.tensor_tensor(
            out=ibb, in0=ibb, in1=t_cj1[:, :], op=ALU.mult))
        if debug:
            p.dma("gpsimd", sp_inc[pp], t_inc[pp][:, :],
                  waits=[("s_V", tok_inc[pp])])

    srA_t = AR.view(rE, [JC, TW])
    xinA_t = AR.view(rB, [JC, TW])
    ekA_t = AR.view(rC, [JC, TW])
    Aarr = AR.view(rD, [JC, TW])
    rwkv_t = ABF.view(bR1, [JC, TW])
    par_tok = {0: list(prev["betafree"][0]), 1: list(prev["betafree"][1])}
    stA = {"ek": prev["ekfree"], "r": prev["rfree"], "xin": prev["xinfree"],
           "arrD": [], "rwkv": []}
    last_pass = [None]

    def stageA_front(pp, resident):
        par = pp % 2
        betaA_t = AR.view(rA0 if par == 0 else rA1, [JC, TW])
        Barr = AR.view(rA1 if par == 0 else rA0, [JC, TW])
        if resident:
            tok_l1 = tok_l2 = tok_l3 = None
            rel = []
        else:
            tok_l2 = p.dma("gpsimd", betaA_t, sp_beta[pp], waits=par_tok[par])
            tok_l1 = p.dma("gpsimd", ekA_t, sp_ek[pp], waits=stA["ek"])
            tok_l3 = p.dma("gpsimd", srA_t, sp_r[pp], waits=stA["r"])
            rel = [tok_l3]
        tok_sa = None
        for j2 in range(JC):
            w0 = []
            if j2 == 0:
                w0 = [("s_V", tok_inc[pp])] + stA["arrD"]
                if tok_l2 is not None:
                    w0.append(tok_l2)
            tok_sa = vi(lambda e, j=j2, pp=pp, betaA_t=betaA_t: e.tensor_tensor_scan(
                Aarr[:, j, :], ew_bcast(j, TW), betaA_t[:, j, :],
                t_inc[pp][:, j:j + 1], ALU.mult, ALU.add), waits=w0)
        for j2 in range(JC):
            w0 = list(par_tok[1 - par]) if j2 == 0 else []
            if j2 == 0 and tok_l1 is not None:
                w0.append(tok_l1)
            tok_sa = vi(lambda e, j=j2, pp=pp, Barr=Barr: e.tensor_tensor_scan(
                Barr[:, j, :], ew_bcast(j, TW), ekA_t[:, j, :],
                t_inc[pp][:, JC + j:JC + j + 1], ALU.mult, ALU.add), waits=w0)
        for j2 in range(JC):
            vi(lambda e, j=j2, betaA_t=betaA_t: e.scalar_tensor_tensor(
                out=betaA_t[:, j, 1:TW], in0=betaA_t[:, j, 1:TW],
                scalar=t_eu[:, j:j + 1], in1=Aarr[:, j, 0:TW - 1],
                op0=ALU.mult, op1=ALU.add),
                waits=[("s_V", tok_sa), ("s_S", tok_eu)] if j2 == 0 else ())
            vi(lambda e, j=j2, pp=pp, betaA_t=betaA_t: e.scalar_tensor_tensor(
                out=betaA_t[:, j, 0:1], in0=betaA_t[:, j, 0:1],
                scalar=t_eu[:, j:j + 1], in1=t_inc[pp][:, j:j + 1],
                op0=ALU.mult, op1=ALU.add))
        tok_nd = None
        for j2 in range(JC):
            vi(lambda e, j=j2, Barr=Barr: e.scalar_tensor_tensor(
                out=ekA_t[:, j, 1:TW], in0=ekA_t[:, j, 1:TW],
                scalar=t_eu[:, j:j + 1], in1=Barr[:, j, 0:TW - 1],
                op0=ALU.mult, op1=ALU.add))
            tok_nd = vi(lambda e, j=j2, pp=pp: e.scalar_tensor_tensor(
                out=ekA_t[:, j, 0:1], in0=ekA_t[:, j, 0:1],
                scalar=t_eu[:, j:j + 1], in1=t_inc[pp][:, JC + j:JC + j + 1],
                op0=ALU.mult, op1=ALU.add))
        par_tok[1 - par] = [("s_V", tok_nd)]
        stA["arrD"] = [("s_V", tok_nd)]
        ek_flat = ekA_t.rearrange("p a b -> p (a b)")
        tok_rc = vi(lambda e: e.reciprocal(out=ek_flat, in_=ek_flat),
                    waits=[("s_V", tok_nd)])
        tok_y = vi(lambda e, betaA_t=betaA_t: e.tensor_tensor(
            out=betaA_t.rearrange("p a b -> p (a b)"),
            in0=betaA_t.rearrange("p a b -> p (a b)"), in1=ek_flat, op=ALU.mult))
        stA["ek"] = [("s_V", tok_y)]
        drain_tasks(24)
        return dict(pp=pp, par=par, betaA_t=betaA_t, sr_rel=rel)

    def stageA_back(ctx, front_next):
        pp, par, betaA_t = ctx["pp"], ctx["par"], ctx["betaA_t"]
        tok_rwkv = vi(lambda e, betaA_t=betaA_t: e.tensor_tensor(
            out=rwkv_t.rearrange("p a b -> p (a b)"),
            in0=betaA_t.rearrange("p a b -> p (a b)"),
            in1=srA_t.rearrange("p a b -> p (a b)"), op=ALU.mult),
            waits=list(ctx["sr_rel"]) + stA["rwkv"])
        par_tok[par] = [("s_V", tok_rwkv)]
        stA["r"] = [("s_V", tok_rwkv)]
        views, vtoks = wo_res["views"], wo_res["vtoks"]
        mminfo = []
        lastmm = None
        for mc in range(JC):
            ib, bnk, wts = get_bank()
            for kc in range(JC):
                lastmm = ti(lambda e, bnk=bnk, v=views[kc], mc=mc, kc=kc: e.matmul(
                    bnk[:, 0:TW], v[:, mc * 128:(mc + 1) * 128], rwkv_t[:, kc, :],
                    start=(kc == 0), stop=(kc == JC - 1)),
                    waits=(wts + [vtoks[kc], ("s_V", tok_rwkv)]) if kc == 0
                    else ([vtoks[kc]] if vtoks[kc] is not vtoks[kc - 1] else ()))
            mminfo.append((ib, bnk, lastmm))
        stA["rwkv"] = [("s_T", lastmm)]
        wo_res["last"] = lastmm
        ctx_next = front_next() if front_next is not None else None
        tok_x1 = None
        for mc, (ib, bnk, mm) in enumerate(mminfo):
            tok_x1 = vi(lambda e, bnk=bnk, mc=mc: e.tensor_tensor(
                out=xinA_t[:, mc, :], in0=xinA_t[:, mc, :], in1=bnk[:, 0:TW],
                op=ALU.add),
                waits=[("s_T", mm)] + (stA["xinload"] if mc == 0 else []))
            bank_done(ib, [("s_V", tok_x1)])
            drain_tasks(5)
        tok_col = vi(lambda e, pp=pp: e.tensor_copy(t_x1c[pp][:, :],
                                                    xinA_t[:, :, TW - 1:TW]),
                     waits=[("s_V", tok_x1)])
        tok_spx1 = p.dma("scalar", sp_x1[pp], xinA_t, waits=[("s_V", tok_col)])
        stA["xin"] = [tok_spx1]
        return tok_spx1, tok_col, ctx_next

    orderA = [NP - 1] + list(range(NP - 1))
    tok_sp_x1 = {}
    stA["xinload"] = []
    wo_res = {"views": [], "vtoks": [], "last": None, "slots": []}
    _kc = 0
    _slot = 0
    while _kc < JC:
        _kh = min(JC, _kc + WR_SLOT // C)
        wts = list(wslot_tok[_slot]) + [wcast_done["wo"]]
        wslot_tok[_slot] = []
        _v = wslot[_slot][:, 0:(_kh - _kc) * C].rearrange(
            "p (a c) -> p a c", a=_kh - _kc, c=C)
        _tok = p.dma("sync", wslot[_slot][:, 0:(_kh - _kc) * C],
                     sc_w["wo"][:, _kc * C:_kh * C], waits=wts)
        for _i in range(_kh - _kc):
            wo_res["views"].append(_v[:, _i, :])
            wo_res["vtoks"].append(_tok)
        wo_res["slots"].append(_slot)
        _slot += 1
        _kc = _kh
    first_res = not debug
    ctxA = stageA_front(orderA[0], resident=first_res)
    if not first_res:
        stA["xinload"] = [p.dma("scalar", xinA_t, sp_xin[orderA[0]],
                                waits=stA["xin"])]
    for idx, pp in enumerate(orderA):
        nxt = orderA[idx + 1] if idx + 1 < NP else None
        fe = (lambda nxt=nxt: stageA_front(nxt, resident=False)) \
            if nxt is not None else None
        tok_spx1, tok_col, ctxA_next = stageA_back(ctxA, fe)
        tok_sp_x1[pp] = tok_spx1
        if nxt is not None:
            stA["xinload"] = [p.dma("scalar", xinA_t, sp_xin[nxt],
                                    waits=[tok_spx1])]
            ctxA = ctxA_next
        if nxt is None:
            slots_release(wo_res["slots"], [("s_T", wo_res["last"])])
            ring_rr[0] = 0
        if pp == NP - 1:
            tok_pub = p.dma("scalar", cc_row_in.ap(), t_x1c[NP - 1][:, :],
                            waits=[("s_V", tok_col)])
            tok_cc2 = p.emit("gpsimd", lambda e: e.collective_compute(
                "AllGather", ALU.bypass, replica_groups=[list(range(NCORES))],
                ins=[cc_row_in.ap().opt()], outs=[cc_row_out.ap().opt()]),
                inc="s_cc", amt=1, waits=[tok_pub])
            rgtoks = []
            for r in range(NCORES):
                rgtoks.append(p.dma("scalar", t_rgath[:, r, :], cc_row_out.ap()[r],
                              waits=[("s_cc", tok_cc2)] if r == 0 else ()))
        drain_tasks(6)
    drain_all_tasks()
    tok_sel = vd(lambda e: e.tensor_scalar(out=t_rowa[:, :], in0=t_rgath[:, 0, :],
                                           scalar1=t_mrow[:, 0:1], scalar2=None,
                                           op0=ALU.mult), waits=[rgtoks[0]])
    for r in range(1, NCORES):
        tok_sel = vd(lambda e, r=r: e.scalar_tensor_tensor(
            out=t_rowa[:, :], in0=t_rgath[:, r, :], scalar=t_mrow[:, r:r + 1],
            in1=t_rowa[:, :], op0=ALU.mult, op1=ALU.add), waits=[rgtoks[r]])
    row_ln(t_rowa[:, :], t_rowa[:, :], g=PV["ln2g"][:, :], b=PV["ln2b"][:, :],
           waits=[("s_V", tok_sel)])
    tok_gcol0 = vd(lambda e: e.tensor_scalar(out=t_gcol[:, :], in0=t_rowa[:, :],
                                             scalar1=t_pvalid[:, :], scalar2=None,
                                             op0=ALU.mult))

    x1B_par = [AR.view(rA0, [JC, TW]), AR.view(rA1, [JC, TW])]
    gB_t = AR.view(rB, [JC, TW])
    d2B_t = AR.view(rD, [JC, TW])
    sigB_t = AR.view(rC, [JC, TW])
    xB_t = AR.view(rE, [JC, TW])
    outB_t = AR.view(rE, [JC, TW])
    relu_st = AR.view(rS3, [TW])
    ffn_st = AR.view(rS3, [TW])
    gk_t = ABF.view(bR1, [JC, TW])
    gr_t = ABF.view(bR1 + TWJC, [JC, TW])
    kf_t = ABF.view(bR2, [JF, TW])
    h2_t = ABF.view(bR2, [JF, TW])
    stg3a = ABF.view(bR2, [JC, TW])
    stg3b = ABF.view(bR2 + TWJC, [JC, TW])
    xf_t = ABF.view(bR5, [JC, TW])
    xa_t = ABF.view(bR4, [TW])
    xa2_t = ABF.view(bR4 + TW, [TW])

    stB = {"x1p0": [], "x1p1": [], "g": [], "d2": [], "sig": [], "x": [],
           "out": [], "gk": [], "kf": [], "xf": [], "bfc": [], "bproj": [],
           "stage": [], "mfree": []}

    def stageB_pass(pp):
        x1B_t = x1B_par[pp % 2]
        gcol = t_gcol if pp == 0 else t_gc[pp - 1]
        gcol_tok = [("s_V", tok_gcol0)] if pp == 0 else stB[f"gcol{pp - 1}"]
        tok_lx1 = p.dma("gpsimd", x1B_t, sp_x1[pp],
                        waits=[tok_sp_x1[pp]] + stB[f"x1p{pp % 2}"])
        tok_rstd, _ = ln_stats(x1B_t, stg3a, stg3b, waits=[tok_lx1],
                               stage_free=stB["stage"], m_free=stB["mfree"])
        tok_g = ln_apply(x1B_t, gB_t, tok_rstd, g=PV["ln2g"], b=PV["ln2b"],
                         waits=stB["g"])
        tok_gc = vi(lambda e, pp=pp: e.tensor_copy(t_gc[pp][:, :],
                                                   gB_t[:, :, TW - 1:TW]),
                    waits=[("s_V", tok_g)])
        stB[f"gcol{pp}"] = [("s_V", tok_gc)]
        if debug:
            p.dma("gpsimd", sp_g[pp], gB_t, waits=[("s_V", tok_gc)])
        for j in range(JC):
            vi(lambda e, j=j: e.tensor_tensor(
                out=d2B_t[:, j, 1:TW], in0=gB_t[:, j, 1:TW], in1=gB_t[:, j, 0:TW - 1],
                op=ALU.subtract), waits=stB["d2"] if j == 0 else ())
            vd(lambda e, j=j, gcol=gcol: e.tensor_tensor(
                out=d2B_t[:, j, 0:1], in0=gB_t[:, j, 0:1], in1=gcol[:, j:j + 1],
                op=ALU.subtract), waits=gcol_tok if j == 0 else ())
        tok_mix = None
        for fm, xo in ((PV["fmk"], gk_t), (PV["fmr"], gr_t)):
            for j in range(JC):
                vi(lambda e, j=j, fm=fm, xo=xo: e.scalar_tensor_tensor(
                    out=xo[:, j, 1:TW], in0=d2B_t[:, j, 1:TW], scalar=fm[:, j:j + 1],
                    in1=gB_t[:, j, 0:TW - 1], op0=ALU.mult, op1=ALU.add),
                    waits=stB["gk"] if (j == 0 and xo is gk_t) else ())
                tok_mix = vd(lambda e, j=j, fm=fm, xo=xo, gcol=gcol:
                             e.scalar_tensor_tensor(
                    out=xo[:, j, 0:1], in0=d2B_t[:, j, 0:1], scalar=fm[:, j:j + 1],
                    in1=gcol[:, j:j + 1], op0=ALU.mult, op1=ALU.add))
        views, vtoks, slots = stream_chunks("fr", 0, JC)
        sig_toks = [None] * JC
        lastmm = None
        for mc in range(JC):
            ib, bnk, wts = get_bank()
            for kc in range(JC):
                lastmm = ti(lambda e, bnk=bnk, v=views[kc], mc=mc, kc=kc: e.matmul(
                    bnk[:, 0:TW], v[:, mc * 128:(mc + 1) * 128], gr_t[:, kc, :],
                    start=(kc == 0), stop=(kc == JC - 1)),
                    waits=(wts + [vtoks[kc], ("s_V", tok_mix)]) if kc == 0
                    else ([vtoks[kc]] if vtoks[kc] is not vtoks[kc - 1] else ()))
            tok_sg = si(lambda e, bnk=bnk, mc=mc: e.activation(
                out=sigB_t[:, mc, :], in_=bnk[:, 0:TW], func=AF.Sigmoid, bias=0.0),
                waits=[("s_T", lastmm)] + (stB["sig"] if mc == 0 else []))
            sig_toks[mc] = tok_sg
            bank_done(ib, [("s_S", tok_sg)])
        slots_release(slots, [("s_T", lastmm)])
        kf_toks = []
        NG = JF // 8
        for g8 in range(NG):
            views, vtoks, slots = stream_cols("fk", g8)
            for jfo in range(8):
                jf = g8 * 8 + jfo
                ib, bnk, wts = get_bank()
                for kc in range(JC):
                    lastmm = ti(lambda e, bnk=bnk, v=views[kc], jfo=jfo, kc=kc:
                                e.matmul(
                        bnk[:, 0:TW], v[:, jfo * 128:(jfo + 1) * 128],
                        gk_t[:, kc, :], start=(kc == 0), stop=(kc == JC - 1)),
                        waits=(wts + [vtoks[kc]]) if (kc == 0 and jfo == 0)
                        else ([vtoks[kc]] if jfo == 0 else ()))
                tok_rl = si(lambda e, bnk=bnk: e.activation(
                    out=relu_st[:, :], in_=bnk[:, 0:TW], func=AF.Relu, bias=0.0),
                    waits=[("s_T", lastmm)] + (stB["kf"] if jf == 0 else []))
                tok_kf = si(lambda e, jf=jf: e.activation(
                    out=kf_t[:, jf, :], in_=relu_st[:, :], func=AF.Square, bias=0.0))
                kf_toks.append(tok_kf)
                bank_done(ib, [("s_S", tok_kf)])
            slots_release(slots, [("s_T", lastmm)])
        tok_after_fk = ("s_T", p.counts["s_T"])
        banks, bwaits = get_all_banks()
        x2_toks = [None] * JC
        for g8 in range(NG):
            views, vtoks, slots = stream_chunks("fv", g8 * 8, g8 * 8 + 8)
            for kci in range(8):
                kc = g8 * 8 + kci
                for mc in range(JC):
                    w8 = []
                    if g8 == 0 and kci == 0 and mc == 0:
                        w8 = bwaits + [vtoks[kci], ("s_S", kf_toks[-1])]
                    elif mc == 0 and (kci == 0 or vtoks[kci] is not vtoks[kci - 1]):
                        w8 = [vtoks[kci]]
                    lastmm = ti(lambda e, v=views[kci], mc=mc, kc=kc: e.matmul(
                        pbank[mc][:, 0:TW], v[:, mc * 128:(mc + 1) * 128],
                        kf_t[:, kc, :], start=(kc == 0), stop=(kc == JF - 1)),
                        waits=w8)
            slots_release(slots, [("s_T", lastmm)])
        for mc in range(JC):
            vi(lambda e, mc=mc: e.tensor_tensor(
                out=sigB_t[:, mc, :], in0=sigB_t[:, mc, :], in1=pbank[mc][:, 0:TW],
                op=ALU.mult), waits=[("s_T", lastmm)] if mc == 0 else ())
            tok_x2 = vi(lambda e, mc=mc: e.tensor_tensor(
                out=x1B_t[:, mc, :], in0=x1B_t[:, mc, :], in1=sigB_t[:, mc, :],
                op=ALU.add))
            x2_toks[mc] = tok_x2
            bank_done(mc, [("s_V", tok_x2)])
        tok_lx = p.dma("gpsimd", xB_t, xin[pp], waits=stB["x"])
        tok_xa = vi(lambda e: e.tensor_tensor(
            out=x1B_t.rearrange("p a b -> p (a b)"),
            in0=x1B_t.rearrange("p a b -> p (a b)"),
            in1=xB_t.rearrange("p a b -> p (a b)"), op=ALU.add),
            waits=[tok_lx, ("s_V", x2_toks[-1])])
        if debug:
            p.dma("gpsimd", sp_xaf[pp], x1B_t, waits=[("s_V", tok_xa)])
            p.dma("gpsimd", sp_sig[pp], sigB_t, waits=[("s_V", x2_toks[-1])])
        tok_rstd, tok_stg3 = ln_stats(x1B_t, stg3a, stg3b,
                                      waits=[("s_V", tok_xa),
                                             ("s_S", kf_toks[-1])])
        tok_xf = ln_apply(x1B_t, xf_t, tok_rstd, waits=stB["xf"], scratch=d2B_t)
        ib, bnk, wts = get_bank()
        for kc in range(JC):
            lastmm = ti(lambda e, bnk=bnk, kc=kc: e.matmul(
                bnk[0:LORA, 0:TW], afcb[:, kc, :], xf_t[:, kc, :],
                start=(kc == 0), stop=(kc == JC - 1)),
                waits=(wts + [("s_V", tok_xf)]) if kc == 0 else ())
        tok_xa1 = si(lambda e, bnk=bnk: e.activation(
            out=xa_t[0:LORA, :], in_=bnk[0:LORA, 0:TW], func=AF.Identity,
            bias=0.0, scale=2.0), waits=[("s_T", lastmm)] + stB["bfc"])
        bank_done(ib, [("s_S", tok_xa1)])
        tok_bfc = p.dma("sync", bfcb[:, :], sc_bfc,
                        waits=[wcast_done["bfc"]] + stB["bfc"])
        h2_toks = []
        for g8 in range(NG):
            views, vtoks, slots = stream_cols("wfc", g8)
            for jfo in range(8):
                jf = g8 * 8 + jfo
                ib, bnk, wts = get_bank()
                for kc in range(JC):
                    ti(lambda e, bnk=bnk, v=views[kc], jfo=jfo, kc=kc: e.matmul(
                        bnk[:, 0:TW], v[:, jfo * 128:(jfo + 1) * 128],
                        xf_t[:, kc, :], start=(kc == 0), stop=False),
                        waits=(wts + [vtoks[kc]]) if (kc == 0 and jfo == 0)
                        else ([vtoks[kc]] if jfo == 0 else ()))
                lastmm = ti(lambda e, bnk=bnk, jf=jf: e.matmul(
                    bnk[:, 0:TW], bfcb[:, jf * 128:(jf + 1) * 128], xa_t[0:LORA, :],
                    start=False, stop=True),
                    waits=[tok_bfc, ("s_S", tok_xa1)]
                    if (g8 == 0 and jfo == 0) else ())
                tok_h2 = si(lambda e, bnk=bnk, jf=jf: e.activation(
                    out=h2_t[:, jf, :], in_=bnk[:, 0:TW], func=AF.Gelu,
                    bias=t_bfcv[:, jf:jf + 1], scale=1.0),
                    waits=[("s_T", lastmm)])
                h2_toks.append(tok_h2)
                bank_done(ib, [("s_S", tok_h2)])
            slots_release(slots, [("s_T", lastmm)])
        tok_after_wfc = ("s_T", p.counts["s_T"])
        if debug:
            p.dma("gpsimd", sp_xf[pp], xf_t, waits=[("s_V", tok_xf)])
            p.dma("gpsimd", sp_h2[pp], h2_t, waits=[("s_S", h2_toks[-1])])
        ib, bnk, wts = get_bank()
        for kc in range(JF):
            lastmm = ti(lambda e, bnk=bnk, kc=kc: e.matmul(
                bnk[0:LORA, 0:TW], aprojb[:, kc, :], h2_t[:, kc, :],
                start=(kc == 0), stop=(kc == JF - 1)),
                waits=(wts + [("s_S", h2_toks[-1])]) if kc == 0 else ())
        tok_xa2 = si(lambda e, bnk=bnk: e.activation(
            out=xa2_t[0:LORA, :], in_=bnk[0:LORA, 0:TW], func=AF.Identity,
            bias=0.0, scale=2.0), waits=[("s_T", lastmm)])
        bank_done(ib, [("s_S", tok_xa2)])
        tok_bproj = p.dma("sync", bprojb[:, :], sc_bproj,
                          waits=[wcast_done["bproj"]] + stB["bproj"])
        banks, bwaits = get_all_banks()
        for g8 in range(NG):
            views, vtoks, slots = stream_chunks("wproj", g8 * 8, g8 * 8 + 8)
            for kci in range(8):
                kc = g8 * 8 + kci
                for mc in range(JC):
                    w8 = []
                    if g8 == 0 and kci == 0 and mc == 0:
                        w8 = bwaits + [vtoks[kci]]
                    elif mc == 0 and (kci == 0 or vtoks[kci] is not vtoks[kci - 1]):
                        w8 = [vtoks[kci]]
                    lastmm = ti(lambda e, v=views[kci], mc=mc, kc=kc: e.matmul(
                        pbank[mc][:, 0:TW], v[:, mc * 128:(mc + 1) * 128],
                        h2_t[:, kc, :], start=(kc == 0), stop=False),
                        waits=w8)
            slots_release(slots, [("s_T", lastmm)])
        for mc in range(JC):
            lastmm = ti(lambda e, mc=mc: e.matmul(
                pbank[mc][:, 0:TW], bprojb[:, mc * 128:(mc + 1) * 128],
                xa2_t[0:LORA, :], start=False, stop=True),
                waits=[tok_bproj, ("s_S", tok_xa2)] if mc == 0 else ())
        tok_out = None
        prev_out = [None, None]
        fst = [ffn_st, AR.view(rS1, [TW])]
        for mc in range(JC):
            fb = fst[mc % 2]
            w9 = [("s_T", lastmm)] + (stB["out"] if mc == 0 else [])
            if prev_out[mc % 2] is not None:
                w9.append(("s_V", prev_out[mc % 2]))
            tok_f = si(lambda e, mc=mc, fb=fb: e.activation(
                out=fb[:, :], in_=pbank[mc][:, 0:TW], func=AF.Identity,
                bias=t_bprojv[:, mc:mc + 1], scale=1.0), waits=w9)
            tok_out = vi(lambda e, mc=mc, fb=fb: e.tensor_tensor(
                out=outB_t[:, mc, :], in0=x1B_t[:, mc, :], in1=fb[:, :],
                op=ALU.add), waits=[("s_S", tok_f)])
            prev_out[mc % 2] = tok_out
            bank_done(mc, [("s_V", tok_out)])
        tok_o = p.dma("sync", out[pp], outB_t, waits=[("s_V", tok_out)])
        stB[f"x1p{pp % 2}"] = [("s_V", tok_out)]
        stB["g"] = [("s_V", tok_mix)]
        stB["d2"] = [("s_V", tok_xf)]
        stB["sig"] = [("s_V", x2_toks[-1])]
        stB["x"] = [("s_V", tok_xa), tok_o]
        stB["out"] = [tok_o]
        stB["gk"] = [tok_after_fk]
        stB["kf"] = [("s_S", h2_toks[-1] if h2_toks else kf_toks[-1])]
        stB["xf"] = [tok_after_wfc]
        stB["bfc"] = [("s_T", p.counts["s_T"])]
        stB["bproj"] = [("s_T", p.counts["s_T"])]
        stB["stage"] = [tok_stg3]
        stB["mfree"] = [("s_V", tok_xf)]
        return tok_o

    vnow = [("s_V", p.counts["s_V"])]
    snow = [("s_S", p.counts["s_S"])]
    tnow = [("s_T", p.counts["s_T"])]
    stB["x1p0"] = vnow
    stB["x1p1"] = vnow

    stB["g"] = vnow + [tok_sp_x1[pp2] for pp2 in range(NP)]
    stB["d2"] = vnow
    stB["sig"] = vnow + snow
    stB["x"] = vnow
    stB["out"] = vnow
    stB["kf"] = vnow
    stB["gk"] = vnow + tnow

    tok_os = []
    for pp in range(NP):
        tok_os.append(stageB_pass(pp))

    p.emit("sync", lambda e: None, waits=tok_os)

    with nc.Block() as block:
        @block.sync
        def _(eng):
            for ev in p.events["sync"]:
                ev(eng)

        @block.tensor
        def _(eng):
            for ev in p.events["tensor"]:
                ev(eng)

        @block.vector
        def _(eng):
            for ev in p.events["vector"]:
                ev(eng)

        @block.scalar
        def _(eng):
            for ev in p.events["scalar"]:
                ev(eng)

        @block.gpsimd
        def _(eng):
            for ev in p.events["gpsimd"]:
                ev(eng)

    es.close()
    return nc


def _featmaj_vec(v, j):
    return np.ascontiguousarray(np.asarray(v, dtype=np.float32).reshape(j, 128).T)


def _prep_inputs(inputs, cfg):
    C, DF, NP, TW, LORA = cfg["C"], cfg["DF"], cfg["NP"], cfg["TW"], cfg["LORA"]
    JC, JF = C // 128, DF // 128
    TC = NP * TW
    QPS = NCORES // 2
    g = lambda k: np.asarray(inputs[k], dtype=np.float32)

    x = g("L0_attn_x")
    B, T, _ = x.shape
    assert T == QPS * TC and B == 2

    shared = {}
    wT = lambda a: np.asarray(a, np.float32).T

    def pmaj(a, kcn, m):
        return np.ascontiguousarray(
            a.reshape(kcn, 128, m).transpose(1, 0, 2).reshape(128, kcn * m))

    def pmaj_grp(a, kcn, m, gw=1024):
        ng = m // gw
        return np.ascontiguousarray(
            a.reshape(kcn, 128, ng, gw).transpose(1, 2, 0, 3).reshape(
                128, ng * kcn * gw))

    shared["wk"] = pmaj(wT(g("L0_attn_Wk")), JC, C)
    shared["wv"] = pmaj(wT(g("L0_attn_Wv")), JC, C)
    shared["wr"] = pmaj(wT(g("L0_attn_Wr")), JC, C)
    shared["wo"] = pmaj(wT(g("L0_attn_Wo")), JC, C)
    shared["fr"] = pmaj(wT(g("L0_attn_Fr")), JC, C)
    shared["fk"] = pmaj_grp(wT(g("L0_attn_Fk")), JC, DF)
    shared["fv"] = pmaj(wT(g("L0_attn_Fv")), JF, C)
    shared["wfc"] = pmaj_grp(wT(g("L0_ffn_W_fc")), JC, DF)
    shared["wproj"] = pmaj(wT(g("L0_ffn_W_proj")), JF, C)
    shared["afc"] = pmaj(wT(g("L0_ffn_A_fc")), JC, LORA)
    shared["aproj"] = pmaj(wT(g("L0_ffn_A_proj")), JF, LORA)
    shared["bfc"] = np.ascontiguousarray(wT(g("L0_ffn_B_fc")))
    shared["bproj"] = np.ascontiguousarray(wT(g("L0_ffn_B_proj")))
    shared["ln1g"] = _featmaj_vec(g("L0_attn_ln1_g"), JC)
    shared["ln1b"] = _featmaj_vec(g("L0_attn_ln1_b"), JC)
    shared["ln2g"] = _featmaj_vec(g("L0_attn_ln2_g"), JC)
    shared["ln2b"] = _featmaj_vec(g("L0_attn_ln2_b"), JC)
    shared["tmk"] = _featmaj_vec(g("L0_attn_tm_k"), JC)
    shared["tmv"] = _featmaj_vec(g("L0_attn_tm_v"), JC)
    shared["tmr"] = _featmaj_vec(g("L0_attn_tm_r"), JC)
    shared["fmk"] = _featmaj_vec(g("L0_attn_fm_k"), JC)
    shared["fmr"] = _featmaj_vec(g("L0_attn_fm_r"), JC)
    shared["tdecay"] = _featmaj_vec(g("L0_attn_time_decay"), JC)
    shared["tfirst"] = _featmaj_vec(g("L0_attn_time_first"), JC)
    shared["bfcv"] = _featmaj_vec(g("L0_ffn_b_fc"), JF)
    shared["bprojv"] = _featmaj_vec(g("L0_ffn_b_proj"), JC)

    in_maps = []
    for core in range(NCORES):
        s, q = core // QPS, core % QPS
        xs = x[s, q * TC:(q + 1) * TC, :]
        xs = np.ascontiguousarray(
            xs.reshape(NP, TW, JC, 128).transpose(0, 3, 2, 1))
        if q > 0:
            xp = np.ascontiguousarray(x[s, q * TC - 1, :].reshape(JC, 128).T)
            pvalid = np.ones((128, 1), np.float32)
        else:
            xp = np.zeros((128, JC), np.float32)
            pvalid = np.zeros((128, 1), np.float32)
        mpre = np.zeros(NCORES, np.float32)
        for r in range(NCORES):
            if r // QPS == s and r % QPS < q:
                mpre[r] = 1.0
        mrow = np.zeros(NCORES, np.float32)
        if q > 0:
            mrow[core - 1] = 1.0
        m = dict(shared)
        m["xin"] = xs
        m["xprev"] = xp
        m["prevvalid"] = pvalid
        m["maskpre"] = np.ascontiguousarray(np.broadcast_to(mpre, (128, NCORES)))
        m["maskbig"] = np.ascontiguousarray(
            np.broadcast_to((mpre - 1.0) * 1e30, (128, NCORES)).astype(np.float32))
        m["maskrow"] = np.ascontiguousarray(np.broadcast_to(mrow, (128, NCORES)))
        in_maps.append(m)
    return in_maps


def run(inputs, cfg=FULL_CFG, debug=False):
    C, NP, TW = cfg["C"], cfg["NP"], cfg["TW"]
    JC = C // 128
    TC = NP * TW
    QPS = NCORES // 2
    nc = build_program(cfg, debug=debug)
    in_maps = _prep_inputs(inputs, cfg)
    res = run_bass_kernel_spmd(nc, in_maps, core_ids=list(range(NCORES)))
    T = QPS * TC
    outx = np.empty((2, T, C), dtype=np.float32)
    for core in range(NCORES):
        s, q = core // QPS, core % QPS
        o = np.asarray(res.results[core]["out"])
        outx[s, q * TC:(q + 1) * TC, :] = o.transpose(0, 3, 2, 1).reshape(TC, C)
    return outx, res


def kernel(**inputs):
    outx, _ = run(inputs, FULL_CFG, debug=False)
    return outx


# version 18
# speedup vs baseline: 1.0183x; 1.0080x over previous
import numpy as np
from contextlib import ExitStack

import concourse.bass as bass
import concourse.mybir as mybir
from concourse.bass_utils import run_bass_kernel_spmd

F32 = mybir.dt.float32
BF16 = mybir.dt.bfloat16
I32 = mybir.dt.int32
AF = mybir.ActivationFunctionType
ALU = mybir.AluOpType

NCORES = 8
EPS = 1e-5
NEG_BIG = -1e30

FULL_CFG = dict(C=1024, DF=4096, NP=4, TW=512, LORA=16)


class Prog:
    ENGINES = ("sync", "tensor", "vector", "scalar", "gpsimd")

    def __init__(self, nc, es):
        self.nc = nc
        self.es = es
        self.events = {e: [] for e in self.ENGINES}
        self.sems = {}
        self.counts = {}

    def sem(self, name):
        self.sems[name] = self.es.enter_context(self.nc.semaphore(name))
        self.counts[name] = 0

    def emit(self, engine, fn, inc=None, amt=1, waits=()):
        waits = tuple(waits)
        post = None
        if inc is not None:
            self.counts[inc] += amt
            post = self.counts[inc]
        sems = self.sems

        def ev(eng, fn=fn, waits=waits, inc=inc, amt=amt):
            for s, v in waits:
                eng.wait_ge(sems[s], v)
            inst = fn(eng)
            if inc is not None:
                inst.then_inc(sems[inc], amt)
        self.events[engine].append(ev)
        return post

    def init_dma_pool(self, engine, n):
        names = []
        for i in range(n):
            nm = f"d_{engine}{i}"
            self.sem(nm)
            names.append(nm)
        self.dma_state = getattr(self, "dma_state", {})
        self.dma_state[engine] = {"sems": names, "rr": 0, "last": [None] * n}

    def dma(self, engine, out, in_, inc=None, waits=()):
        st = self.dma_state[engine]
        n = len(st["sems"])
        i = st["rr"] % n
        st["rr"] += 1
        sem = st["sems"][i]
        w = list(waits)
        if st["last"][i] is not None:
            w.append(st["last"][i])
        v = self.emit(engine, lambda eng, o=out, i_=in_: eng.dma_start(out=o, in_=i_),
                      inc=sem, amt=16, waits=w)
        st["last"][i] = (sem, v)
        return (sem, v)


class Arena:

    def __init__(self, nc, es, name, n_el, dtype):
        self.t = es.enter_context(nc.sbuf_tensor(name, [128, n_el], dtype))
        self.n_el = n_el
        self.off = 0

    def region(self, n):
        off = (self.off + 15) // 16 * 16
        assert off + n <= self.n_el, f"arena {self.t} overflow: {off}+{n}>{self.n_el}"
        self.off = off + n
        return off

    def view(self, off, shape):
        n = int(np.prod(shape))
        ap = self.t[:, off:off + n]
        if len(shape) == 2:
            ap = ap.rearrange("p (a b) -> p a b", a=shape[0], b=shape[1])
        elif len(shape) == 3:
            ap = ap.rearrange("p (a b c) -> p a b c", a=shape[0], b=shape[1], c=shape[2])
        return ap

    def alloc(self, shape):
        return self.view(self.region(int(np.prod(shape))), shape)


def build_program(cfg=FULL_CFG, debug=False):
    C, DF, NP, TW, LORA = cfg["C"], cfg["DF"], cfg["NP"], cfg["TW"], cfg["LORA"]
    JC = C // 128
    JF = DF // 128

    nc = bass.Bass("TRN2", target_bir_lowering=False, detect_race_conditions=False)
    es = ExitStack()
    p = Prog(nc, es)

    def din(name, shape, dt=F32):
        return nc.dram_tensor(name, shape, dt, kind="ExternalInput").ap()

    xin = din("xin", [NP, 128, JC, TW])
    xprev = din("xprev", [128, JC])
    prevvalid = din("prevvalid", [128, 1])
    maskpre = din("maskpre", [128, NCORES])
    maskbig = din("maskbig", [128, NCORES])
    maskrow = din("maskrow", [128, NCORES])

    W_INNER = {"wk": JC * C, "wv": JC * C, "wr": JC * C, "wo": JC * C,
               "fr": JC * C, "fk": JC * DF, "fv": JF * C,
               "wfc": JC * DF, "wproj": JF * C,
               "afc": JC * LORA, "aproj": JF * LORA}
    w_in = {nm: din(nm, [128, inner]) for nm, inner in W_INNER.items()}
    w_in["bfc"] = din("bfc", [LORA, DF])
    w_in["bproj"] = din("bproj", [LORA, C])

    PVN = ["ln1g", "ln1b", "ln2g", "ln2b", "tmk", "tmv", "tmr", "fmk", "fmr",
           "tdecay", "tfirst"]
    pvin = {nm: din(nm, [128, JC]) for nm in PVN}
    bfcv_in = din("bfcv", [128, JF])
    bprojv_in = din("bprojv", [128, JC])

    out = nc.dram_tensor("out", [NP, 128, JC, TW], F32, kind="ExternalOutput").ap()

    def dintt(name, shape, dt=F32, dbg=False):
        kind = "ExternalOutput" if (debug and dbg) else "Internal"
        return nc.dram_tensor(name, shape, dt, kind=kind)

    sp_ek = dintt("sp_ek", [NP, 128, JC, TW], dbg=True).ap()
    sp_beta = dintt("sp_beta", [NP, 128, JC, TW], dbg=True).ap()
    sp_r = dintt("sp_r", [NP, 128, JC, TW], dbg=True).ap()
    sp_xin = dintt("sp_xin", [NP, 128, JC, TW], dbg=True).ap()
    sp_x1 = dintt("sp_x1", [NP, 128, JC, TW], dbg=True).ap()
    sp_h = dintt("sp_h", [NP, 128, JC, TW], dbg=True).ap()
    sp_xk = dintt("sp_xk", [NP, 128, JC, TW], BF16, dbg=True).ap()
    sp_inc = dintt("sp_inc", [NP, 128, 2 * JC], dbg=True).ap()
    sp_acc = dintt("sp_acc", [128, 3 * JC], dbg=True).ap()
    sp_g = dintt("sp_g", [NP, 128, JC, TW], dbg=True).ap()
    sp_xaf = dintt("sp_xaf", [NP, 128, JC, TW], dbg=True).ap()
    sp_sig = dintt("sp_sig", [NP, 128, JC, TW], dbg=True).ap()
    sp_xf = dintt("sp_xf", [NP, 128, JC, TW], BF16, dbg=True).ap()
    sp_h2 = dintt("sp_h2", [NP, 128, JF, TW], BF16, dbg=True).ap()
    sp_k = dintt("sp_k", [NP, 128, JC, TW], dbg=True).ap()
    sp_mk = dintt("sp_mk", [NP, 128, JC], dbg=True).ap()
    sp_nmk = dintt("sp_nmk", [NP, 128, JC], dbg=True).ap()
    SCW = ["wk", "wv", "wr", "wo", "fr", "fk", "fv", "wfc", "wproj"]
    sc_w = {nm: dintt("sc_" + nm, [128, W_INNER[nm]], BF16).ap() for nm in SCW}
    sc_bfc = dintt("sc_bfc", [LORA, DF], BF16).ap()
    sc_bproj = dintt("sc_bproj", [LORA, C], BF16).ap()
    cc_sum_in = dintt("cc_sum_in", [128, 3 * JC])
    cc_sum_out = dintt("cc_sum_out", [NCORES, 128, 3 * JC])
    dbg_sum = dintt("dbg_sum", [128, NCORES, 3 * JC], dbg=True).ap() if debug else None
    cc_row_in = dintt("cc_row_in", [128, JC])
    cc_row_out = dintt("cc_row_out", [NCORES, 128, JC])
    qdram = dintt("qdram", [128, 2])

    for s in ["s_cc", "s_T", "s_V", "s_S", "s_G"]:
        p.sem(s)
    p.init_dma_pool("sync", 10)
    p.init_dma_pool("scalar", 4)
    p.init_dma_pool("gpsimd", 10)

    sbt = lambda name, shape, dt=F32: es.enter_context(nc.sbuf_tensor(name, shape, dt))
    PV = {nm: sbt("t_" + nm, [128, JC]) for nm in PVN}
    t_bfcv = sbt("t_bfcv", [128, JF])
    t_bprojv = sbt("t_bprojv", [128, JC])
    t_xprev = sbt("t_xprev", [128, JC])
    t_pvalid = sbt("t_pvalid", [128, 1])
    t_mpre = sbt("t_mpre", [128, NCORES])
    t_mbig = sbt("t_mbig", [128, NCORES])
    t_mrow = sbt("t_mrow", [128, NCORES])
    t_w = sbt("t_w", [128, JC])
    t_ew = sbt("t_ew", [128, JC])
    t_eu = sbt("t_eu", [128, JC])
    t_tww = sbt("t_tww", [128, JC])
    t_spanw = sbt("t_spanw", [128, JC])
    t_eps = sbt("t_eps", [128, 1])
    t_ones = sbt("t_ones", [128, 128], BF16)
    t_hcol = sbt("t_hcol", [128, JC])
    t_gcol = sbt("t_gcol", [128, JC])
    t_mk = [sbt(f"t_mk{i}", [128, JC]) for i in range(NP)]
    t_negmk = [sbt(f"t_negmk{i}", [128, JC]) for i in range(NP)]
    t_za = sbt("t_za", [128, JC])
    t_zb = sbt("t_zb", [128, JC])
    t_run = [sbt(f"t_run{i}", [128, 3 * JC]) for i in range(NP + 1)]
    t_inc = [sbt(f"t_inc{i}", [128, 2 * JC]) for i in range(NP)]
    t_gath = sbt("t_gath", [128, NCORES, 3 * JC])
    t_rgath = sbt("t_rgath", [128, NCORES, JC])
    t_acc = sbt("t_acc", [128, 3 * JC])
    t_cj1 = sbt("t_cj1", [128, JC])
    t_cj2 = sbt("t_cj2", [128, JC])
    t_cj3 = sbt("t_cj3", [128, JC])
    t_qv = sbt("t_qv", [128, 2])
    t_qrow = sbt("t_qrow", [1, 256])
    t_qs = sbt("t_qs", [128, 2])
    t_qis = sbt("t_qis", [128, 2])
    t_rowbf = sbt("t_rowbf", [128, JC], BF16)
    t_rowbf2 = sbt("t_rowbf2", [128, JC], BF16)
    t_r1 = sbt("t_r1", [128, 1])
    t_r2 = sbt("t_r2", [128, 1])
    t_r3 = sbt("t_r3", [128, 1])
    t_rowa = sbt("t_rowa", [128, JC])
    t_x1c = [sbt(f"t_x1c{i}", [128, JC]) for i in range(NP)]
    t_gc = [sbt(f"t_gc{i}", [128, JC]) for i in range(NP)]
    t_zero = sbt("t_zero", [128, 1])

    afcb = sbt("afcb", [128, JC, LORA], BF16)
    aprojb = sbt("aprojb", [128, JF, LORA], BF16)

    WR_SLOT = 3072
    NSLOT = 3
    wring = sbt("wring", [128, NSLOT * WR_SLOT], BF16)
    wslot = [wring[:, i * WR_SLOT:(i + 1) * WR_SLOT] for i in range(NSLOT)]
    wslot_tok = [[] for _ in range(NSLOT)]
    ring_rr = [0]

    WST_N = 512
    NWST = 4
    WST = [sbt(f"wst{i}", [128, WST_N], F32) for i in range(NWST)]
    WSTI = sbt("wsti", [128, WST_N], I32)
    WSTO2 = [sbt(f"wsto{i}", [128, WST_N], BF16) for i in range(2)]
    WSTO = WSTO2[0]
    wst_free = [[] for _ in range(NWST)]
    wsto_free = [[], []]
    wsti_free = [[]]

    TWJC = JC * TW
    AR = Arena(nc, es, "arena_f", 6 * TWJC + 3 * TW + 64, F32)
    rA0 = AR.region(TWJC)
    rA1 = AR.region(TWJC)
    rB = AR.region(TWJC)
    rC = AR.region(TWJC)
    rD = AR.region(TWJC)
    rE = AR.region(TWJC)
    rS1 = AR.region(TW)
    rS2 = AR.region(TW)
    rS3 = AR.region(TW)
    ABF = Arena(nc, es, "arena_b", 8 * 1024 + 16 * 1024 + 1024 + 4096 + 32, BF16)
    bR1 = ABF.region(8192)
    bR2 = ABF.region(16384)
    bR4 = ABF.region(1024)
    bR5 = ABF.region(4096)
    bfcb = sbt("bfcb", [LORA, DF], BF16)
    bprojb = sbt("bprojb", [LORA, C], BF16)

    pbank = [es.enter_context(nc.psum_tensor(f"pb{i}", [128, 512], F32)) for i in range(8)]
    bank_tok = [[] for _ in range(8)]
    bank_rr = [0]

    def get_bank(extra_waits=()):
        i = bank_rr[0] % 8
        bank_rr[0] += 1
        w = list(bank_tok[i]) + list(extra_waits)
        bank_tok[i] = []
        return i, pbank[i], w

    def bank_done(i, toks):
        bank_tok[i] = list(toks)

    def get_all_banks():
        w = []
        for i in range(8):
            w += bank_tok[i]
            bank_tok[i] = []
        bank_rr[0] = 0
        return pbank, w

    def vi(fn, waits=()):
        return p.emit("vector", fn, inc="s_V", waits=waits)

    def vd(fn, waits=()):
        return p.emit("vector", fn, inc="s_V",
                      waits=list(waits) + [("s_V", p.counts["s_V"])])

    def sd(fn, waits=()):
        return p.emit("scalar", fn, inc="s_S",
                      waits=list(waits) + [("s_S", p.counts["s_S"])])

    def si(fn, waits=()):
        return p.emit("scalar", fn, inc="s_S", waits=waits)

    def ti(fn, waits=()):
        return p.emit("tensor", fn, inc="s_T", waits=waits)

    _ptoks = []
    for nm in PVN:
        _ptoks.append(p.dma("sync", PV[nm][:, :], pvin[nm]))
    _ptoks.append(p.dma("sync", t_bfcv[:, :], bfcv_in))
    _ptoks.append(p.dma("sync", t_bprojv[:, :], bprojv_in))
    _ptoks.append(p.dma("sync", t_xprev[:, :], xprev))
    _ptoks.append(p.dma("sync", t_pvalid[:, :], prevvalid))
    _ptoks.append(p.dma("sync", t_mpre[:, :], maskpre))
    _ptoks.append(p.dma("sync", t_mbig[:, :], maskbig))
    _ptoks.append(p.dma("sync", t_mrow[:, :], maskrow))
    _tp = vi(lambda e: e.memset(t_r3[:, :], 0.0), waits=_ptoks)
    tok_params = ("s_V", _tp)

    vi(lambda e: e.memset(t_ones[:, :], 1.0))
    vi(lambda e: e.memset(t_eps[:, :], EPS))
    vi(lambda e: e.memset(t_zero[:, :], 0.0))
    tok = si(lambda e: e.activation(out=t_w[:, :], in_=PV["tdecay"][:, :], func=AF.Exp,
                                    bias=0.0), waits=[tok_params])
    tok_w = vi(lambda e: e.tensor_scalar(out=t_w[:, :], in0=t_w[:, :], scalar1=-1.0,
                                         scalar2=None, op0=ALU.mult),
               waits=[("s_S", tok)])
    tok_ew = si(lambda e: e.activation(out=t_ew[:, :], in_=t_w[:, :], func=AF.Exp,
                                       bias=0.0), waits=[("s_V", tok_w)])
    tok_eu = si(lambda e: e.activation(out=t_eu[:, :], in_=PV["tfirst"][:, :],
                                       func=AF.Exp, bias=0.0))
    tok_tww = vd(lambda e: e.tensor_scalar(out=t_tww[:, :], in0=t_w[:, :],
                                           scalar1=float(TW), scalar2=None,
                                           op0=ALU.mult), waits=[("s_V", tok_w)])
    tok_spanw = vd(lambda e: e.tensor_scalar(out=t_spanw[:, :], in0=t_w[:, :],
                                             scalar1=float(NP * TW), scalar2=None,
                                             op0=ALU.mult))
    vi(lambda e: e.memset(t_run[0][:, 0:2 * JC], 0.0))
    tok_run = vi(lambda e: e.memset(t_run[0][:, 2 * JC:3 * JC], NEG_BIG))

    def ew_bcast(j, n):
        return bass.AP(t_ew, j, [[JC, 128], [0, n]])

    wcast_done = {}
    _wq = {"buf": 0, "obuf": 0}
    QIDX = {"wfc": 0, "wproj": 1}
    tok_qscale = [None]

    task_q = []
    task_qv = []

    def drain_tasks(n):
        for _ in range(min(n, len(task_q))):
            task_q.pop(0)()

    def drain_vtasks(n):
        for _ in range(min(n, len(task_qv))):
            task_qv.pop(0)()

    def drain_all_tasks():
        while task_qv:
            task_qv.pop(0)()
        while task_q:
            task_q.pop(0)()

    def cast_pieces(nm, quant=False):
        inner = W_INNER[nm]
        srcf = w_in[nm].rearrange("a b -> (a b)")
        dstf = sc_w[nm].rearrange("a b -> (a b)")
        total = 128 * inner
        piece = 128 * WST_N
        parts = (total + piece - 1) // piece
        out = []
        for h in range(parts):
            def body(h=h):
                nw = min(piece, total - h * piece)
                mw = nw // 128
                b = _wq["buf"]; _wq["buf"] = (b + 1) % NWST
                st = WST[b]
                tok_l = p.dma("sync", st[:, 0:mw], srcf[h * piece:h * piece + nw],
                              waits=wst_free[b])
                ob = _wq["obuf"]; _wq["obuf"] ^= 1
                OT = WSTO2[ob]
                if quant:
                    qi = QIDX[nm]
                    assert tok_qscale[0] is not None
                    tok_i = si(lambda e, st=st, mw=mw, qi=qi: e.activation(
                        out=WSTI[:, 0:mw], in_=st[:, 0:mw], func=AF.Identity,
                        bias=0.0, scale=t_qis[:, qi:qi + 1]),
                        waits=[tok_l, ("s_S", tok_qscale[0])] + wsti_free[0])
                    tok_o = si(lambda e, mw=mw, qi=qi, OT=OT: e.activation(
                        out=OT[:, 0:mw], in_=WSTI[:, 0:mw], func=AF.Identity,
                        bias=0.0, scale=t_qs[:, qi:qi + 1]), waits=wsto_free[ob])
                    wsti_free[0] = [("s_S", tok_o)]
                else:
                    tok_o = si(lambda e, st=st, mw=mw, OT=OT: e.activation(
                        out=OT[:, 0:mw], in_=st[:, 0:mw], func=AF.Copy,
                        bias=0.0), waits=[tok_l] + wsto_free[ob])
                wst_free[b] = [("s_S", tok_o)]
                tok_g = p.dma(
                    "scalar", dstf[h * piece:h * piece + nw],
                    OT[:, 0:mw], waits=[("s_S", tok_o)])
                wsto_free[ob] = [tok_g]
                wcast_done[nm] = tok_g
            out.append(body)
        return out

    def emit_cast_tasks(nm, quant=False):
        for f in cast_pieces(nm, quant):
            f()

    def emit_lora_tasks():
        for nm, dst, kcn in [("afc", afcb, JC), ("aproj", aprojb, JF)]:
            n = kcn * LORA
            b = _wq["buf"]; _wq["buf"] = (b + 1) % NWST
            st = WST[b]
            stv = st[:, 0:n].rearrange("p (a c) -> p a c", a=kcn, c=LORA)
            tok_l = p.dma("sync", st[:, 0:n], w_in[nm], waits=wst_free[b])
            tok_o = si(lambda e, stv=stv, dst=dst: e.activation(
                out=dst[:, :, :], in_=stv, func=AF.Copy, bias=0.0),
                waits=[tok_l])
            wst_free[b] = [("s_S", tok_o)]
        for nm, dst, m in [("bfc", sc_bfc, DF), ("bproj", sc_bproj, C)]:
            parts = (m + WST_N - 1) // WST_N
            for h in range(parts):
                mw = min(WST_N, m - h * WST_N)
                b = _wq["buf"]; _wq["buf"] = (b + 1) % NWST
                st = WST[b]
                tok_l = p.dma("sync", st[0:LORA, 0:mw],
                              w_in[nm][:, h * WST_N:h * WST_N + mw],
                              waits=wst_free[b])
                tok_o = si(lambda e, st=st, mw=mw: e.activation(
                    out=WSTO[0:LORA, 0:mw], in_=st[0:LORA, 0:mw], func=AF.Copy,
                    bias=0.0), waits=[tok_l] + wsto_free[0])
                wst_free[b] = [("s_S", tok_o)]
                tok_g = p.dma("scalar", dst[:, h * WST_N:h * WST_N + mw],
                              WSTO[0:LORA, 0:mw], waits=[("s_S", tok_o)])
                wsto_free[0] = [tok_g]
            wcast_done[nm] = tok_g

    def quant_scale_pieces():
        tok_red = {0: None, 1: None}
        qstate = {0: True, 1: True}
        out = []
        for qi, nm in enumerate(["wfc", "wproj"]):
            srcf = w_in[nm].rearrange("a b -> (a b)")
            total = 128 * W_INNER[nm]
            piece = 128 * WST_N
            parts = (total + piece - 1) // piece
            for h in range(parts):
                def body(h=h, qi=qi, srcf=srcf, total=total):
                    nw = min(piece, total - h * piece)
                    mw = nw // 128
                    b = _wq["buf"]; _wq["buf"] = (b + 1) % NWST
                    st = WST[b]
                    tok_l = p.dma("sync", st[:, 0:mw],
                                  srcf[h * piece:h * piece + nw],
                                  waits=wst_free[b])
                    if qstate[qi]:
                        tok_r = vi(lambda e, st=st, mw=mw, qi=qi: e.tensor_reduce(
                            out=t_qv[:, qi:qi + 1], in_=st[:, 0:mw],
                            axis=mybir.AxisListType.X, op=ALU.max,
                            apply_absolute_value=True), waits=[tok_l])
                        qstate[qi] = False
                    else:
                        tok_r = vi(lambda e, st=st, mw=mw: e.tensor_reduce(
                            out=t_r1[:, :], in_=st[:, 0:mw],
                            axis=mybir.AxisListType.X, op=ALU.max,
                            apply_absolute_value=True), waits=[tok_l])
                        tok_r = vi(lambda e, qi=qi: e.tensor_tensor(
                            out=t_qv[:, qi:qi + 1], in0=t_qv[:, qi:qi + 1],
                            in1=t_r1[:, :], op=ALU.max), waits=[("s_V", tok_r)])
                    wst_free[b] = [("s_V", tok_r)]
                    tok_red[qi] = tok_r
                out.append(body)
        out.append(lambda: quant_scale_final(tok_red))
        return out

    def quant_scale_final(tok_red):
        tok_d1 = p.dma("gpsimd", qdram.ap(), t_qv[:, 0:2],
                       waits=[("s_V", tok_red[0]), ("s_V", tok_red[1])])
        tok_d2 = p.dma("gpsimd", t_qrow[:, :], qdram.ap().rearrange("a b -> (a b)"),
                       waits=[tok_d1])
        tok_ms = []
        for qi in range(2):
            qv = bass.AP(t_qrow, qi, [[256, 1], [2, 128]])
            tok_ms.append(vd(lambda e, qv=qv, qi=qi: e.tensor_reduce(
                out=t_qrow[:, 254 + qi:255 + qi], in_=qv,
                axis=mybir.AxisListType.X, op=ALU.max),
                waits=[tok_d2]))
        tok_qq = vd(lambda e: e.tensor_scalar(out=t_qrow[:, 252:254],
                                              in0=t_qrow[:, 254:256],
                                              scalar1=1.0 / 127.0, scalar2=None,
                                              op0=ALU.mult),
                    waits=[("s_V", tok_ms[-1])])
        vd(lambda e: e.reciprocal(out=t_qrow[:, 250:252], in_=t_qrow[:, 252:254]),
           waits=[("s_V", tok_qq)])
        tok_of = vd(lambda e: e.memset(t_qrow[:, 248:249], 1.0))
        ib, bnk, wts = get_bank([("s_V", tok_of)])
        onesrow = bass.AP(t_qrow, 248, [[256, 1], [0, 128]])
        tok_t = ti(lambda e, bnk=bnk: e.matmul(bnk[:, 0:4], onesrow,
                                               t_qrow[:, 250:254],
                                               start=True, stop=True), waits=wts)
        sd(lambda e, bnk=bnk: e.activation(out=t_qis[:, 0:2], in_=bnk[:, 0:2],
                                           func=AF.Identity, bias=0.0, scale=1.0),
           waits=[("s_T", tok_t)])
        tok_qs = sd(lambda e, bnk=bnk: e.activation(out=t_qs[:, 0:2], in_=bnk[:, 2:4],
                                                    func=AF.Identity, bias=0.0,
                                                    scale=1.0))
        bank_done(ib, [("s_S", tok_qs)])
        tok_qscale[0] = tok_qs

    def stream_chunks(src_name, k0, k1, waits=(), m=None):
        src = sc_w[src_name]
        if m is None:
            m = C
        per_slot = WR_SLOT // m
        views = []
        vtoks = []
        slots = []
        kc = k0
        while kc < k1:
            kh = min(k1, kc + per_slot)
            slot = ring_rr[0] % NSLOT
            ring_rr[0] += 1
            wts = list(wslot_tok[slot]) + list(waits) + [wcast_done[src_name]]
            wslot_tok[slot] = []
            v = wslot[slot][:, 0:(kh - kc) * m].rearrange("p (a c) -> p a c",
                                                          a=kh - kc, c=m)
            tok = p.dma("sync", wslot[slot][:, 0:(kh - kc) * m],
                        src[:, kc * m:kh * m], waits=wts)
            for i in range(kh - kc):
                views.append(v[:, i, :])
                vtoks.append(tok)
            slots.append(slot)
            kc = kh
        return views, vtoks, slots

    def stream_cols(src_name, g8, waits=()):
        return stream_chunks(src_name, g8 * JC, g8 * JC + JC, waits=waits, m=1024)

    def slots_release(slots, toks):
        for s in slots:
            wslot_tok[s] = list(wslot_tok[s]) + list(toks)

    m_st = AR.view(rS1, [TW])
    e2_st = AR.view(rS2, [TW])
    msq_st = AR.view(rS3, [TW])

    def ln_stats(x_ap, xb_ap, xsq_ap, waits, stage_free=(), m_free=()):
        si(lambda e: e.activation(out=xb_ap, in_=x_ap, func=AF.Copy, bias=0.0),
           waits=list(waits) + list(stage_free))
        tok_s = si(lambda e: e.activation(out=xsq_ap, in_=x_ap, func=AF.Square,
                                          bias=0.0))
        ib1, b1, w1 = get_bank([("s_S", tok_s)])
        for j in range(JC):
            tok_t1 = ti(lambda e, j=j, b1=b1: e.matmul(
                b1[:, 0:TW], t_ones[:, :], xb_ap[:, j, :],
                start=(j == 0), stop=(j == JC - 1)), waits=w1 if j == 0 else ())
        ib2, b2, w2 = get_bank()
        for j in range(JC):
            tok_t2 = ti(lambda e, j=j, b2=b2: e.matmul(
                b2[:, 0:TW], t_ones[:, :], xsq_ap[:, j, :],
                start=(j == 0), stop=(j == JC - 1)), waits=w2 if j == 0 else ())
        tok_m = si(lambda e, b1=b1: e.activation(out=m_st[:, :], in_=b1[:, 0:TW],
                                                 func=AF.Identity, bias=0.0,
                                                 scale=1.0 / C),
                   waits=[("s_T", tok_t1)] + list(m_free))
        tok_e2 = si(lambda e, b2=b2: e.activation(out=e2_st[:, :], in_=b2[:, 0:TW],
                                                  func=AF.Identity, bias=0.0,
                                                  scale=1.0 / C),
                    waits=[("s_T", tok_t2)])
        bank_done(ib1, [("s_S", tok_m)])
        bank_done(ib2, [("s_S", tok_e2)])
        tok_msq = si(lambda e: e.activation(out=msq_st[:, :], in_=m_st[:, :],
                                            func=AF.Square, bias=0.0))
        tok_var = vi(lambda e: e.tensor_tensor(out=e2_st[:, :], in0=e2_st[:, :],
                                               in1=msq_st[:, :], op=ALU.subtract),
                     waits=[("s_S", tok_msq)])
        si(lambda e: e.activation(out=e2_st[:, :], in_=e2_st[:, :], func=AF.Ln,
                                  bias=t_eps[:, :], scale=1.0),
           waits=[("s_V", tok_var)])
        tok_rstd = si(lambda e: e.activation(out=e2_st[:, :], in_=e2_st[:, :],
                                             func=AF.Exp, bias=0.0, scale=-0.5))
        return tok_rstd, ("s_T", tok_t2)

    def ln_apply(x_ap, out_ap, tok_rstd, g=None, b=None, waits=(), scratch=None):
        sc = scratch if scratch is not None else out_ap
        tok = None
        for j in range(JC):
            tok = vi(lambda e, j=j: e.tensor_tensor(
                out=sc[:, j, :], in0=x_ap[:, j, :], in1=m_st[:, :], op=ALU.subtract),
                waits=(list(waits) + [("s_S", tok_rstd)]) if j == 0 else ())
        for j in range(JC):
            if g is None:
                tok = vi(lambda e, j=j: e.tensor_tensor(
                    out=out_ap[:, j, :], in0=sc[:, j, :], in1=e2_st[:, :],
                    op=ALU.mult))
            else:
                vi(lambda e, j=j: e.tensor_tensor(
                    out=sc[:, j, :], in0=sc[:, j, :], in1=e2_st[:, :], op=ALU.mult))
                tok = vi(lambda e, j=j: e.tensor_scalar(
                    out=out_ap[:, j, :], in0=sc[:, j, :],
                    scalar1=g[:, j:j + 1], scalar2=b[:, j:j + 1],
                    op0=ALU.mult, op1=ALU.add))
        return tok

    def row_ln(x_ap, out_ap, g=None, b=None, waits=()):
        sd(lambda e: e.activation(out=t_rowbf[:, :], in_=x_ap, func=AF.Copy, bias=0.0),
           waits=list(waits) + [("s_T", p.counts["s_T"]), ("s_V", p.counts["s_V"])])
        tok_s = sd(lambda e: e.activation(out=t_rowbf2[:, :], in_=x_ap,
                                          func=AF.Square, bias=0.0))
        ib, bnk, wts = get_bank([("s_S", tok_s)])
        ti(lambda e, bnk=bnk: e.matmul(bnk[:, 0:JC], t_ones[:, :], t_rowbf[:, :],
                                       start=True, stop=True), waits=wts)
        tok_t = ti(lambda e, bnk=bnk: e.matmul(bnk[:, JC:2 * JC], t_ones[:, :],
                                               t_rowbf2[:, :], start=True, stop=True))
        vd(lambda e, bnk=bnk: e.tensor_reduce(
            out=t_r1[:, :], in_=bnk[:, 0:JC], axis=mybir.AxisListType.X, op=ALU.add),
            waits=[("s_T", tok_t)])
        tok_v = vd(lambda e, bnk=bnk: e.tensor_reduce(
            out=t_r2[:, :], in_=bnk[:, JC:2 * JC], axis=mybir.AxisListType.X,
            op=ALU.add))
        bank_done(ib, [("s_V", tok_v)])
        vd(lambda e: e.tensor_scalar(out=t_r1[:, :], in0=t_r1[:, :], scalar1=1.0 / C,
                                     scalar2=None, op0=ALU.mult),
           waits=[("s_V", tok_v)])
        tok_v = vd(lambda e: e.tensor_scalar(out=t_r2[:, :], in0=t_r2[:, :],
                                             scalar1=1.0 / C, scalar2=None,
                                             op0=ALU.mult))
        tok_s = sd(lambda e: e.activation(out=t_r3[:, :], in_=t_r1[:, :],
                                          func=AF.Square, bias=0.0),
                   waits=[("s_V", tok_v)])
        tok_v = vd(lambda e: e.tensor_tensor(out=t_r2[:, :], in0=t_r2[:, :],
                                             in1=t_r3[:, :], op=ALU.subtract),
                   waits=[("s_S", tok_s)])
        sd(lambda e: e.activation(out=t_r2[:, :], in_=t_r2[:, :], func=AF.Ln,
                                  bias=t_eps[:, :], scale=1.0), waits=[("s_V", tok_v)])
        tok_s = sd(lambda e: e.activation(out=t_r2[:, :], in_=t_r2[:, :], func=AF.Exp,
                                          bias=0.0, scale=-0.5))
        tok_v = vd(lambda e: e.tensor_scalar(out=out_ap, in0=x_ap, scalar1=t_r1[:, :],
                                             scalar2=t_r2[:, :], op0=ALU.subtract,
                                             op1=ALU.mult), waits=[("s_S", tok_s)])
        if g is not None:
            vd(lambda e: e.tensor_tensor(out=out_ap, in0=out_ap, in1=g, op=ALU.mult))
            tok_v = vd(lambda e: e.tensor_tensor(out=out_ap, in0=out_ap, in1=b,
                                                 op=ALU.add))
        return tok_v

    x_t = [AR.view(rA0, [JC, TW]), AR.view(rA1, [JC, TW])]
    beta_t = [AR.view(rA0, [JC, TW]), AR.view(rA1, [JC, TW])]
    xin_t = AR.view(rB, [JC, TW])
    h_t = AR.view(rC, [JC, TW])
    ek1_t = AR.view(rC, [JC, TW])
    d_t = AR.view(rD, [JC, TW])
    prod_t = AR.view(rD, [JC, TW])
    r1_t = AR.view(rE, [JC, TW])
    xb_t = ABF.view(bR1, [JC, TW])
    xsq_t = ABF.view(bR1 + TWJC, [JC, TW])
    xk_t = ABF.view(bR2, [JC, TW])
    xv_t = ABF.view(bR2 + TWJC, [JC, TW])
    xr_t = ABF.view(bR2 + 2 * TWJC, [JC, TW])

    prev = {"xfree": [[], []], "xinfree": [], "ekfree": [], "betafree": [[], []],
            "rfree": [], "stagefree": [], "mixfree": [], "mfree": [],
            "applyfree": []}

    def phase1_pass(pp):
        par = pp % 2
        tx = x_t[par]
        tbeta = beta_t[par]
        if pp in xpre:
            tok_x = xpre.pop(pp)
        else:
            tok_x = p.dma("sync", tx, xin[pp], waits=prev["xfree"][par])
        tok_rstd, tok_stg = ln_stats(tx, xb_t, xsq_t, waits=[tok_x],
                                     stage_free=prev["stagefree"],
                                     m_free=prev["mfree"])
        tok_xin = ln_apply(tx, xin_t, tok_rstd, waits=prev["xinfree"])
        if pp == NP - 1 and not debug:
            tok_sp_xin = None
        else:
            tok_sp_xin = p.dma("gpsimd", sp_xin[pp], xin_t,
                               waits=[("s_V", tok_xin)])
        tok_rstd, tok_stg = ln_stats(xin_t, xb_t, xsq_t, waits=[("s_V", tok_xin)])
        tok_h = ln_apply(xin_t, h_t, tok_rstd, g=PV["ln1g"], b=PV["ln1b"],
                         waits=prev["mixfree"])
        prev["xinfree"] = [tok_sp_xin] if tok_sp_xin is not None else []
        prev["stagefree"] = [tok_stg]
        prev["mfree"] = [("s_V", tok_h)]
        for j in range(JC):
            vi(lambda e, j=j: e.tensor_tensor(
                out=d_t[:, j, 1:TW], in0=h_t[:, j, 1:TW], in1=h_t[:, j, 0:TW - 1],
                op=ALU.subtract))
            vd(lambda e, j=j: e.tensor_tensor(
                out=d_t[:, j, 0:1], in0=h_t[:, j, 0:1], in1=t_hcol[:, j:j + 1],
                op=ALU.subtract))
        tok_mix = None
        mix_toks = []
        for tm, xo in ((PV["tmk"], xk_t), (PV["tmv"], xv_t), (PV["tmr"], xr_t)):
            for j in range(JC):
                vi(lambda e, j=j, tm=tm, xo=xo: e.scalar_tensor_tensor(
                    out=xo[:, j, 1:TW], in0=d_t[:, j, 1:TW], scalar=tm[:, j:j + 1],
                    in1=h_t[:, j, 0:TW - 1], op0=ALU.mult, op1=ALU.add))
                tok_mix = vd(lambda e, j=j, tm=tm, xo=xo: e.scalar_tensor_tensor(
                    out=xo[:, j, 0:1], in0=d_t[:, j, 0:1], scalar=tm[:, j:j + 1],
                    in1=t_hcol[:, j:j + 1], op0=ALU.mult, op1=ALU.add))
            mix_toks.append(tok_mix)
        tok_hcol = vi(lambda e: e.tensor_copy(t_hcol[:, :], h_t[:, :, TW - 1:TW]),
                      waits=[("s_V", tok_h)])
        if debug:
            p.dma("gpsimd", sp_h[pp], h_t, waits=[("s_V", tok_hcol)])
            p.dma("gpsimd", sp_xk[pp], xk_t, waits=[("s_V", tok_mix)])
        ek_toks = [None] * JC
        views, vtoks, slots = stream_chunks("wk", 0, JC)
        lastmm = None
        for mc in range(JC):
            ib, bnk, wts = get_bank()
            for kc in range(JC):
                lastmm = ti(lambda e, bnk=bnk, v=views[kc], mc=mc, kc=kc: e.matmul(
                    bnk[:, 0:TW], v[:, mc * 128:(mc + 1) * 128], xk_t[:, kc, :],
                    start=(kc == 0), stop=(kc == JC - 1)),
                    waits=(wts + [vtoks[kc], ("s_V", mix_toks[0])]) if kc == 0
                    else ([vtoks[kc]] if vtoks[kc] is not vtoks[kc - 1] else ()))
            tok_red = vi(lambda e, bnk=bnk, mc=mc, pp=pp: e.tensor_reduce(
                out=t_mk[pp][:, mc:mc + 1], in_=bnk[:, 0:TW],
                axis=mybir.AxisListType.X, op=ALU.max), waits=[("s_T", lastmm)])
            vi(lambda e, mc=mc, pp=pp: e.tensor_scalar(
                out=t_negmk[pp][:, mc:mc + 1], in0=t_mk[pp][:, mc:mc + 1],
                scalar1=-1.0, scalar2=None, op0=ALU.mult),
                waits=[("s_V", tok_red)])
            if debug:
                si(lambda e, bnk=bnk, mc=mc: e.activation(
                    out=d_t[:, mc, :], in_=bnk[:, 0:TW], func=AF.Identity,
                    bias=0.0, scale=1.0), waits=[("s_V", p.counts["s_V"])])
            tok_ek = si(lambda e, bnk=bnk, mc=mc, pp=pp: e.activation(
                out=ek1_t[:, mc, :], in_=bnk[:, 0:TW], func=AF.Exp,
                bias=t_negmk[pp][:, mc:mc + 1], scale=1.0),
                waits=[("s_V", p.counts["s_V"])] + prev["ekfree"] if mc == 0
                else [("s_V", p.counts["s_V"])])
            ek_toks[mc] = tok_ek
            bank_done(ib, [("s_S", tok_ek)])
            drain_vtasks(1)
        slots_release(slots, [("s_T", lastmm)])
        dbg_ktok = []
        if debug:
            tk1 = p.dma("gpsimd", sp_k[pp], d_t, waits=[("s_S", ek_toks[-1])])
            p.dma("gpsimd", sp_mk[pp], t_mk[pp][:, :], waits=[("s_S", ek_toks[-1])])
            p.dma("gpsimd", sp_nmk[pp], t_negmk[pp][:, :], waits=[("s_S", ek_toks[-1])])
            dbg_ktok = [tk1]
        views, vtoks, slots = stream_chunks("wv", 0, JC)
        beta_toks = [None] * JC
        for mc in range(JC):
            ib, bnk, wts = get_bank()
            for kc in range(JC):
                lastmm = ti(lambda e, bnk=bnk, v=views[kc], mc=mc, kc=kc: e.matmul(
                    bnk[:, 0:TW], v[:, mc * 128:(mc + 1) * 128], xv_t[:, kc, :],
                    start=(kc == 0), stop=(kc == JC - 1)),
                    waits=(wts + [vtoks[kc], ("s_V", mix_toks[1])])
                    if (kc == 0 and mc == 0) else (wts + [vtoks[kc]]) if kc == 0
                    else ([vtoks[kc]] if vtoks[kc] is not vtoks[kc - 1] else ()))
            tok_b = vi(lambda e, bnk=bnk, mc=mc, tbeta=tbeta: e.tensor_tensor(
                out=tbeta[:, mc, :], in0=ek1_t[:, mc, :], in1=bnk[:, 0:TW],
                op=ALU.mult),
                waits=[("s_T", lastmm), ("s_S", ek_toks[mc])]
                + (prev["betafree"][par] if mc == 0 else []))
            beta_toks[mc] = tok_b
            bank_done(ib, [("s_V", tok_b)])
            drain_vtasks(1)
        slots_release(slots, [("s_T", lastmm)])
        views, vtoks, slots = stream_chunks("wr", 0, JC)
        r_toks = [None] * JC
        for mc in range(JC):
            ib, bnk, wts = get_bank()
            for kc in range(JC):
                lastmm = ti(lambda e, bnk=bnk, v=views[kc], mc=mc, kc=kc: e.matmul(
                    bnk[:, 0:TW], v[:, mc * 128:(mc + 1) * 128], xr_t[:, kc, :],
                    start=(kc == 0), stop=(kc == JC - 1)),
                    waits=(wts + [vtoks[kc], ("s_V", mix_toks[2])])
                    if (kc == 0 and mc == 0) else (wts + [vtoks[kc]]) if kc == 0
                    else ([vtoks[kc]] if vtoks[kc] is not vtoks[kc - 1] else ()))
            tok_r = si(lambda e, bnk=bnk, mc=mc: e.activation(
                out=r1_t[:, mc, :], in_=bnk[:, 0:TW], func=AF.Sigmoid, bias=0.0,
                scale=1.0),
                waits=[("s_T", lastmm)] + (prev["rfree"] if mc == 0 else []))
            r_toks[mc] = tok_r
            bank_done(ib, [("s_S", tok_r)])
        slots_release(slots, [("s_T", lastmm)])
        tok_sca = None
        for j in range(JC):
            tok_sca = vi(lambda e, j=j, tbeta=tbeta: e.tensor_tensor_scan(
                prod_t[:, j, :], ew_bcast(j, TW), tbeta[:, j, :], 0.0,
                ALU.mult, ALU.add),
                (([("s_V", beta_toks[-1]), ("s_S", tok_ew)] + dbg_ktok)
                 if j == 0 else ()))
        tok_za = vi(lambda e: e.tensor_copy(t_za[:, :], prod_t[:, :, TW - 1:TW]),
                    waits=[("s_V", tok_sca)])
        for j in range(JC):
            tok_sca = vi(lambda e, j=j: e.tensor_tensor_scan(
                prod_t[:, j, :], ew_bcast(j, TW), ek1_t[:, j, :], 0.0,
                ALU.mult, ALU.add),
                waits=[("s_S", ek_toks[-1]), ("s_V", tok_za)] if j == 0 else ())
        tok_zb = vi(lambda e: e.tensor_copy(t_zb[:, :], prod_t[:, :, TW - 1:TW]),
                    waits=[("s_V", tok_sca)])
        ro, rn = t_run[pp], t_run[pp + 1]
        a_o, b_o, m_o = ro[:, 0:JC], ro[:, JC:2 * JC], ro[:, 2 * JC:3 * JC]
        a_n, b_n, m_n = rn[:, 0:JC], rn[:, JC:2 * JC], rn[:, 2 * JC:3 * JC]
        vd(lambda e: e.tensor_tensor(out=t_cj1[:, :], in0=m_o, in1=t_tww[:, :],
                                     op=ALU.add), waits=[("s_V", tok_zb)])
        vd(lambda e: e.tensor_tensor(out=m_n, in0=t_cj1[:, :], in1=t_mk[pp][:, :],
                                     op=ALU.max))
        tok_v = vd(lambda e: e.tensor_tensor(out=t_cj1[:, :], in0=t_cj1[:, :],
                                             in1=m_n, op=ALU.subtract))
        sd(lambda e: e.activation(out=t_cj1[:, :], in_=t_cj1[:, :], func=AF.Exp,
                                  bias=0.0), waits=[("s_V", tok_v)])
        tok_v = vd(lambda e: e.tensor_tensor(out=t_cj2[:, :], in0=t_mk[pp][:, :],
                                             in1=m_n, op=ALU.subtract))
        tok_s = sd(lambda e: e.activation(out=t_cj2[:, :], in_=t_cj2[:, :],
                                          func=AF.Exp, bias=0.0),
                   waits=[("s_V", tok_v)])
        vd(lambda e: e.tensor_tensor(out=a_n, in0=a_o, in1=t_cj1[:, :], op=ALU.mult),
           waits=[("s_S", tok_s)])
        vd(lambda e: e.tensor_tensor(out=t_cj3[:, :], in0=t_za[:, :],
                                     in1=t_cj2[:, :], op=ALU.mult))
        vd(lambda e: e.tensor_tensor(out=a_n, in0=a_n, in1=t_cj3[:, :], op=ALU.add))
        vd(lambda e: e.tensor_tensor(out=b_n, in0=b_o, in1=t_cj1[:, :], op=ALU.mult))
        vd(lambda e: e.tensor_tensor(out=t_cj3[:, :], in0=t_zb[:, :],
                                     in1=t_cj2[:, :], op=ALU.mult))
        tok_run = vd(lambda e: e.tensor_tensor(out=b_n, in0=b_n, in1=t_cj3[:, :],
                                               op=ALU.add))
        if pp == NP - 1 and not debug:
            prev["ekfree"] = []
            prev["betafree"][par] = []
            prev["xfree"][par] = []
            prev["rfree"] = []
        else:
            tok_sp1 = p.dma("gpsimd", sp_ek[pp], ek1_t,
                            waits=[("s_S", ek_toks[-1]), ("s_V", tok_zb)])
            tok_sp2 = p.dma("gpsimd", sp_beta[pp], tbeta,
                            waits=[("s_V", tok_za)])
            tok_sp3 = p.dma("gpsimd", sp_r[pp], r1_t,
                            waits=[("s_S", r_toks[-1])])
            prev["ekfree"] = [tok_sp1]
            prev["betafree"][par] = [tok_sp2]
            prev["xfree"][par] = [tok_sp2]
            prev["rfree"] = [tok_sp3]
        prev["mixfree"] = ([("s_V", tok_mix), ("s_T", lastmm)]
                           + list(prev["ekfree"]))
        return tok_run

    xpre = {0: p.dma("sync", x_t[0], xin[0], waits=[])}
    emit_lora_tasks()
    emit_cast_tasks("wk")
    emit_cast_tasks("wv")
    emit_cast_tasks("wr")
    row_ln(t_xprev[:, :], t_rowa[:, :], waits=[tok_params])
    row_ln(t_rowa[:, :], t_rowa[:, :], g=PV["ln1g"][:, :], b=PV["ln1b"][:, :])
    vi(lambda e: e.tensor_scalar(out=t_hcol[:, :], in0=t_rowa[:, :],
                                 scalar1=t_pvalid[:, :], scalar2=None, op0=ALU.mult))

    after_pass = {0: ["wo"]}
    task_qv.extend(quant_scale_pieces())
    for pp in range(NP):
        tok_run_last = phase1_pass(pp)
        for nm in after_pass.get(pp, []):
            emit_cast_tasks(nm)
        drain_vtasks(16)
    task_q.extend(cast_pieces("fr"))
    task_q.extend(cast_pieces("fk"))
    task_q.extend(cast_pieces("fv"))
    task_q.extend(cast_pieces("wfc", quant=True))
    task_q.extend(cast_pieces("wproj", quant=True))

    tok_pack = p.dma("gpsimd", cc_sum_in.ap(), t_run[NP][:, :],
                     waits=[("s_V", tok_run_last)])
    tok_cc = p.emit("gpsimd", lambda e: e.collective_compute(
        "AllGather", ALU.bypass, replica_groups=[list(range(NCORES))],
        ins=[cc_sum_in.ap().opt()], outs=[cc_sum_out.ap().opt()]),
        inc="s_cc", amt=1, waits=[tok_pack])
    while task_qv:
        task_qv.pop(0)()
    drain_tasks(16)
    gtoks = []
    for r in range(NCORES):
        gtoks.append(p.dma("gpsimd", t_gath[:, r, :], cc_sum_out.ap()[r],
                           waits=[("s_cc", tok_cc)] if r == 0 else ()))
    vd(lambda e: e.memset(t_acc[:, 0:2 * JC], 0.0))
    tok_acc = vd(lambda e: e.memset(t_acc[:, 2 * JC:3 * JC], NEG_BIG))
    aa, ab, am = t_acc[:, 0:JC], t_acc[:, JC:2 * JC], t_acc[:, 2 * JC:3 * JC]
    for r in range(NCORES):
        g_a = t_gath[:, r, 0:JC]
        g_b = t_gath[:, r, JC:2 * JC]
        g_m = t_gath[:, r, 2 * JC:3 * JC]
        vd(lambda e, g_m=g_m, r=r: e.tensor_scalar(
            out=t_cj1[:, :], in0=g_m, scalar1=t_mpre[:, r:r + 1],
            scalar2=t_mbig[:, r:r + 1], op0=ALU.mult, op1=ALU.add),
            waits=[gtoks[r]] + ([("s_V", tok_acc)] if r == 0 else []))
        vd(lambda e, r=r: e.tensor_scalar(out=t_cj2[:, :], in0=t_spanw[:, :],
                                          scalar1=t_mpre[:, r:r + 1], scalar2=None,
                                          op0=ALU.mult))
        vd(lambda e: e.tensor_tensor(out=t_cj2[:, :], in0=t_cj2[:, :], in1=am,
                                     op=ALU.add))
        vd(lambda e: e.tensor_tensor(out=am, in0=t_cj2[:, :], in1=t_cj1[:, :],
                                     op=ALU.max))
        tok_v = vd(lambda e: e.tensor_tensor(out=t_cj2[:, :], in0=t_cj2[:, :], in1=am,
                                             op=ALU.subtract))
        sd(lambda e: e.activation(out=t_cj2[:, :], in_=t_cj2[:, :], func=AF.Exp,
                                  bias=0.0), waits=[("s_V", tok_v)])
        tok_v = vd(lambda e: e.tensor_tensor(out=t_cj1[:, :], in0=t_cj1[:, :], in1=am,
                                             op=ALU.subtract))
        tok_s = sd(lambda e: e.activation(out=t_cj1[:, :], in_=t_cj1[:, :],
                                          func=AF.Exp, bias=0.0),
                   waits=[("s_V", tok_v)])
        vd(lambda e: e.tensor_tensor(out=aa, in0=aa, in1=t_cj2[:, :], op=ALU.mult),
           waits=[("s_S", tok_s)])
        vd(lambda e, g_a=g_a: e.tensor_tensor(out=t_cj3[:, :], in0=g_a,
                                              in1=t_cj1[:, :], op=ALU.mult))
        vd(lambda e: e.tensor_tensor(out=aa, in0=aa, in1=t_cj3[:, :], op=ALU.add))
        vd(lambda e: e.tensor_tensor(out=ab, in0=ab, in1=t_cj2[:, :], op=ALU.mult))
        vd(lambda e, g_b=g_b: e.tensor_tensor(out=t_cj3[:, :], in0=g_b,
                                              in1=t_cj1[:, :], op=ALU.mult))
        tok_acc = vd(lambda e: e.tensor_tensor(out=ab, in0=ab, in1=t_cj3[:, :],
                                               op=ALU.add))
    if debug:
        p.dma("gpsimd", sp_acc, t_acc[:, :], waits=[("s_V", tok_acc)])
        p.dma("gpsimd", dbg_sum, t_gath[:, :, :], waits=[("s_V", tok_acc)])
    tok_inc = [None] * NP
    for pp in range(NP):
        ro = t_run[pp]
        a_o2, b_o2, m_o2 = ro[:, 0:JC], ro[:, JC:2 * JC], ro[:, 2 * JC:3 * JC]
        vd(lambda e, pp=pp: e.tensor_scalar(out=t_cj1[:, :], in0=t_tww[:, :],
                                            scalar1=float(pp), scalar2=None,
                                            op0=ALU.mult), waits=[("s_V", tok_acc)])
        vd(lambda e: e.tensor_tensor(out=t_cj1[:, :], in0=t_cj1[:, :], in1=am,
                                     op=ALU.add))
        vd(lambda e, m_o2=m_o2: e.tensor_tensor(out=t_cj3[:, :], in0=t_cj1[:, :],
                                                in1=m_o2, op=ALU.max))
        tok_v = vd(lambda e: e.tensor_tensor(out=t_cj1[:, :], in0=t_cj1[:, :],
                                             in1=t_cj3[:, :], op=ALU.subtract))
        sd(lambda e: e.activation(out=t_cj1[:, :], in_=t_cj1[:, :], func=AF.Exp,
                                  bias=0.0), waits=[("s_V", tok_v)])
        tok_v = vd(lambda e, m_o2=m_o2: e.tensor_tensor(out=t_cj2[:, :], in0=m_o2,
                                                        in1=t_cj3[:, :],
                                                        op=ALU.subtract))
        tok_s = sd(lambda e: e.activation(out=t_cj2[:, :], in_=t_cj2[:, :],
                                          func=AF.Exp, bias=0.0),
                   waits=[("s_V", tok_v)])
        ia = t_inc[pp][:, 0:JC]
        ibb = t_inc[pp][:, JC:2 * JC]
        vd(lambda e, ia=ia: e.tensor_tensor(out=ia, in0=aa, in1=t_cj1[:, :],
                                            op=ALU.mult), waits=[("s_S", tok_s)])
        vd(lambda e, ia=ia, a_o2=a_o2: e.tensor_tensor(out=t_cj3[:, :], in0=a_o2,
                                                       in1=t_cj2[:, :], op=ALU.mult))
        vd(lambda e, ia=ia: e.tensor_tensor(out=ia, in0=ia, in1=t_cj3[:, :],
                                            op=ALU.add))
        vd(lambda e, ibb=ibb: e.tensor_tensor(out=ibb, in0=ab, in1=t_cj1[:, :],
                                              op=ALU.mult))
        vd(lambda e, ibb=ibb, b_o2=b_o2: e.tensor_tensor(out=t_cj3[:, :], in0=b_o2,
                                                         in1=t_cj2[:, :],
                                                         op=ALU.mult))
        vd(lambda e, ibb=ibb: e.tensor_tensor(out=ibb, in0=ibb, in1=t_cj3[:, :],
                                              op=ALU.add))
        vd(lambda e, pp=pp: e.tensor_scalar(out=t_cj1[:, :], in0=t_tww[:, :],
                                            scalar1=float(pp), scalar2=None,
                                            op0=ALU.mult))
        vd(lambda e: e.tensor_tensor(out=t_cj1[:, :], in0=t_cj1[:, :], in1=am,
                                     op=ALU.add))
        vd(lambda e, m_o2=m_o2: e.tensor_tensor(out=t_cj1[:, :], in0=t_cj1[:, :],
                                                in1=m_o2, op=ALU.max))
        tok_v = vd(lambda e, pp=pp: e.tensor_tensor(out=t_cj1[:, :], in0=t_cj1[:, :],
                                                    in1=t_mk[pp][:, :],
                                                    op=ALU.subtract))
        tok_s = sd(lambda e: e.activation(out=t_cj1[:, :], in_=t_cj1[:, :],
                                          func=AF.Exp, bias=0.0),
                   waits=[("s_V", tok_v)])
        vd(lambda e, ia=ia: e.tensor_tensor(out=ia, in0=ia, in1=t_cj1[:, :],
                                            op=ALU.mult), waits=[("s_S", tok_s)])
        tok_inc[pp] = vd(lambda e, ibb=ibb: e.tensor_tensor(
            out=ibb, in0=ibb, in1=t_cj1[:, :], op=ALU.mult))
        if debug:
            p.dma("gpsimd", sp_inc[pp], t_inc[pp][:, :],
                  waits=[("s_V", tok_inc[pp])])

    srA_t = AR.view(rE, [JC, TW])
    xinA_t = AR.view(rB, [JC, TW])
    ekA_t = AR.view(rC, [JC, TW])
    Aarr = AR.view(rD, [JC, TW])
    rwkv_t = ABF.view(bR1, [JC, TW])
    par_tok = {0: list(prev["betafree"][0]), 1: list(prev["betafree"][1])}
    stA = {"ek": prev["ekfree"], "r": prev["rfree"], "xin": prev["xinfree"],
           "arrD": [], "rwkv": []}
    last_pass = [None]

    def stageA_front(pp, resident):
        par = pp % 2
        betaA_t = AR.view(rA0 if par == 0 else rA1, [JC, TW])
        Barr = AR.view(rA1 if par == 0 else rA0, [JC, TW])
        if resident:
            tok_l1 = tok_l2 = tok_l3 = None
            rel = []
        else:
            tok_l2 = p.dma("gpsimd", betaA_t, sp_beta[pp], waits=par_tok[par])
            tok_l1 = p.dma("gpsimd", ekA_t, sp_ek[pp], waits=stA["ek"])
            tok_l3 = p.dma("gpsimd", srA_t, sp_r[pp], waits=stA["r"])
            rel = [tok_l3]
        tok_sa = None
        for j2 in range(JC):
            w0 = []
            if j2 == 0:
                w0 = [("s_V", tok_inc[pp])] + stA["arrD"]
                if tok_l2 is not None:
                    w0.append(tok_l2)
            tok_sa = vi(lambda e, j=j2, pp=pp, betaA_t=betaA_t: e.tensor_tensor_scan(
                Aarr[:, j, :], ew_bcast(j, TW), betaA_t[:, j, :],
                t_inc[pp][:, j:j + 1], ALU.mult, ALU.add), waits=w0)
        for j2 in range(JC):
            w0 = list(par_tok[1 - par]) if j2 == 0 else []
            if j2 == 0 and tok_l1 is not None:
                w0.append(tok_l1)
            tok_sa = vi(lambda e, j=j2, pp=pp, Barr=Barr: e.tensor_tensor_scan(
                Barr[:, j, :], ew_bcast(j, TW), ekA_t[:, j, :],
                t_inc[pp][:, JC + j:JC + j + 1], ALU.mult, ALU.add), waits=w0)
        for j2 in range(JC):
            vi(lambda e, j=j2, betaA_t=betaA_t: e.scalar_tensor_tensor(
                out=betaA_t[:, j, 1:TW], in0=betaA_t[:, j, 1:TW],
                scalar=t_eu[:, j:j + 1], in1=Aarr[:, j, 0:TW - 1],
                op0=ALU.mult, op1=ALU.add),
                waits=[("s_V", tok_sa), ("s_S", tok_eu)] if j2 == 0 else ())
            vi(lambda e, j=j2, pp=pp, betaA_t=betaA_t: e.scalar_tensor_tensor(
                out=betaA_t[:, j, 0:1], in0=betaA_t[:, j, 0:1],
                scalar=t_eu[:, j:j + 1], in1=t_inc[pp][:, j:j + 1],
                op0=ALU.mult, op1=ALU.add))
        tok_nd = None
        for j2 in range(JC):
            vi(lambda e, j=j2, Barr=Barr: e.scalar_tensor_tensor(
                out=ekA_t[:, j, 1:TW], in0=ekA_t[:, j, 1:TW],
                scalar=t_eu[:, j:j + 1], in1=Barr[:, j, 0:TW - 1],
                op0=ALU.mult, op1=ALU.add))
            tok_nd = vi(lambda e, j=j2, pp=pp: e.scalar_tensor_tensor(
                out=ekA_t[:, j, 0:1], in0=ekA_t[:, j, 0:1],
                scalar=t_eu[:, j:j + 1], in1=t_inc[pp][:, JC + j:JC + j + 1],
                op0=ALU.mult, op1=ALU.add))
        par_tok[1 - par] = [("s_V", tok_nd)]
        stA["arrD"] = [("s_V", tok_nd)]
        ek_flat = ekA_t.rearrange("p a b -> p (a b)")
        tok_rc = vi(lambda e: e.reciprocal(out=ek_flat, in_=ek_flat),
                    waits=[("s_V", tok_nd)])
        tok_y = vi(lambda e, betaA_t=betaA_t: e.tensor_tensor(
            out=betaA_t.rearrange("p a b -> p (a b)"),
            in0=betaA_t.rearrange("p a b -> p (a b)"), in1=ek_flat, op=ALU.mult))
        stA["ek"] = [("s_V", tok_y)]
        drain_tasks(24)
        return dict(pp=pp, par=par, betaA_t=betaA_t, sr_rel=rel)

    def stageA_back(ctx, front_next):
        pp, par, betaA_t = ctx["pp"], ctx["par"], ctx["betaA_t"]
        tok_rwkv = vi(lambda e, betaA_t=betaA_t: e.tensor_tensor(
            out=rwkv_t.rearrange("p a b -> p (a b)"),
            in0=betaA_t.rearrange("p a b -> p (a b)"),
            in1=srA_t.rearrange("p a b -> p (a b)"), op=ALU.mult),
            waits=list(ctx["sr_rel"]) + stA["rwkv"])
        par_tok[par] = [("s_V", tok_rwkv)]
        stA["r"] = [("s_V", tok_rwkv)]
        views, vtoks = wo_res["views"], wo_res["vtoks"]
        mminfo = []
        lastmm = None
        for mc in range(JC):
            ib, bnk, wts = get_bank()
            for kc in range(JC):
                lastmm = ti(lambda e, bnk=bnk, v=views[kc], mc=mc, kc=kc: e.matmul(
                    bnk[:, 0:TW], v[:, mc * 128:(mc + 1) * 128], rwkv_t[:, kc, :],
                    start=(kc == 0), stop=(kc == JC - 1)),
                    waits=(wts + [vtoks[kc], ("s_V", tok_rwkv)]) if kc == 0
                    else ([vtoks[kc]] if vtoks[kc] is not vtoks[kc - 1] else ()))
            mminfo.append((ib, bnk, lastmm))
        stA["rwkv"] = [("s_T", lastmm)]
        wo_res["last"] = lastmm
        ctx_next = front_next() if front_next is not None else None
        tok_x1 = None
        for mc, (ib, bnk, mm) in enumerate(mminfo):
            tok_x1 = vi(lambda e, bnk=bnk, mc=mc: e.tensor_tensor(
                out=xinA_t[:, mc, :], in0=xinA_t[:, mc, :], in1=bnk[:, 0:TW],
                op=ALU.add),
                waits=[("s_T", mm)] + (stA["xinload"] if mc == 0 else []))
            bank_done(ib, [("s_V", tok_x1)])
            drain_tasks(5)
        tok_col = vi(lambda e, pp=pp: e.tensor_copy(t_x1c[pp][:, :],
                                                    xinA_t[:, :, TW - 1:TW]),
                     waits=[("s_V", tok_x1)])
        tok_spx1 = p.dma("scalar", sp_x1[pp], xinA_t, waits=[("s_V", tok_col)])
        stA["xin"] = [tok_spx1]
        return tok_spx1, tok_col, ctx_next

    orderA = [NP - 1] + list(range(NP - 1))
    tok_sp_x1 = {}
    stA["xinload"] = []
    wo_res = {"views": [], "vtoks": [], "last": None, "slots": []}
    _kc = 0
    _slot = 0
    while _kc < JC:
        _kh = min(JC, _kc + WR_SLOT // C)
        wts = list(wslot_tok[_slot]) + [wcast_done["wo"]]
        wslot_tok[_slot] = []
        _v = wslot[_slot][:, 0:(_kh - _kc) * C].rearrange(
            "p (a c) -> p a c", a=_kh - _kc, c=C)
        _tok = p.dma("sync", wslot[_slot][:, 0:(_kh - _kc) * C],
                     sc_w["wo"][:, _kc * C:_kh * C], waits=wts)
        for _i in range(_kh - _kc):
            wo_res["views"].append(_v[:, _i, :])
            wo_res["vtoks"].append(_tok)
        wo_res["slots"].append(_slot)
        _slot += 1
        _kc = _kh
    first_res = not debug
    ctxA = stageA_front(orderA[0], resident=first_res)
    if not first_res:
        stA["xinload"] = [p.dma("scalar", xinA_t, sp_xin[orderA[0]],
                                waits=stA["xin"])]
    for idx, pp in enumerate(orderA):
        nxt = orderA[idx + 1] if idx + 1 < NP else None
        fe = (lambda nxt=nxt: stageA_front(nxt, resident=False)) \
            if nxt is not None else None
        tok_spx1, tok_col, ctxA_next = stageA_back(ctxA, fe)
        tok_sp_x1[pp] = tok_spx1
        if nxt is not None:
            stA["xinload"] = [p.dma("scalar", xinA_t, sp_xin[nxt],
                                    waits=[tok_spx1])]
            ctxA = ctxA_next
        if nxt is None:
            slots_release(wo_res["slots"], [("s_T", wo_res["last"])])
            ring_rr[0] = 0
        if pp == NP - 1:
            tok_pub = p.dma("gpsimd", cc_row_in.ap(), t_x1c[NP - 1][:, :],
                            waits=[("s_V", tok_col)])
            tok_cc2 = p.emit("gpsimd", lambda e: e.collective_compute(
                "AllGather", ALU.bypass, replica_groups=[list(range(NCORES))],
                ins=[cc_row_in.ap().opt()], outs=[cc_row_out.ap().opt()]),
                inc="s_cc", amt=1, waits=[tok_pub])
            rgtoks = []
            for r in range(NCORES):
                rgtoks.append(p.dma("scalar", t_rgath[:, r, :], cc_row_out.ap()[r],
                              waits=[("s_cc", tok_cc2)] if r == 0 else ()))
        drain_tasks(6)
    drain_all_tasks()
    tok_sel = vd(lambda e: e.tensor_scalar(out=t_rowa[:, :], in0=t_rgath[:, 0, :],
                                           scalar1=t_mrow[:, 0:1], scalar2=None,
                                           op0=ALU.mult), waits=[rgtoks[0]])
    for r in range(1, NCORES):
        tok_sel = vd(lambda e, r=r: e.scalar_tensor_tensor(
            out=t_rowa[:, :], in0=t_rgath[:, r, :], scalar=t_mrow[:, r:r + 1],
            in1=t_rowa[:, :], op0=ALU.mult, op1=ALU.add), waits=[rgtoks[r]])
    row_ln(t_rowa[:, :], t_rowa[:, :], g=PV["ln2g"][:, :], b=PV["ln2b"][:, :],
           waits=[("s_V", tok_sel)])
    tok_gcol0 = vd(lambda e: e.tensor_scalar(out=t_gcol[:, :], in0=t_rowa[:, :],
                                             scalar1=t_pvalid[:, :], scalar2=None,
                                             op0=ALU.mult))

    x1B_par = [AR.view(rA0, [JC, TW]), AR.view(rA1, [JC, TW])]
    gB_t = AR.view(rB, [JC, TW])
    d2B_t = AR.view(rD, [JC, TW])
    sigB_t = AR.view(rC, [JC, TW])
    xB_t = AR.view(rE, [JC, TW])
    outB_t = AR.view(rE, [JC, TW])
    relu_st = AR.view(rS3, [TW])
    ffn_st = AR.view(rS3, [TW])
    gk_t = ABF.view(bR1, [JC, TW])
    gr_t = ABF.view(bR1 + TWJC, [JC, TW])
    kf_t = ABF.view(bR2, [JF, TW])
    h2_t = ABF.view(bR2, [JF, TW])
    stg3a = ABF.view(bR2, [JC, TW])
    stg3b = ABF.view(bR2 + TWJC, [JC, TW])
    xf_t = ABF.view(bR5, [JC, TW])
    xa_t = ABF.view(bR4, [TW])
    xa2_t = ABF.view(bR4 + TW, [TW])

    stB = {"x1p0": [], "x1p1": [], "g": [], "d2": [], "sig": [], "x": [],
           "out": [], "gk": [], "kf": [], "xf": [], "bfc": [], "bproj": [],
           "stage": [], "mfree": []}

    def stageB_pass(pp):
        x1B_t = x1B_par[pp % 2]
        gcol = t_gcol if pp == 0 else t_gc[pp - 1]
        gcol_tok = [("s_V", tok_gcol0)] if pp == 0 else stB[f"gcol{pp - 1}"]
        tok_lx1 = p.dma("gpsimd", x1B_t, sp_x1[pp],
                        waits=[tok_sp_x1[pp]] + stB[f"x1p{pp % 2}"])
        tok_rstd, _ = ln_stats(x1B_t, stg3a, stg3b, waits=[tok_lx1],
                               stage_free=stB["stage"], m_free=stB["mfree"])
        tok_g = ln_apply(x1B_t, gB_t, tok_rstd, g=PV["ln2g"], b=PV["ln2b"],
                         waits=stB["g"])
        tok_gc = vi(lambda e, pp=pp: e.tensor_copy(t_gc[pp][:, :],
                                                   gB_t[:, :, TW - 1:TW]),
                    waits=[("s_V", tok_g)])
        stB[f"gcol{pp}"] = [("s_V", tok_gc)]
        if debug:
            p.dma("gpsimd", sp_g[pp], gB_t, waits=[("s_V", tok_gc)])
        for j in range(JC):
            vi(lambda e, j=j: e.tensor_tensor(
                out=d2B_t[:, j, 1:TW], in0=gB_t[:, j, 1:TW], in1=gB_t[:, j, 0:TW - 1],
                op=ALU.subtract), waits=stB["d2"] if j == 0 else ())
            vd(lambda e, j=j, gcol=gcol: e.tensor_tensor(
                out=d2B_t[:, j, 0:1], in0=gB_t[:, j, 0:1], in1=gcol[:, j:j + 1],
                op=ALU.subtract), waits=gcol_tok if j == 0 else ())
        tok_mix = None
        for fm, xo in ((PV["fmk"], gk_t), (PV["fmr"], gr_t)):
            for j in range(JC):
                vi(lambda e, j=j, fm=fm, xo=xo: e.scalar_tensor_tensor(
                    out=xo[:, j, 1:TW], in0=d2B_t[:, j, 1:TW], scalar=fm[:, j:j + 1],
                    in1=gB_t[:, j, 0:TW - 1], op0=ALU.mult, op1=ALU.add),
                    waits=stB["gk"] if (j == 0 and xo is gk_t) else ())
                tok_mix = vd(lambda e, j=j, fm=fm, xo=xo, gcol=gcol:
                             e.scalar_tensor_tensor(
                    out=xo[:, j, 0:1], in0=d2B_t[:, j, 0:1], scalar=fm[:, j:j + 1],
                    in1=gcol[:, j:j + 1], op0=ALU.mult, op1=ALU.add))
        views, vtoks, slots = stream_chunks("fr", 0, JC)
        sig_toks = [None] * JC
        lastmm = None
        for mc in range(JC):
            ib, bnk, wts = get_bank()
            for kc in range(JC):
                lastmm = ti(lambda e, bnk=bnk, v=views[kc], mc=mc, kc=kc: e.matmul(
                    bnk[:, 0:TW], v[:, mc * 128:(mc + 1) * 128], gr_t[:, kc, :],
                    start=(kc == 0), stop=(kc == JC - 1)),
                    waits=(wts + [vtoks[kc], ("s_V", tok_mix)]) if kc == 0
                    else ([vtoks[kc]] if vtoks[kc] is not vtoks[kc - 1] else ()))
            tok_sg = si(lambda e, bnk=bnk, mc=mc: e.activation(
                out=sigB_t[:, mc, :], in_=bnk[:, 0:TW], func=AF.Sigmoid, bias=0.0),
                waits=[("s_T", lastmm)] + (stB["sig"] if mc == 0 else []))
            sig_toks[mc] = tok_sg
            bank_done(ib, [("s_S", tok_sg)])
        slots_release(slots, [("s_T", lastmm)])
        kf_toks = []
        NG = JF // 8
        for g8 in range(NG):
            views, vtoks, slots = stream_cols("fk", g8)
            for jfo in range(8):
                jf = g8 * 8 + jfo
                ib, bnk, wts = get_bank()
                for kc in range(JC):
                    lastmm = ti(lambda e, bnk=bnk, v=views[kc], jfo=jfo, kc=kc:
                                e.matmul(
                        bnk[:, 0:TW], v[:, jfo * 128:(jfo + 1) * 128],
                        gk_t[:, kc, :], start=(kc == 0), stop=(kc == JC - 1)),
                        waits=(wts + [vtoks[kc]]) if (kc == 0 and jfo == 0)
                        else ([vtoks[kc]] if jfo == 0 else ()))
                tok_rl = si(lambda e, bnk=bnk: e.activation(
                    out=relu_st[:, :], in_=bnk[:, 0:TW], func=AF.Relu, bias=0.0),
                    waits=[("s_T", lastmm)] + (stB["kf"] if jf == 0 else []))
                tok_kf = si(lambda e, jf=jf: e.activation(
                    out=kf_t[:, jf, :], in_=relu_st[:, :], func=AF.Square, bias=0.0))
                kf_toks.append(tok_kf)
                bank_done(ib, [("s_S", tok_kf)])
            slots_release(slots, [("s_T", lastmm)])
        tok_after_fk = ("s_T", p.counts["s_T"])
        banks, bwaits = get_all_banks()
        x2_toks = [None] * JC
        for g8 in range(NG):
            views, vtoks, slots = stream_chunks("fv", g8 * 8, g8 * 8 + 8)
            for kci in range(8):
                kc = g8 * 8 + kci
                for mc in range(JC):
                    w8 = []
                    if g8 == 0 and kci == 0 and mc == 0:
                        w8 = bwaits + [vtoks[kci], ("s_S", kf_toks[-1])]
                    elif mc == 0 and (kci == 0 or vtoks[kci] is not vtoks[kci - 1]):
                        w8 = [vtoks[kci]]
                    lastmm = ti(lambda e, v=views[kci], mc=mc, kc=kc: e.matmul(
                        pbank[mc][:, 0:TW], v[:, mc * 128:(mc + 1) * 128],
                        kf_t[:, kc, :], start=(kc == 0), stop=(kc == JF - 1)),
                        waits=w8)
            slots_release(slots, [("s_T", lastmm)])
        for mc in range(JC):
            vi(lambda e, mc=mc: e.tensor_tensor(
                out=sigB_t[:, mc, :], in0=sigB_t[:, mc, :], in1=pbank[mc][:, 0:TW],
                op=ALU.mult), waits=[("s_T", lastmm)] if mc == 0 else ())
            tok_x2 = vi(lambda e, mc=mc: e.tensor_tensor(
                out=x1B_t[:, mc, :], in0=x1B_t[:, mc, :], in1=sigB_t[:, mc, :],
                op=ALU.add))
            x2_toks[mc] = tok_x2
            bank_done(mc, [("s_V", tok_x2)])
        tok_lx = p.dma("gpsimd", xB_t, xin[pp], waits=stB["x"])
        tok_xa = vi(lambda e: e.tensor_tensor(
            out=x1B_t.rearrange("p a b -> p (a b)"),
            in0=x1B_t.rearrange("p a b -> p (a b)"),
            in1=xB_t.rearrange("p a b -> p (a b)"), op=ALU.add),
            waits=[tok_lx, ("s_V", x2_toks[-1])])
        if debug:
            p.dma("gpsimd", sp_xaf[pp], x1B_t, waits=[("s_V", tok_xa)])
            p.dma("gpsimd", sp_sig[pp], sigB_t, waits=[("s_V", x2_toks[-1])])
        tok_rstd, tok_stg3 = ln_stats(x1B_t, stg3a, stg3b,
                                      waits=[("s_V", tok_xa),
                                             ("s_S", kf_toks[-1])])
        tok_xf = ln_apply(x1B_t, xf_t, tok_rstd, waits=stB["xf"], scratch=d2B_t)
        ib, bnk, wts = get_bank()
        for kc in range(JC):
            lastmm = ti(lambda e, bnk=bnk, kc=kc: e.matmul(
                bnk[0:LORA, 0:TW], afcb[:, kc, :], xf_t[:, kc, :],
                start=(kc == 0), stop=(kc == JC - 1)),
                waits=(wts + [("s_V", tok_xf)]) if kc == 0 else ())
        tok_xa1 = si(lambda e, bnk=bnk: e.activation(
            out=xa_t[0:LORA, :], in_=bnk[0:LORA, 0:TW], func=AF.Identity,
            bias=0.0, scale=2.0), waits=[("s_T", lastmm)] + stB["bfc"])
        bank_done(ib, [("s_S", tok_xa1)])
        tok_bfc = p.dma("sync", bfcb[:, :], sc_bfc,
                        waits=[wcast_done["bfc"]] + stB["bfc"])
        h2_toks = []
        for g8 in range(NG):
            views, vtoks, slots = stream_cols("wfc", g8)
            for jfo in range(8):
                jf = g8 * 8 + jfo
                ib, bnk, wts = get_bank()
                for kc in range(JC):
                    ti(lambda e, bnk=bnk, v=views[kc], jfo=jfo, kc=kc: e.matmul(
                        bnk[:, 0:TW], v[:, jfo * 128:(jfo + 1) * 128],
                        xf_t[:, kc, :], start=(kc == 0), stop=False),
                        waits=(wts + [vtoks[kc]]) if (kc == 0 and jfo == 0)
                        else ([vtoks[kc]] if jfo == 0 else ()))
                lastmm = ti(lambda e, bnk=bnk, jf=jf: e.matmul(
                    bnk[:, 0:TW], bfcb[:, jf * 128:(jf + 1) * 128], xa_t[0:LORA, :],
                    start=False, stop=True),
                    waits=[tok_bfc, ("s_S", tok_xa1)]
                    if (g8 == 0 and jfo == 0) else ())
                tok_h2 = si(lambda e, bnk=bnk, jf=jf: e.activation(
                    out=h2_t[:, jf, :], in_=bnk[:, 0:TW], func=AF.Gelu,
                    bias=t_bfcv[:, jf:jf + 1], scale=1.0),
                    waits=[("s_T", lastmm)])
                h2_toks.append(tok_h2)
                bank_done(ib, [("s_S", tok_h2)])
            slots_release(slots, [("s_T", lastmm)])
        tok_after_wfc = ("s_T", p.counts["s_T"])
        if debug:
            p.dma("gpsimd", sp_xf[pp], xf_t, waits=[("s_V", tok_xf)])
            p.dma("gpsimd", sp_h2[pp], h2_t, waits=[("s_S", h2_toks[-1])])
        ib, bnk, wts = get_bank()
        for kc in range(JF):
            lastmm = ti(lambda e, bnk=bnk, kc=kc: e.matmul(
                bnk[0:LORA, 0:TW], aprojb[:, kc, :], h2_t[:, kc, :],
                start=(kc == 0), stop=(kc == JF - 1)),
                waits=(wts + [("s_S", h2_toks[-1])]) if kc == 0 else ())
        tok_xa2 = si(lambda e, bnk=bnk: e.activation(
            out=xa2_t[0:LORA, :], in_=bnk[0:LORA, 0:TW], func=AF.Identity,
            bias=0.0, scale=2.0), waits=[("s_T", lastmm)])
        bank_done(ib, [("s_S", tok_xa2)])
        tok_bproj = p.dma("sync", bprojb[:, :], sc_bproj,
                          waits=[wcast_done["bproj"]] + stB["bproj"])
        banks, bwaits = get_all_banks()
        for g8 in range(NG):
            views, vtoks, slots = stream_chunks("wproj", g8 * 8, g8 * 8 + 8)
            for kci in range(8):
                kc = g8 * 8 + kci
                for mc in range(JC):
                    w8 = []
                    if g8 == 0 and kci == 0 and mc == 0:
                        w8 = bwaits + [vtoks[kci]]
                    elif mc == 0 and (kci == 0 or vtoks[kci] is not vtoks[kci - 1]):
                        w8 = [vtoks[kci]]
                    lastmm = ti(lambda e, v=views[kci], mc=mc, kc=kc: e.matmul(
                        pbank[mc][:, 0:TW], v[:, mc * 128:(mc + 1) * 128],
                        h2_t[:, kc, :], start=(kc == 0), stop=False),
                        waits=w8)
            slots_release(slots, [("s_T", lastmm)])
        for mc in range(JC):
            lastmm = ti(lambda e, mc=mc: e.matmul(
                pbank[mc][:, 0:TW], bprojb[:, mc * 128:(mc + 1) * 128],
                xa2_t[0:LORA, :], start=False, stop=True),
                waits=[tok_bproj, ("s_S", tok_xa2)] if mc == 0 else ())
        tok_out = None
        prev_out = [None, None]
        fst = [ffn_st, AR.view(rS1, [TW])]
        for mc in range(JC):
            fb = fst[mc % 2]
            w9 = [("s_T", lastmm)] + (stB["out"] if mc == 0 else [])
            if prev_out[mc % 2] is not None:
                w9.append(("s_V", prev_out[mc % 2]))
            tok_f = si(lambda e, mc=mc, fb=fb: e.activation(
                out=fb[:, :], in_=pbank[mc][:, 0:TW], func=AF.Identity,
                bias=t_bprojv[:, mc:mc + 1], scale=1.0), waits=w9)
            tok_out = vi(lambda e, mc=mc, fb=fb: e.tensor_tensor(
                out=outB_t[:, mc, :], in0=x1B_t[:, mc, :], in1=fb[:, :],
                op=ALU.add), waits=[("s_S", tok_f)])
            prev_out[mc % 2] = tok_out
            bank_done(mc, [("s_V", tok_out)])
        tok_o = p.dma("sync", out[pp], outB_t, waits=[("s_V", tok_out)])
        stB[f"x1p{pp % 2}"] = [("s_V", tok_out)]
        stB["g"] = [("s_V", tok_mix)]
        stB["d2"] = [("s_V", tok_xf)]
        stB["sig"] = [("s_V", x2_toks[-1])]
        stB["x"] = [("s_V", tok_xa), tok_o]
        stB["out"] = [tok_o]
        stB["gk"] = [tok_after_fk]
        stB["kf"] = [("s_S", h2_toks[-1] if h2_toks else kf_toks[-1])]
        stB["xf"] = [tok_after_wfc]
        stB["bfc"] = [("s_T", p.counts["s_T"])]
        stB["bproj"] = [("s_T", p.counts["s_T"])]
        stB["stage"] = [tok_stg3]
        stB["mfree"] = [("s_V", tok_xf)]
        return tok_o

    vnow = [("s_V", p.counts["s_V"])]
    snow = [("s_S", p.counts["s_S"])]
    tnow = [("s_T", p.counts["s_T"])]
    stB["x1p0"] = vnow
    stB["x1p1"] = vnow

    stB["g"] = vnow + [tok_sp_x1[pp2] for pp2 in range(NP)]
    stB["d2"] = vnow
    stB["sig"] = vnow + snow
    stB["x"] = vnow
    stB["out"] = vnow
    stB["kf"] = vnow
    stB["gk"] = vnow + tnow

    tok_os = []
    for pp in range(NP):
        tok_os.append(stageB_pass(pp))

    p.emit("sync", lambda e: None, waits=tok_os)

    with nc.Block() as block:
        @block.sync
        def _(eng):
            for ev in p.events["sync"]:
                ev(eng)

        @block.tensor
        def _(eng):
            for ev in p.events["tensor"]:
                ev(eng)

        @block.vector
        def _(eng):
            for ev in p.events["vector"]:
                ev(eng)

        @block.scalar
        def _(eng):
            for ev in p.events["scalar"]:
                ev(eng)

        @block.gpsimd
        def _(eng):
            for ev in p.events["gpsimd"]:
                ev(eng)

    es.close()
    return nc


def _featmaj_vec(v, j):
    return np.ascontiguousarray(np.asarray(v, dtype=np.float32).reshape(j, 128).T)


def _prep_inputs(inputs, cfg):
    C, DF, NP, TW, LORA = cfg["C"], cfg["DF"], cfg["NP"], cfg["TW"], cfg["LORA"]
    JC, JF = C // 128, DF // 128
    TC = NP * TW
    QPS = NCORES // 2
    g = lambda k: np.asarray(inputs[k], dtype=np.float32)

    x = g("L0_attn_x")
    B, T, _ = x.shape
    assert T == QPS * TC and B == 2

    shared = {}
    wT = lambda a: np.asarray(a, np.float32).T

    def pmaj(a, kcn, m):
        return np.ascontiguousarray(
            a.reshape(kcn, 128, m).transpose(1, 0, 2).reshape(128, kcn * m))

    def pmaj_grp(a, kcn, m, gw=1024):
        ng = m // gw
        return np.ascontiguousarray(
            a.reshape(kcn, 128, ng, gw).transpose(1, 2, 0, 3).reshape(
                128, ng * kcn * gw))

    shared["wk"] = pmaj(wT(g("L0_attn_Wk")), JC, C)
    shared["wv"] = pmaj(wT(g("L0_attn_Wv")), JC, C)
    shared["wr"] = pmaj(wT(g("L0_attn_Wr")), JC, C)
    shared["wo"] = pmaj(wT(g("L0_attn_Wo")), JC, C)
    shared["fr"] = pmaj(wT(g("L0_attn_Fr")), JC, C)
    shared["fk"] = pmaj_grp(wT(g("L0_attn_Fk")), JC, DF)
    shared["fv"] = pmaj(wT(g("L0_attn_Fv")), JF, C)
    shared["wfc"] = pmaj_grp(wT(g("L0_ffn_W_fc")), JC, DF)
    shared["wproj"] = pmaj(wT(g("L0_ffn_W_proj")), JF, C)
    shared["afc"] = pmaj(wT(g("L0_ffn_A_fc")), JC, LORA)
    shared["aproj"] = pmaj(wT(g("L0_ffn_A_proj")), JF, LORA)
    shared["bfc"] = np.ascontiguousarray(wT(g("L0_ffn_B_fc")))
    shared["bproj"] = np.ascontiguousarray(wT(g("L0_ffn_B_proj")))
    shared["ln1g"] = _featmaj_vec(g("L0_attn_ln1_g"), JC)
    shared["ln1b"] = _featmaj_vec(g("L0_attn_ln1_b"), JC)
    shared["ln2g"] = _featmaj_vec(g("L0_attn_ln2_g"), JC)
    shared["ln2b"] = _featmaj_vec(g("L0_attn_ln2_b"), JC)
    shared["tmk"] = _featmaj_vec(g("L0_attn_tm_k"), JC)
    shared["tmv"] = _featmaj_vec(g("L0_attn_tm_v"), JC)
    shared["tmr"] = _featmaj_vec(g("L0_attn_tm_r"), JC)
    shared["fmk"] = _featmaj_vec(g("L0_attn_fm_k"), JC)
    shared["fmr"] = _featmaj_vec(g("L0_attn_fm_r"), JC)
    shared["tdecay"] = _featmaj_vec(g("L0_attn_time_decay"), JC)
    shared["tfirst"] = _featmaj_vec(g("L0_attn_time_first"), JC)
    shared["bfcv"] = _featmaj_vec(g("L0_ffn_b_fc"), JF)
    shared["bprojv"] = _featmaj_vec(g("L0_ffn_b_proj"), JC)

    in_maps = []
    for core in range(NCORES):
        s, q = core // QPS, core % QPS
        xs = x[s, q * TC:(q + 1) * TC, :]
        xs = np.ascontiguousarray(
            xs.reshape(NP, TW, JC, 128).transpose(0, 3, 2, 1))
        if q > 0:
            xp = np.ascontiguousarray(x[s, q * TC - 1, :].reshape(JC, 128).T)
            pvalid = np.ones((128, 1), np.float32)
        else:
            xp = np.zeros((128, JC), np.float32)
            pvalid = np.zeros((128, 1), np.float32)
        mpre = np.zeros(NCORES, np.float32)
        for r in range(NCORES):
            if r // QPS == s and r % QPS < q:
                mpre[r] = 1.0
        mrow = np.zeros(NCORES, np.float32)
        if q > 0:
            mrow[core - 1] = 1.0
        m = dict(shared)
        m["xin"] = xs
        m["xprev"] = xp
        m["prevvalid"] = pvalid
        m["maskpre"] = np.ascontiguousarray(np.broadcast_to(mpre, (128, NCORES)))
        m["maskbig"] = np.ascontiguousarray(
            np.broadcast_to((mpre - 1.0) * 1e30, (128, NCORES)).astype(np.float32))
        m["maskrow"] = np.ascontiguousarray(np.broadcast_to(mrow, (128, NCORES)))
        in_maps.append(m)
    return in_maps


def run(inputs, cfg=FULL_CFG, debug=False):
    C, NP, TW = cfg["C"], cfg["NP"], cfg["TW"]
    JC = C // 128
    TC = NP * TW
    QPS = NCORES // 2
    nc = build_program(cfg, debug=debug)
    in_maps = _prep_inputs(inputs, cfg)
    res = run_bass_kernel_spmd(nc, in_maps, core_ids=list(range(NCORES)))
    T = QPS * TC
    outx = np.empty((2, T, C), dtype=np.float32)
    for core in range(NCORES):
        s, q = core // QPS, core % QPS
        o = np.asarray(res.results[core]["out"])
        outx[s, q * TC:(q + 1) * TC, :] = o.transpose(0, 3, 2, 1).reshape(TC, C)
    return outx, res


def kernel(**inputs):
    outx, _ = run(inputs, FULL_CFG, debug=False)
    return outx


# version 19 (final)
# speedup vs baseline: 1.0495x; 1.0306x over previous
import numpy as np
from contextlib import ExitStack

import concourse.bass as bass
import concourse.mybir as mybir
from concourse.bass_utils import run_bass_kernel_spmd

F32 = mybir.dt.float32
BF16 = mybir.dt.bfloat16
I32 = mybir.dt.int32
AF = mybir.ActivationFunctionType
ALU = mybir.AluOpType

NCORES = 8
EPS = 1e-5
NEG_BIG = -1e30

FULL_CFG = dict(C=1024, DF=4096, NP=4, TW=512, LORA=16)


class Prog:
    ENGINES = ("sync", "tensor", "vector", "scalar", "gpsimd")

    def __init__(self, nc, es):
        self.nc = nc
        self.es = es
        self.events = {e: [] for e in self.ENGINES}
        self.sems = {}
        self.counts = {}

    def sem(self, name):
        self.sems[name] = self.es.enter_context(self.nc.semaphore(name))
        self.counts[name] = 0

    def emit(self, engine, fn, inc=None, amt=1, waits=()):
        waits = tuple(waits)
        post = None
        if inc is not None:
            self.counts[inc] += amt
            post = self.counts[inc]
        sems = self.sems

        def ev(eng, fn=fn, waits=waits, inc=inc, amt=amt):
            for s, v in waits:
                eng.wait_ge(sems[s], v)
            inst = fn(eng)
            if inc is not None:
                inst.then_inc(sems[inc], amt)
        self.events[engine].append(ev)
        return post

    def init_dma_pool(self, engine, n):
        names = []
        for i in range(n):
            nm = f"d_{engine}{i}"
            self.sem(nm)
            names.append(nm)
        self.dma_state = getattr(self, "dma_state", {})
        self.dma_state[engine] = {"sems": names, "rr": 0, "last": [None] * n}

    def dma(self, engine, out, in_, inc=None, waits=()):
        st = self.dma_state[engine]
        n = len(st["sems"])
        i = st["rr"] % n
        st["rr"] += 1
        sem = st["sems"][i]
        w = list(waits)
        if st["last"][i] is not None:
            w.append(st["last"][i])
        v = self.emit(engine, lambda eng, o=out, i_=in_: eng.dma_start(out=o, in_=i_),
                      inc=sem, amt=16, waits=w)
        st["last"][i] = (sem, v)
        return (sem, v)


class Arena:

    def __init__(self, nc, es, name, n_el, dtype):
        self.t = es.enter_context(nc.sbuf_tensor(name, [128, n_el], dtype))
        self.n_el = n_el
        self.off = 0

    def region(self, n):
        off = (self.off + 15) // 16 * 16
        assert off + n <= self.n_el, f"arena {self.t} overflow: {off}+{n}>{self.n_el}"
        self.off = off + n
        return off

    def view(self, off, shape):
        n = int(np.prod(shape))
        ap = self.t[:, off:off + n]
        if len(shape) == 2:
            ap = ap.rearrange("p (a b) -> p a b", a=shape[0], b=shape[1])
        elif len(shape) == 3:
            ap = ap.rearrange("p (a b c) -> p a b c", a=shape[0], b=shape[1], c=shape[2])
        return ap

    def alloc(self, shape):
        return self.view(self.region(int(np.prod(shape))), shape)


def build_program(cfg=FULL_CFG, debug=False):
    C, DF, NP, TW, LORA = cfg["C"], cfg["DF"], cfg["NP"], cfg["TW"], cfg["LORA"]
    JC = C // 128
    JF = DF // 128

    nc = bass.Bass("TRN2", target_bir_lowering=False, detect_race_conditions=False)
    es = ExitStack()
    p = Prog(nc, es)

    def din(name, shape, dt=F32):
        return nc.dram_tensor(name, shape, dt, kind="ExternalInput").ap()

    xin = din("xin", [NP, 128, JC, TW])
    xprev = din("xprev", [128, JC])
    prevvalid = din("prevvalid", [128, 1])
    maskpre = din("maskpre", [128, NCORES])
    maskbig = din("maskbig", [128, NCORES])
    maskrow = din("maskrow", [128, NCORES])

    W_INNER = {"wk": JC * C, "wv": JC * C, "wr": JC * C, "wo": JC * C,
               "fr": JC * C, "fk": JC * DF, "fv": JF * C,
               "wfc": JC * DF, "wproj": JF * C,
               "afc": JC * LORA, "aproj": JF * LORA}
    w_in = {nm: din(nm, [128, inner]) for nm, inner in W_INNER.items()}
    w_in["bfc"] = din("bfc", [LORA, DF])
    w_in["bproj"] = din("bproj", [LORA, C])

    PVN = ["ln1g", "ln1b", "ln2g", "ln2b", "tmk", "tmv", "tmr", "fmk", "fmr",
           "tdecay", "tfirst"]
    pvin = {nm: din(nm, [128, JC]) for nm in PVN}
    bfcv_in = din("bfcv", [128, JF])
    bprojv_in = din("bprojv", [128, JC])

    out = nc.dram_tensor("out", [NP, 128, JC, TW], F32, kind="ExternalOutput").ap()

    def dintt(name, shape, dt=F32, dbg=False):
        kind = "ExternalOutput" if (debug and dbg) else "Internal"
        return nc.dram_tensor(name, shape, dt, kind=kind)

    sp_ek = dintt("sp_ek", [NP, 128, JC, TW], dbg=True).ap()
    sp_beta = dintt("sp_beta", [NP, 128, JC, TW], dbg=True).ap()
    sp_r = dintt("sp_r", [NP, 128, JC, TW], dbg=True).ap()
    sp_xin = dintt("sp_xin", [NP, 128, JC, TW], dbg=True).ap()
    sp_x1 = dintt("sp_x1", [NP, 128, JC, TW], dbg=True).ap()
    sp_h = dintt("sp_h", [NP, 128, JC, TW], dbg=True).ap()
    sp_xk = dintt("sp_xk", [NP, 128, JC, TW], BF16, dbg=True).ap()
    sp_inc = dintt("sp_inc", [NP, 128, 2 * JC], dbg=True).ap()
    sp_acc = dintt("sp_acc", [128, 3 * JC], dbg=True).ap()
    sp_g = dintt("sp_g", [NP, 128, JC, TW], dbg=True).ap()
    sp_xaf = dintt("sp_xaf", [NP, 128, JC, TW], dbg=True).ap()
    sp_sig = dintt("sp_sig", [NP, 128, JC, TW], dbg=True).ap()
    sp_xf = dintt("sp_xf", [NP, 128, JC, TW], BF16, dbg=True).ap()
    sp_h2 = dintt("sp_h2", [NP, 128, JF, TW], BF16, dbg=True).ap()
    sp_k = dintt("sp_k", [NP, 128, JC, TW], dbg=True).ap()
    sp_mk = dintt("sp_mk", [NP, 128, JC], dbg=True).ap()
    sp_nmk = dintt("sp_nmk", [NP, 128, JC], dbg=True).ap()
    SCW = ["wk", "wv", "wr", "wo", "fr", "fk", "fv", "wfc", "wproj"]
    sc_w = {nm: dintt("sc_" + nm, [128, W_INNER[nm]], BF16).ap() for nm in SCW}
    sc_bfc = dintt("sc_bfc", [LORA, DF], BF16).ap()
    sc_bproj = dintt("sc_bproj", [LORA, C], BF16).ap()
    cc_sum_in = dintt("cc_sum_in", [128, 3 * JC])
    cc_sum_out = dintt("cc_sum_out", [NCORES, 128, 3 * JC])
    dbg_sum = dintt("dbg_sum", [128, NCORES, 3 * JC], dbg=True).ap() if debug else None
    cc_row_in = dintt("cc_row_in", [128, JC])
    cc_row_out = dintt("cc_row_out", [NCORES, 128, JC])
    qdram = dintt("qdram", [128, 2])

    for s in ["s_cc", "s_T", "s_V", "s_S", "s_G"]:
        p.sem(s)
    p.init_dma_pool("sync", 10)
    p.init_dma_pool("scalar", 4)
    p.init_dma_pool("gpsimd", 10)

    sbt = lambda name, shape, dt=F32: es.enter_context(nc.sbuf_tensor(name, shape, dt))
    PV = {nm: sbt("t_" + nm, [128, JC]) for nm in PVN}
    t_bfcv = sbt("t_bfcv", [128, JF])
    t_bprojv = sbt("t_bprojv", [128, JC])
    t_xprev = sbt("t_xprev", [128, JC])
    t_pvalid = sbt("t_pvalid", [128, 1])
    t_mpre = sbt("t_mpre", [128, NCORES])
    t_mbig = sbt("t_mbig", [128, NCORES])
    t_mrow = sbt("t_mrow", [128, NCORES])
    t_w = sbt("t_w", [128, JC])
    t_ew = sbt("t_ew", [128, JC])
    t_eu = sbt("t_eu", [128, JC])
    t_tww = sbt("t_tww", [128, JC])
    t_spanw = sbt("t_spanw", [128, JC])
    t_eps = sbt("t_eps", [128, 1])
    t_ones = sbt("t_ones", [128, 128], BF16)
    t_hcol = sbt("t_hcol", [128, JC])
    t_gcol = sbt("t_gcol", [128, JC])
    t_mk = [sbt(f"t_mk{i}", [128, JC]) for i in range(NP)]
    t_negmk = [sbt(f"t_negmk{i}", [128, JC]) for i in range(NP)]
    t_za = sbt("t_za", [128, JC])
    t_zb = sbt("t_zb", [128, JC])
    t_run = [sbt(f"t_run{i}", [128, 3 * JC]) for i in range(NP + 1)]
    t_inc = [sbt(f"t_inc{i}", [128, 2 * JC]) for i in range(NP)]
    t_gath = sbt("t_gath", [128, NCORES, 3 * JC])
    t_rgath = sbt("t_rgath", [128, NCORES, JC])
    t_acc = sbt("t_acc", [128, 3 * JC])
    t_cj1 = sbt("t_cj1", [128, JC])
    t_cj2 = sbt("t_cj2", [128, JC])
    t_cj3 = sbt("t_cj3", [128, JC])
    t_qv = sbt("t_qv", [128, 2])
    t_qrow = sbt("t_qrow", [1, 256])
    t_qs = sbt("t_qs", [128, 2])
    t_qis = sbt("t_qis", [128, 2])
    t_rowbf = sbt("t_rowbf", [128, JC], BF16)
    t_rowbf2 = sbt("t_rowbf2", [128, JC], BF16)
    t_r1 = sbt("t_r1", [128, 1])
    t_r2 = sbt("t_r2", [128, 1])
    t_r3 = sbt("t_r3", [128, 1])
    t_rowa = sbt("t_rowa", [128, JC])
    t_x1c = [sbt(f"t_x1c{i}", [128, JC]) for i in range(NP)]
    t_gc = [sbt(f"t_gc{i}", [128, JC]) for i in range(NP)]
    t_zero = sbt("t_zero", [128, 1])

    afcb = sbt("afcb", [128, JC, LORA], BF16)
    aprojb = sbt("aprojb", [128, JF, LORA], BF16)

    WR_SLOT = 1024
    NSLOT = 9
    wring = sbt("wring", [128, NSLOT * WR_SLOT], BF16)
    wslot = [wring[:, i * WR_SLOT:(i + 1) * WR_SLOT] for i in range(NSLOT)]
    wslot_tok = [[] for _ in range(NSLOT)]
    ring_rr = [0]

    WST_N = 512
    NWST = 4
    WST = [sbt(f"wst{i}", [128, WST_N], F32) for i in range(NWST)]
    WSTI = sbt("wsti", [128, WST_N], I32)
    WSTO2 = [sbt(f"wsto{i}", [128, WST_N], BF16) for i in range(2)]
    WSTO = WSTO2[0]
    wst_free = [[] for _ in range(NWST)]
    wsto_free = [[], []]
    wsti_free = [[]]

    TWJC = JC * TW
    AR = Arena(nc, es, "arena_f", 6 * TWJC + 3 * TW + 64, F32)
    rA0 = AR.region(TWJC)
    rA1 = AR.region(TWJC)
    rB = AR.region(TWJC)
    rC = AR.region(TWJC)
    rD = AR.region(TWJC)
    rE = AR.region(TWJC)
    rS1 = AR.region(TW)
    rS2 = AR.region(TW)
    rS3 = AR.region(TW)
    ABF = Arena(nc, es, "arena_b", 8 * 1024 + 16 * 1024 + 1024 + 4096 + 32, BF16)
    bR1 = ABF.region(8192)
    bR2 = ABF.region(16384)
    bR4 = ABF.region(1024)
    bR5 = ABF.region(4096)
    bfcb = sbt("bfcb", [LORA, DF], BF16)
    bprojb = sbt("bprojb", [LORA, C], BF16)

    pbank = [es.enter_context(nc.psum_tensor(f"pb{i}", [128, 512], F32)) for i in range(8)]
    bank_tok = [[] for _ in range(8)]
    bank_rr = [0]

    def get_bank(extra_waits=()):
        i = bank_rr[0] % 8
        bank_rr[0] += 1
        w = list(bank_tok[i]) + list(extra_waits)
        bank_tok[i] = []
        return i, pbank[i], w

    def bank_done(i, toks):
        bank_tok[i] = list(toks)

    def get_all_banks():
        w = []
        for i in range(8):
            w += bank_tok[i]
            bank_tok[i] = []
        bank_rr[0] = 0
        return pbank, w

    def vi(fn, waits=()):
        return p.emit("vector", fn, inc="s_V", waits=waits)

    def vd(fn, waits=()):
        return p.emit("vector", fn, inc="s_V",
                      waits=list(waits) + [("s_V", p.counts["s_V"])])

    def sd(fn, waits=()):
        return p.emit("scalar", fn, inc="s_S",
                      waits=list(waits) + [("s_S", p.counts["s_S"])])

    def si(fn, waits=()):
        return p.emit("scalar", fn, inc="s_S", waits=waits)

    def ti(fn, waits=()):
        return p.emit("tensor", fn, inc="s_T", waits=waits)

    _ptoks = []
    for nm in PVN:
        _ptoks.append(p.dma("sync", PV[nm][:, :], pvin[nm]))
    _ptoks.append(p.dma("sync", t_bfcv[:, :], bfcv_in))
    _ptoks.append(p.dma("sync", t_bprojv[:, :], bprojv_in))
    _ptoks.append(p.dma("sync", t_xprev[:, :], xprev))
    _ptoks.append(p.dma("sync", t_pvalid[:, :], prevvalid))
    _ptoks.append(p.dma("sync", t_mpre[:, :], maskpre))
    _ptoks.append(p.dma("sync", t_mbig[:, :], maskbig))
    _ptoks.append(p.dma("sync", t_mrow[:, :], maskrow))
    _tp = vi(lambda e: e.memset(t_r3[:, :], 0.0), waits=_ptoks)
    tok_params = ("s_V", _tp)

    vi(lambda e: e.memset(t_ones[:, :], 1.0))
    vi(lambda e: e.memset(t_eps[:, :], EPS))
    vi(lambda e: e.memset(t_zero[:, :], 0.0))
    tok = si(lambda e: e.activation(out=t_w[:, :], in_=PV["tdecay"][:, :], func=AF.Exp,
                                    bias=0.0), waits=[tok_params])
    tok_w = vi(lambda e: e.tensor_scalar(out=t_w[:, :], in0=t_w[:, :], scalar1=-1.0,
                                         scalar2=None, op0=ALU.mult),
               waits=[("s_S", tok)])
    tok_ew = si(lambda e: e.activation(out=t_ew[:, :], in_=t_w[:, :], func=AF.Exp,
                                       bias=0.0), waits=[("s_V", tok_w)])
    tok_eu = si(lambda e: e.activation(out=t_eu[:, :], in_=PV["tfirst"][:, :],
                                       func=AF.Exp, bias=0.0))
    tok_tww = vd(lambda e: e.tensor_scalar(out=t_tww[:, :], in0=t_w[:, :],
                                           scalar1=float(TW), scalar2=None,
                                           op0=ALU.mult), waits=[("s_V", tok_w)])
    tok_spanw = vd(lambda e: e.tensor_scalar(out=t_spanw[:, :], in0=t_w[:, :],
                                             scalar1=float(NP * TW), scalar2=None,
                                             op0=ALU.mult))
    vi(lambda e: e.memset(t_run[0][:, 0:2 * JC], 0.0))
    tok_run = vi(lambda e: e.memset(t_run[0][:, 2 * JC:3 * JC], NEG_BIG))

    def ew_bcast(j, n):
        return bass.AP(t_ew, j, [[JC, 128], [0, n]])

    wcast_done = {}
    _wq = {"buf": 0, "obuf": 0}
    QIDX = {"wfc": 0, "wproj": 1}
    tok_qscale = [None]

    task_q = []
    task_qv = []

    def drain_tasks(n):
        for _ in range(min(n, len(task_q))):
            task_q.pop(0)()

    def drain_vtasks(n):
        for _ in range(min(n, len(task_qv))):
            task_qv.pop(0)()

    def drain_all_tasks():
        while task_qv:
            task_qv.pop(0)()
        while task_q:
            task_q.pop(0)()

    def cast_pieces(nm, quant=False):
        inner = W_INNER[nm]
        srcf = w_in[nm].rearrange("a b -> (a b)")
        dstf = sc_w[nm].rearrange("a b -> (a b)")
        total = 128 * inner
        piece = 128 * WST_N
        parts = (total + piece - 1) // piece
        out = []
        for h in range(parts):
            def body(h=h):
                nw = min(piece, total - h * piece)
                mw = nw // 128
                b = _wq["buf"]; _wq["buf"] = (b + 1) % NWST
                st = WST[b]
                tok_l = p.dma("sync", st[:, 0:mw], srcf[h * piece:h * piece + nw],
                              waits=wst_free[b])
                ob = _wq["obuf"]; _wq["obuf"] ^= 1
                OT = WSTO2[ob]
                if quant:
                    qi = QIDX[nm]
                    assert tok_qscale[0] is not None
                    tok_i = si(lambda e, st=st, mw=mw, qi=qi: e.activation(
                        out=WSTI[:, 0:mw], in_=st[:, 0:mw], func=AF.Identity,
                        bias=0.0, scale=t_qis[:, qi:qi + 1]),
                        waits=[tok_l, ("s_S", tok_qscale[0])] + wsti_free[0])
                    tok_o = si(lambda e, mw=mw, qi=qi, OT=OT: e.activation(
                        out=OT[:, 0:mw], in_=WSTI[:, 0:mw], func=AF.Identity,
                        bias=0.0, scale=t_qs[:, qi:qi + 1]), waits=wsto_free[ob])
                    wsti_free[0] = [("s_S", tok_o)]
                else:
                    tok_o = si(lambda e, st=st, mw=mw, OT=OT: e.activation(
                        out=OT[:, 0:mw], in_=st[:, 0:mw], func=AF.Copy,
                        bias=0.0), waits=[tok_l] + wsto_free[ob])
                wst_free[b] = [("s_S", tok_o)]
                tok_g = p.dma(
                    "scalar", dstf[h * piece:h * piece + nw],
                    OT[:, 0:mw], waits=[("s_S", tok_o)])
                wsto_free[ob] = [tok_g]
                wcast_done[nm] = tok_g
            out.append(body)
        return out

    def emit_cast_tasks(nm, quant=False):
        for f in cast_pieces(nm, quant):
            f()

    def emit_lora_tasks():
        for nm, dst, kcn in [("afc", afcb, JC), ("aproj", aprojb, JF)]:
            n = kcn * LORA
            b = _wq["buf"]; _wq["buf"] = (b + 1) % NWST
            st = WST[b]
            stv = st[:, 0:n].rearrange("p (a c) -> p a c", a=kcn, c=LORA)
            tok_l = p.dma("sync", st[:, 0:n], w_in[nm], waits=wst_free[b])
            tok_o = si(lambda e, stv=stv, dst=dst: e.activation(
                out=dst[:, :, :], in_=stv, func=AF.Copy, bias=0.0),
                waits=[tok_l])
            wst_free[b] = [("s_S", tok_o)]
        for nm, dst, m in [("bfc", sc_bfc, DF), ("bproj", sc_bproj, C)]:
            parts = (m + WST_N - 1) // WST_N
            for h in range(parts):
                mw = min(WST_N, m - h * WST_N)
                b = _wq["buf"]; _wq["buf"] = (b + 1) % NWST
                st = WST[b]
                tok_l = p.dma("sync", st[0:LORA, 0:mw],
                              w_in[nm][:, h * WST_N:h * WST_N + mw],
                              waits=wst_free[b])
                tok_o = si(lambda e, st=st, mw=mw: e.activation(
                    out=WSTO[0:LORA, 0:mw], in_=st[0:LORA, 0:mw], func=AF.Copy,
                    bias=0.0), waits=[tok_l] + wsto_free[0])
                wst_free[b] = [("s_S", tok_o)]
                tok_g = p.dma("scalar", dst[:, h * WST_N:h * WST_N + mw],
                              WSTO[0:LORA, 0:mw], waits=[("s_S", tok_o)])
                wsto_free[0] = [tok_g]
            wcast_done[nm] = tok_g

    def quant_scale_pieces():
        tok_red = {0: None, 1: None}
        qstate = {0: True, 1: True}
        out = []
        for qi, nm in enumerate(["wfc", "wproj"]):
            srcf = w_in[nm].rearrange("a b -> (a b)")
            total = 128 * W_INNER[nm]
            piece = 128 * WST_N
            parts = (total + piece - 1) // piece
            for h in range(parts):
                def body(h=h, qi=qi, srcf=srcf, total=total):
                    nw = min(piece, total - h * piece)
                    mw = nw // 128
                    b = _wq["buf"]; _wq["buf"] = (b + 1) % NWST
                    st = WST[b]
                    tok_l = p.dma("sync", st[:, 0:mw],
                                  srcf[h * piece:h * piece + nw],
                                  waits=wst_free[b])
                    if qstate[qi]:
                        tok_r = vi(lambda e, st=st, mw=mw, qi=qi: e.tensor_reduce(
                            out=t_qv[:, qi:qi + 1], in_=st[:, 0:mw],
                            axis=mybir.AxisListType.X, op=ALU.max,
                            apply_absolute_value=True), waits=[tok_l])
                        qstate[qi] = False
                    else:
                        tok_r = vi(lambda e, st=st, mw=mw: e.tensor_reduce(
                            out=t_r1[:, :], in_=st[:, 0:mw],
                            axis=mybir.AxisListType.X, op=ALU.max,
                            apply_absolute_value=True), waits=[tok_l])
                        tok_r = vi(lambda e, qi=qi: e.tensor_tensor(
                            out=t_qv[:, qi:qi + 1], in0=t_qv[:, qi:qi + 1],
                            in1=t_r1[:, :], op=ALU.max), waits=[("s_V", tok_r)])
                    wst_free[b] = [("s_V", tok_r)]
                    tok_red[qi] = tok_r
                out.append(body)
        out.append(lambda: quant_scale_final(tok_red))
        return out

    def quant_scale_final(tok_red):
        tok_d1 = p.dma("gpsimd", qdram.ap(), t_qv[:, 0:2],
                       waits=[("s_V", tok_red[0]), ("s_V", tok_red[1])])
        tok_d2 = p.dma("gpsimd", t_qrow[:, :], qdram.ap().rearrange("a b -> (a b)"),
                       waits=[tok_d1])
        tok_ms = []
        for qi in range(2):
            qv = bass.AP(t_qrow, qi, [[256, 1], [2, 128]])
            tok_ms.append(vd(lambda e, qv=qv, qi=qi: e.tensor_reduce(
                out=t_qrow[:, 254 + qi:255 + qi], in_=qv,
                axis=mybir.AxisListType.X, op=ALU.max),
                waits=[tok_d2]))
        tok_qq = vd(lambda e: e.tensor_scalar(out=t_qrow[:, 252:254],
                                              in0=t_qrow[:, 254:256],
                                              scalar1=1.0 / 127.0, scalar2=None,
                                              op0=ALU.mult),
                    waits=[("s_V", tok_ms[-1])])
        vd(lambda e: e.reciprocal(out=t_qrow[:, 250:252], in_=t_qrow[:, 252:254]),
           waits=[("s_V", tok_qq)])
        tok_of = vd(lambda e: e.memset(t_qrow[:, 248:249], 1.0))
        ib, bnk, wts = get_bank([("s_V", tok_of)])
        onesrow = bass.AP(t_qrow, 248, [[256, 1], [0, 128]])
        tok_t = ti(lambda e, bnk=bnk: e.matmul(bnk[:, 0:4], onesrow,
                                               t_qrow[:, 250:254],
                                               start=True, stop=True), waits=wts)
        sd(lambda e, bnk=bnk: e.activation(out=t_qis[:, 0:2], in_=bnk[:, 0:2],
                                           func=AF.Identity, bias=0.0, scale=1.0),
           waits=[("s_T", tok_t)])
        tok_qs = sd(lambda e, bnk=bnk: e.activation(out=t_qs[:, 0:2], in_=bnk[:, 2:4],
                                                    func=AF.Identity, bias=0.0,
                                                    scale=1.0))
        bank_done(ib, [("s_S", tok_qs)])
        tok_qscale[0] = tok_qs

    def stream_chunks(src_name, k0, k1, waits=(), m=None):
        src = sc_w[src_name]
        if m is None:
            m = C
        per_slot = WR_SLOT // m
        views = []
        vtoks = []
        slots = []
        kc = k0
        while kc < k1:
            kh = min(k1, kc + per_slot)
            slot = ring_rr[0] % NSLOT
            ring_rr[0] += 1
            wts = list(wslot_tok[slot]) + list(waits) + [wcast_done[src_name]]
            wslot_tok[slot] = []
            v = wslot[slot][:, 0:(kh - kc) * m].rearrange("p (a c) -> p a c",
                                                          a=kh - kc, c=m)
            tok = p.dma("sync", wslot[slot][:, 0:(kh - kc) * m],
                        src[:, kc * m:kh * m], waits=wts)
            for i in range(kh - kc):
                views.append(v[:, i, :])
                vtoks.append(tok)
            slots.append(slot)
            kc = kh
        return views, vtoks, slots

    def stream_cols(src_name, g8, waits=()):
        return stream_chunks(src_name, g8 * JC, g8 * JC + JC, waits=waits, m=1024)

    def slots_release(slots, toks):
        for s in slots:
            wslot_tok[s] = list(wslot_tok[s]) + list(toks)

    m_st = AR.view(rS1, [TW])
    e2_st = AR.view(rS2, [TW])
    msq_st = AR.view(rS3, [TW])

    def ln_stats(x_ap, xb_ap, xsq_ap, waits, stage_free=(), m_free=()):
        si(lambda e: e.activation(out=xb_ap, in_=x_ap, func=AF.Copy, bias=0.0),
           waits=list(waits) + list(stage_free))
        tok_s = si(lambda e: e.activation(out=xsq_ap, in_=x_ap, func=AF.Square,
                                          bias=0.0))
        ib1, b1, w1 = get_bank([("s_S", tok_s)])
        for j in range(JC):
            tok_t1 = ti(lambda e, j=j, b1=b1: e.matmul(
                b1[:, 0:TW], t_ones[:, :], xb_ap[:, j, :],
                start=(j == 0), stop=(j == JC - 1)), waits=w1 if j == 0 else ())
        ib2, b2, w2 = get_bank()
        for j in range(JC):
            tok_t2 = ti(lambda e, j=j, b2=b2: e.matmul(
                b2[:, 0:TW], t_ones[:, :], xsq_ap[:, j, :],
                start=(j == 0), stop=(j == JC - 1)), waits=w2 if j == 0 else ())
        tok_m = si(lambda e, b1=b1: e.activation(out=m_st[:, :], in_=b1[:, 0:TW],
                                                 func=AF.Identity, bias=0.0,
                                                 scale=1.0 / C),
                   waits=[("s_T", tok_t1)] + list(m_free))
        tok_e2 = si(lambda e, b2=b2: e.activation(out=e2_st[:, :], in_=b2[:, 0:TW],
                                                  func=AF.Identity, bias=0.0,
                                                  scale=1.0 / C),
                    waits=[("s_T", tok_t2)])
        bank_done(ib1, [("s_S", tok_m)])
        bank_done(ib2, [("s_S", tok_e2)])
        tok_msq = si(lambda e: e.activation(out=msq_st[:, :], in_=m_st[:, :],
                                            func=AF.Square, bias=0.0))
        tok_var = vi(lambda e: e.tensor_tensor(out=e2_st[:, :], in0=e2_st[:, :],
                                               in1=msq_st[:, :], op=ALU.subtract),
                     waits=[("s_S", tok_msq)])
        si(lambda e: e.activation(out=e2_st[:, :], in_=e2_st[:, :], func=AF.Ln,
                                  bias=t_eps[:, :], scale=1.0),
           waits=[("s_V", tok_var)])
        tok_rstd = si(lambda e: e.activation(out=e2_st[:, :], in_=e2_st[:, :],
                                             func=AF.Exp, bias=0.0, scale=-0.5))
        return tok_rstd, ("s_T", tok_t2)

    def ln_apply(x_ap, out_ap, tok_rstd, g=None, b=None, waits=(), scratch=None):
        sc = scratch if scratch is not None else out_ap
        tok = None
        for j in range(JC):
            tok = vi(lambda e, j=j: e.tensor_tensor(
                out=sc[:, j, :], in0=x_ap[:, j, :], in1=m_st[:, :], op=ALU.subtract),
                waits=(list(waits) + [("s_S", tok_rstd)]) if j == 0 else ())
        for j in range(JC):
            if g is None:
                tok = vi(lambda e, j=j: e.tensor_tensor(
                    out=out_ap[:, j, :], in0=sc[:, j, :], in1=e2_st[:, :],
                    op=ALU.mult))
            else:
                vi(lambda e, j=j: e.tensor_tensor(
                    out=sc[:, j, :], in0=sc[:, j, :], in1=e2_st[:, :], op=ALU.mult))
                tok = vi(lambda e, j=j: e.tensor_scalar(
                    out=out_ap[:, j, :], in0=sc[:, j, :],
                    scalar1=g[:, j:j + 1], scalar2=b[:, j:j + 1],
                    op0=ALU.mult, op1=ALU.add))
        return tok

    def row_ln(x_ap, out_ap, g=None, b=None, waits=()):
        sd(lambda e: e.activation(out=t_rowbf[:, :], in_=x_ap, func=AF.Copy, bias=0.0),
           waits=list(waits) + [("s_T", p.counts["s_T"]), ("s_V", p.counts["s_V"])])
        tok_s = sd(lambda e: e.activation(out=t_rowbf2[:, :], in_=x_ap,
                                          func=AF.Square, bias=0.0))
        ib, bnk, wts = get_bank([("s_S", tok_s)])
        ti(lambda e, bnk=bnk: e.matmul(bnk[:, 0:JC], t_ones[:, :], t_rowbf[:, :],
                                       start=True, stop=True), waits=wts)
        tok_t = ti(lambda e, bnk=bnk: e.matmul(bnk[:, JC:2 * JC], t_ones[:, :],
                                               t_rowbf2[:, :], start=True, stop=True))
        vd(lambda e, bnk=bnk: e.tensor_reduce(
            out=t_r1[:, :], in_=bnk[:, 0:JC], axis=mybir.AxisListType.X, op=ALU.add),
            waits=[("s_T", tok_t)])
        tok_v = vd(lambda e, bnk=bnk: e.tensor_reduce(
            out=t_r2[:, :], in_=bnk[:, JC:2 * JC], axis=mybir.AxisListType.X,
            op=ALU.add))
        bank_done(ib, [("s_V", tok_v)])
        vd(lambda e: e.tensor_scalar(out=t_r1[:, :], in0=t_r1[:, :], scalar1=1.0 / C,
                                     scalar2=None, op0=ALU.mult),
           waits=[("s_V", tok_v)])
        tok_v = vd(lambda e: e.tensor_scalar(out=t_r2[:, :], in0=t_r2[:, :],
                                             scalar1=1.0 / C, scalar2=None,
                                             op0=ALU.mult))
        tok_s = sd(lambda e: e.activation(out=t_r3[:, :], in_=t_r1[:, :],
                                          func=AF.Square, bias=0.0),
                   waits=[("s_V", tok_v)])
        tok_v = vd(lambda e: e.tensor_tensor(out=t_r2[:, :], in0=t_r2[:, :],
                                             in1=t_r3[:, :], op=ALU.subtract),
                   waits=[("s_S", tok_s)])
        sd(lambda e: e.activation(out=t_r2[:, :], in_=t_r2[:, :], func=AF.Ln,
                                  bias=t_eps[:, :], scale=1.0), waits=[("s_V", tok_v)])
        tok_s = sd(lambda e: e.activation(out=t_r2[:, :], in_=t_r2[:, :], func=AF.Exp,
                                          bias=0.0, scale=-0.5))
        tok_v = vd(lambda e: e.tensor_scalar(out=out_ap, in0=x_ap, scalar1=t_r1[:, :],
                                             scalar2=t_r2[:, :], op0=ALU.subtract,
                                             op1=ALU.mult), waits=[("s_S", tok_s)])
        if g is not None:
            vd(lambda e: e.tensor_tensor(out=out_ap, in0=out_ap, in1=g, op=ALU.mult))
            tok_v = vd(lambda e: e.tensor_tensor(out=out_ap, in0=out_ap, in1=b,
                                                 op=ALU.add))
        return tok_v

    x_t = [AR.view(rA0, [JC, TW]), AR.view(rA1, [JC, TW])]
    beta_t = [AR.view(rA0, [JC, TW]), AR.view(rA1, [JC, TW])]
    xin_t = AR.view(rB, [JC, TW])
    h_t = AR.view(rC, [JC, TW])
    ek1_t = AR.view(rC, [JC, TW])
    d_t = AR.view(rD, [JC, TW])
    prod_t = AR.view(rD, [JC, TW])
    r1_t = AR.view(rE, [JC, TW])
    xb_t = ABF.view(bR1, [JC, TW])
    xsq_t = ABF.view(bR1 + TWJC, [JC, TW])
    xk_t = ABF.view(bR2, [JC, TW])
    xv_t = ABF.view(bR2 + TWJC, [JC, TW])
    xr_t = ABF.view(bR2 + 2 * TWJC, [JC, TW])

    prev = {"xfree": [[], []], "xinfree": [], "ekfree": [], "betafree": [[], []],
            "rfree": [], "stagefree": [], "mixfree": [], "mfree": [],
            "applyfree": []}

    def phase1_pass(pp):
        par = pp % 2
        tx = x_t[par]
        tbeta = beta_t[par]
        if pp in xpre:
            tok_x = xpre.pop(pp)
        else:
            tok_x = p.dma("sync", tx, xin[pp], waits=prev["xfree"][par])
        tok_rstd, tok_stg = ln_stats(tx, xb_t, xsq_t, waits=[tok_x],
                                     stage_free=prev["stagefree"],
                                     m_free=prev["mfree"])
        tok_xin = ln_apply(tx, xin_t, tok_rstd, waits=prev["xinfree"])
        if pp == NP - 1 and not debug:
            tok_sp_xin = None
        else:
            tok_sp_xin = p.dma("gpsimd", sp_xin[pp], xin_t,
                               waits=[("s_V", tok_xin)])
        tok_rstd, tok_stg = ln_stats(xin_t, xb_t, xsq_t, waits=[("s_V", tok_xin)])
        tok_h = ln_apply(xin_t, h_t, tok_rstd, g=PV["ln1g"], b=PV["ln1b"],
                         waits=prev["mixfree"])
        prev["xinfree"] = [tok_sp_xin] if tok_sp_xin is not None else []
        prev["stagefree"] = [tok_stg]
        prev["mfree"] = [("s_V", tok_h)]
        for j in range(JC):
            vi(lambda e, j=j: e.tensor_tensor(
                out=d_t[:, j, 1:TW], in0=h_t[:, j, 1:TW], in1=h_t[:, j, 0:TW - 1],
                op=ALU.subtract))
            vd(lambda e, j=j: e.tensor_tensor(
                out=d_t[:, j, 0:1], in0=h_t[:, j, 0:1], in1=t_hcol[:, j:j + 1],
                op=ALU.subtract))
        tok_mix = None
        mix_toks = []
        for tm, xo in ((PV["tmk"], xk_t), (PV["tmv"], xv_t), (PV["tmr"], xr_t)):
            for j in range(JC):
                vi(lambda e, j=j, tm=tm, xo=xo: e.scalar_tensor_tensor(
                    out=xo[:, j, 1:TW], in0=d_t[:, j, 1:TW], scalar=tm[:, j:j + 1],
                    in1=h_t[:, j, 0:TW - 1], op0=ALU.mult, op1=ALU.add))
                tok_mix = vd(lambda e, j=j, tm=tm, xo=xo: e.scalar_tensor_tensor(
                    out=xo[:, j, 0:1], in0=d_t[:, j, 0:1], scalar=tm[:, j:j + 1],
                    in1=t_hcol[:, j:j + 1], op0=ALU.mult, op1=ALU.add))
            mix_toks.append(tok_mix)
        tok_hcol = vi(lambda e: e.tensor_copy(t_hcol[:, :], h_t[:, :, TW - 1:TW]),
                      waits=[("s_V", tok_h)])
        if debug:
            p.dma("gpsimd", sp_h[pp], h_t, waits=[("s_V", tok_hcol)])
            p.dma("gpsimd", sp_xk[pp], xk_t, waits=[("s_V", tok_mix)])
        ek_toks = [None] * JC
        views, vtoks, slots = stream_chunks("wk", 0, JC)
        lastmm = None
        for mc in range(JC):
            ib, bnk, wts = get_bank()
            for kc in range(JC):
                lastmm = ti(lambda e, bnk=bnk, v=views[kc], mc=mc, kc=kc: e.matmul(
                    bnk[:, 0:TW], v[:, mc * 128:(mc + 1) * 128], xk_t[:, kc, :],
                    start=(kc == 0), stop=(kc == JC - 1)),
                    waits=(wts + [vtoks[kc], ("s_V", mix_toks[0])]) if kc == 0
                    else ([vtoks[kc]] if vtoks[kc] is not vtoks[kc - 1] else ()))
            tok_red = vi(lambda e, bnk=bnk, mc=mc, pp=pp: e.tensor_reduce(
                out=t_mk[pp][:, mc:mc + 1], in_=bnk[:, 0:TW],
                axis=mybir.AxisListType.X, op=ALU.max), waits=[("s_T", lastmm)])
            vi(lambda e, mc=mc, pp=pp: e.tensor_scalar(
                out=t_negmk[pp][:, mc:mc + 1], in0=t_mk[pp][:, mc:mc + 1],
                scalar1=-1.0, scalar2=None, op0=ALU.mult),
                waits=[("s_V", tok_red)])
            if debug:
                si(lambda e, bnk=bnk, mc=mc: e.activation(
                    out=d_t[:, mc, :], in_=bnk[:, 0:TW], func=AF.Identity,
                    bias=0.0, scale=1.0), waits=[("s_V", p.counts["s_V"])])
            tok_ek = si(lambda e, bnk=bnk, mc=mc, pp=pp: e.activation(
                out=ek1_t[:, mc, :], in_=bnk[:, 0:TW], func=AF.Exp,
                bias=t_negmk[pp][:, mc:mc + 1], scale=1.0),
                waits=[("s_V", p.counts["s_V"])] + prev["ekfree"] if mc == 0
                else [("s_V", p.counts["s_V"])])
            ek_toks[mc] = tok_ek
            bank_done(ib, [("s_S", tok_ek)])
            drain_vtasks(1)
        slots_release(slots, [("s_T", lastmm)])
        dbg_ktok = []
        if debug:
            tk1 = p.dma("gpsimd", sp_k[pp], d_t, waits=[("s_S", ek_toks[-1])])
            p.dma("gpsimd", sp_mk[pp], t_mk[pp][:, :], waits=[("s_S", ek_toks[-1])])
            p.dma("gpsimd", sp_nmk[pp], t_negmk[pp][:, :], waits=[("s_S", ek_toks[-1])])
            dbg_ktok = [tk1]
        views, vtoks, slots = stream_chunks("wv", 0, JC)
        beta_toks = [None] * JC
        for mc in range(JC):
            ib, bnk, wts = get_bank()
            for kc in range(JC):
                lastmm = ti(lambda e, bnk=bnk, v=views[kc], mc=mc, kc=kc: e.matmul(
                    bnk[:, 0:TW], v[:, mc * 128:(mc + 1) * 128], xv_t[:, kc, :],
                    start=(kc == 0), stop=(kc == JC - 1)),
                    waits=(wts + [vtoks[kc], ("s_V", mix_toks[1])])
                    if (kc == 0 and mc == 0) else (wts + [vtoks[kc]]) if kc == 0
                    else ([vtoks[kc]] if vtoks[kc] is not vtoks[kc - 1] else ()))
            tok_b = vi(lambda e, bnk=bnk, mc=mc, tbeta=tbeta: e.tensor_tensor(
                out=tbeta[:, mc, :], in0=ek1_t[:, mc, :], in1=bnk[:, 0:TW],
                op=ALU.mult),
                waits=[("s_T", lastmm), ("s_S", ek_toks[mc])]
                + (prev["betafree"][par] if mc == 0 else []))
            beta_toks[mc] = tok_b
            bank_done(ib, [("s_V", tok_b)])
            drain_vtasks(1)
        slots_release(slots, [("s_T", lastmm)])
        views, vtoks, slots = stream_chunks("wr", 0, JC)
        r_toks = [None] * JC
        for mc in range(JC):
            ib, bnk, wts = get_bank()
            for kc in range(JC):
                lastmm = ti(lambda e, bnk=bnk, v=views[kc], mc=mc, kc=kc: e.matmul(
                    bnk[:, 0:TW], v[:, mc * 128:(mc + 1) * 128], xr_t[:, kc, :],
                    start=(kc == 0), stop=(kc == JC - 1)),
                    waits=(wts + [vtoks[kc], ("s_V", mix_toks[2])])
                    if (kc == 0 and mc == 0) else (wts + [vtoks[kc]]) if kc == 0
                    else ([vtoks[kc]] if vtoks[kc] is not vtoks[kc - 1] else ()))
            tok_r = si(lambda e, bnk=bnk, mc=mc: e.activation(
                out=r1_t[:, mc, :], in_=bnk[:, 0:TW], func=AF.Sigmoid, bias=0.0,
                scale=1.0),
                waits=[("s_T", lastmm)] + (prev["rfree"] if mc == 0 else []))
            r_toks[mc] = tok_r
            bank_done(ib, [("s_S", tok_r)])
        slots_release(slots, [("s_T", lastmm)])
        tok_sca = None
        for j in range(JC):
            tok_sca = vi(lambda e, j=j, tbeta=tbeta: e.tensor_tensor_scan(
                prod_t[:, j, :], ew_bcast(j, TW), tbeta[:, j, :], 0.0,
                ALU.mult, ALU.add),
                (([("s_V", beta_toks[-1]), ("s_S", tok_ew)] + dbg_ktok)
                 if j == 0 else ()))
        tok_za = vi(lambda e: e.tensor_copy(t_za[:, :], prod_t[:, :, TW - 1:TW]),
                    waits=[("s_V", tok_sca)])
        for j in range(JC):
            tok_sca = vi(lambda e, j=j: e.tensor_tensor_scan(
                prod_t[:, j, :], ew_bcast(j, TW), ek1_t[:, j, :], 0.0,
                ALU.mult, ALU.add),
                waits=[("s_S", ek_toks[-1]), ("s_V", tok_za)] if j == 0 else ())
        tok_zb = vi(lambda e: e.tensor_copy(t_zb[:, :], prod_t[:, :, TW - 1:TW]),
                    waits=[("s_V", tok_sca)])
        ro, rn = t_run[pp], t_run[pp + 1]
        a_o, b_o, m_o = ro[:, 0:JC], ro[:, JC:2 * JC], ro[:, 2 * JC:3 * JC]
        a_n, b_n, m_n = rn[:, 0:JC], rn[:, JC:2 * JC], rn[:, 2 * JC:3 * JC]
        vd(lambda e: e.tensor_tensor(out=t_cj1[:, :], in0=m_o, in1=t_tww[:, :],
                                     op=ALU.add), waits=[("s_V", tok_zb)])
        vd(lambda e: e.tensor_tensor(out=m_n, in0=t_cj1[:, :], in1=t_mk[pp][:, :],
                                     op=ALU.max))
        tok_v = vd(lambda e: e.tensor_tensor(out=t_cj1[:, :], in0=t_cj1[:, :],
                                             in1=m_n, op=ALU.subtract))
        sd(lambda e: e.activation(out=t_cj1[:, :], in_=t_cj1[:, :], func=AF.Exp,
                                  bias=0.0), waits=[("s_V", tok_v)])
        tok_v = vd(lambda e: e.tensor_tensor(out=t_cj2[:, :], in0=t_mk[pp][:, :],
                                             in1=m_n, op=ALU.subtract))
        tok_s = sd(lambda e: e.activation(out=t_cj2[:, :], in_=t_cj2[:, :],
                                          func=AF.Exp, bias=0.0),
                   waits=[("s_V", tok_v)])
        vd(lambda e: e.tensor_tensor(out=a_n, in0=a_o, in1=t_cj1[:, :], op=ALU.mult),
           waits=[("s_S", tok_s)])
        vd(lambda e: e.tensor_tensor(out=t_cj3[:, :], in0=t_za[:, :],
                                     in1=t_cj2[:, :], op=ALU.mult))
        vd(lambda e: e.tensor_tensor(out=a_n, in0=a_n, in1=t_cj3[:, :], op=ALU.add))
        vd(lambda e: e.tensor_tensor(out=b_n, in0=b_o, in1=t_cj1[:, :], op=ALU.mult))
        vd(lambda e: e.tensor_tensor(out=t_cj3[:, :], in0=t_zb[:, :],
                                     in1=t_cj2[:, :], op=ALU.mult))
        tok_run = vd(lambda e: e.tensor_tensor(out=b_n, in0=b_n, in1=t_cj3[:, :],
                                               op=ALU.add))
        if pp == NP - 1 and not debug:
            prev["ekfree"] = []
            prev["betafree"][par] = []
            prev["xfree"][par] = []
            prev["rfree"] = []
        else:
            tok_sp1 = p.dma("gpsimd", sp_ek[pp], ek1_t,
                            waits=[("s_S", ek_toks[-1]), ("s_V", tok_zb)])
            tok_sp2 = p.dma("gpsimd", sp_beta[pp], tbeta,
                            waits=[("s_V", tok_za)])
            tok_sp3 = p.dma("gpsimd", sp_r[pp], r1_t,
                            waits=[("s_S", r_toks[-1])])
            prev["ekfree"] = [tok_sp1]
            prev["betafree"][par] = [tok_sp2]
            prev["xfree"][par] = [tok_sp2]
            prev["rfree"] = [tok_sp3]
        prev["mixfree"] = ([("s_V", tok_mix), ("s_T", lastmm)]
                           + list(prev["ekfree"]))
        return tok_run

    xpre = {0: p.dma("sync", x_t[0], xin[0], waits=[])}
    emit_lora_tasks()
    emit_cast_tasks("wk")
    emit_cast_tasks("wv")
    emit_cast_tasks("wr")
    row_ln(t_xprev[:, :], t_rowa[:, :], waits=[tok_params])
    row_ln(t_rowa[:, :], t_rowa[:, :], g=PV["ln1g"][:, :], b=PV["ln1b"][:, :])
    vi(lambda e: e.tensor_scalar(out=t_hcol[:, :], in0=t_rowa[:, :],
                                 scalar1=t_pvalid[:, :], scalar2=None, op0=ALU.mult))

    after_pass = {0: ["wo"]}
    task_qv.extend(quant_scale_pieces())
    for pp in range(NP):
        tok_run_last = phase1_pass(pp)
        for nm in after_pass.get(pp, []):
            emit_cast_tasks(nm)
        drain_vtasks(16)
    task_q.extend(cast_pieces("fr"))
    task_q.extend(cast_pieces("fk"))
    task_q.extend(cast_pieces("fv"))
    task_q.extend(cast_pieces("wfc", quant=True))
    task_q.extend(cast_pieces("wproj", quant=True))

    tok_pack = p.dma("gpsimd", cc_sum_in.ap(), t_run[NP][:, :],
                     waits=[("s_V", tok_run_last)])
    tok_cc = p.emit("gpsimd", lambda e: e.collective_compute(
        "AllGather", ALU.bypass, replica_groups=[list(range(NCORES))],
        ins=[cc_sum_in.ap().opt()], outs=[cc_sum_out.ap().opt()]),
        inc="s_cc", amt=1, waits=[tok_pack])
    while task_qv:
        task_qv.pop(0)()
    drain_tasks(16)
    gtoks = []
    for r in range(NCORES):
        gtoks.append(p.dma("gpsimd", t_gath[:, r, :], cc_sum_out.ap()[r],
                           waits=[("s_cc", tok_cc)] if r == 0 else ()))
    vd(lambda e: e.memset(t_acc[:, 0:2 * JC], 0.0))
    tok_acc = vd(lambda e: e.memset(t_acc[:, 2 * JC:3 * JC], NEG_BIG))
    aa, ab, am = t_acc[:, 0:JC], t_acc[:, JC:2 * JC], t_acc[:, 2 * JC:3 * JC]
    for r in range(NCORES):
        g_a = t_gath[:, r, 0:JC]
        g_b = t_gath[:, r, JC:2 * JC]
        g_m = t_gath[:, r, 2 * JC:3 * JC]
        vd(lambda e, g_m=g_m, r=r: e.tensor_scalar(
            out=t_cj1[:, :], in0=g_m, scalar1=t_mpre[:, r:r + 1],
            scalar2=t_mbig[:, r:r + 1], op0=ALU.mult, op1=ALU.add),
            waits=[gtoks[r]] + ([("s_V", tok_acc)] if r == 0 else []))
        vd(lambda e, r=r: e.tensor_scalar(out=t_cj2[:, :], in0=t_spanw[:, :],
                                          scalar1=t_mpre[:, r:r + 1], scalar2=None,
                                          op0=ALU.mult))
        vd(lambda e: e.tensor_tensor(out=t_cj2[:, :], in0=t_cj2[:, :], in1=am,
                                     op=ALU.add))
        vd(lambda e: e.tensor_tensor(out=am, in0=t_cj2[:, :], in1=t_cj1[:, :],
                                     op=ALU.max))
        tok_v = vd(lambda e: e.tensor_tensor(out=t_cj2[:, :], in0=t_cj2[:, :], in1=am,
                                             op=ALU.subtract))
        sd(lambda e: e.activation(out=t_cj2[:, :], in_=t_cj2[:, :], func=AF.Exp,
                                  bias=0.0), waits=[("s_V", tok_v)])
        tok_v = vd(lambda e: e.tensor_tensor(out=t_cj1[:, :], in0=t_cj1[:, :], in1=am,
                                             op=ALU.subtract))
        tok_s = sd(lambda e: e.activation(out=t_cj1[:, :], in_=t_cj1[:, :],
                                          func=AF.Exp, bias=0.0),
                   waits=[("s_V", tok_v)])
        vd(lambda e: e.tensor_tensor(out=aa, in0=aa, in1=t_cj2[:, :], op=ALU.mult),
           waits=[("s_S", tok_s)])
        vd(lambda e, g_a=g_a: e.tensor_tensor(out=t_cj3[:, :], in0=g_a,
                                              in1=t_cj1[:, :], op=ALU.mult))
        vd(lambda e: e.tensor_tensor(out=aa, in0=aa, in1=t_cj3[:, :], op=ALU.add))
        vd(lambda e: e.tensor_tensor(out=ab, in0=ab, in1=t_cj2[:, :], op=ALU.mult))
        vd(lambda e, g_b=g_b: e.tensor_tensor(out=t_cj3[:, :], in0=g_b,
                                              in1=t_cj1[:, :], op=ALU.mult))
        tok_acc = vd(lambda e: e.tensor_tensor(out=ab, in0=ab, in1=t_cj3[:, :],
                                               op=ALU.add))
    if debug:
        p.dma("gpsimd", sp_acc, t_acc[:, :], waits=[("s_V", tok_acc)])
        p.dma("gpsimd", dbg_sum, t_gath[:, :, :], waits=[("s_V", tok_acc)])
    tok_inc = [None] * NP
    for pp in range(NP):
        ro = t_run[pp]
        a_o2, b_o2, m_o2 = ro[:, 0:JC], ro[:, JC:2 * JC], ro[:, 2 * JC:3 * JC]
        vd(lambda e, pp=pp: e.tensor_scalar(out=t_cj1[:, :], in0=t_tww[:, :],
                                            scalar1=float(pp), scalar2=None,
                                            op0=ALU.mult), waits=[("s_V", tok_acc)])
        vd(lambda e: e.tensor_tensor(out=t_cj1[:, :], in0=t_cj1[:, :], in1=am,
                                     op=ALU.add))
        vd(lambda e, m_o2=m_o2: e.tensor_tensor(out=t_cj3[:, :], in0=t_cj1[:, :],
                                                in1=m_o2, op=ALU.max))
        tok_v = vd(lambda e: e.tensor_tensor(out=t_cj1[:, :], in0=t_cj1[:, :],
                                             in1=t_cj3[:, :], op=ALU.subtract))
        sd(lambda e: e.activation(out=t_cj1[:, :], in_=t_cj1[:, :], func=AF.Exp,
                                  bias=0.0), waits=[("s_V", tok_v)])
        tok_v = vd(lambda e, m_o2=m_o2: e.tensor_tensor(out=t_cj2[:, :], in0=m_o2,
                                                        in1=t_cj3[:, :],
                                                        op=ALU.subtract))
        tok_s = sd(lambda e: e.activation(out=t_cj2[:, :], in_=t_cj2[:, :],
                                          func=AF.Exp, bias=0.0),
                   waits=[("s_V", tok_v)])
        ia = t_inc[pp][:, 0:JC]
        ibb = t_inc[pp][:, JC:2 * JC]
        vd(lambda e, ia=ia: e.tensor_tensor(out=ia, in0=aa, in1=t_cj1[:, :],
                                            op=ALU.mult), waits=[("s_S", tok_s)])
        vd(lambda e, ia=ia, a_o2=a_o2: e.tensor_tensor(out=t_cj3[:, :], in0=a_o2,
                                                       in1=t_cj2[:, :], op=ALU.mult))
        vd(lambda e, ia=ia: e.tensor_tensor(out=ia, in0=ia, in1=t_cj3[:, :],
                                            op=ALU.add))
        vd(lambda e, ibb=ibb: e.tensor_tensor(out=ibb, in0=ab, in1=t_cj1[:, :],
                                              op=ALU.mult))
        vd(lambda e, ibb=ibb, b_o2=b_o2: e.tensor_tensor(out=t_cj3[:, :], in0=b_o2,
                                                         in1=t_cj2[:, :],
                                                         op=ALU.mult))
        vd(lambda e, ibb=ibb: e.tensor_tensor(out=ibb, in0=ibb, in1=t_cj3[:, :],
                                              op=ALU.add))
        vd(lambda e, pp=pp: e.tensor_scalar(out=t_cj1[:, :], in0=t_tww[:, :],
                                            scalar1=float(pp), scalar2=None,
                                            op0=ALU.mult))
        vd(lambda e: e.tensor_tensor(out=t_cj1[:, :], in0=t_cj1[:, :], in1=am,
                                     op=ALU.add))
        vd(lambda e, m_o2=m_o2: e.tensor_tensor(out=t_cj1[:, :], in0=t_cj1[:, :],
                                                in1=m_o2, op=ALU.max))
        tok_v = vd(lambda e, pp=pp: e.tensor_tensor(out=t_cj1[:, :], in0=t_cj1[:, :],
                                                    in1=t_mk[pp][:, :],
                                                    op=ALU.subtract))
        tok_s = sd(lambda e: e.activation(out=t_cj1[:, :], in_=t_cj1[:, :],
                                          func=AF.Exp, bias=0.0),
                   waits=[("s_V", tok_v)])
        vd(lambda e, ia=ia: e.tensor_tensor(out=ia, in0=ia, in1=t_cj1[:, :],
                                            op=ALU.mult), waits=[("s_S", tok_s)])
        tok_inc[pp] = vd(lambda e, ibb=ibb: e.tensor_tensor(
            out=ibb, in0=ibb, in1=t_cj1[:, :], op=ALU.mult))
        if debug:
            p.dma("gpsimd", sp_inc[pp], t_inc[pp][:, :],
                  waits=[("s_V", tok_inc[pp])])

    srA_t = AR.view(rE, [JC, TW])
    xinA_t = AR.view(rB, [JC, TW])
    ekA_t = AR.view(rC, [JC, TW])
    Aarr = AR.view(rD, [JC, TW])
    rwkv_t = ABF.view(bR1, [JC, TW])
    par_tok = {0: list(prev["betafree"][0]), 1: list(prev["betafree"][1])}
    stA = {"ek": prev["ekfree"], "r": prev["rfree"], "xin": prev["xinfree"],
           "arrD": [], "rwkv": []}
    last_pass = [None]

    def stageA_front(pp, resident):
        par = pp % 2
        betaA_t = AR.view(rA0 if par == 0 else rA1, [JC, TW])
        Barr = AR.view(rA1 if par == 0 else rA0, [JC, TW])
        if resident:
            tok_l1 = tok_l2 = tok_l3 = None
            rel = []
        else:
            tok_l2 = p.dma("gpsimd", betaA_t, sp_beta[pp], waits=par_tok[par])
            tok_l1 = p.dma("gpsimd", ekA_t, sp_ek[pp], waits=stA["ek"])
            tok_l3 = p.dma("gpsimd", srA_t, sp_r[pp], waits=stA["r"])
            rel = [tok_l3]
        tok_sa = None
        for j2 in range(JC):
            w0 = []
            if j2 == 0:
                w0 = [("s_V", tok_inc[pp])] + stA["arrD"]
                if tok_l2 is not None:
                    w0.append(tok_l2)
            tok_sa = vi(lambda e, j=j2, pp=pp, betaA_t=betaA_t: e.tensor_tensor_scan(
                Aarr[:, j, :], ew_bcast(j, TW), betaA_t[:, j, :],
                t_inc[pp][:, j:j + 1], ALU.mult, ALU.add), waits=w0)
        for j2 in range(JC):
            w0 = list(par_tok[1 - par]) if j2 == 0 else []
            if j2 == 0 and tok_l1 is not None:
                w0.append(tok_l1)
            tok_sa = vi(lambda e, j=j2, pp=pp, Barr=Barr: e.tensor_tensor_scan(
                Barr[:, j, :], ew_bcast(j, TW), ekA_t[:, j, :],
                t_inc[pp][:, JC + j:JC + j + 1], ALU.mult, ALU.add), waits=w0)
        for j2 in range(JC):
            vi(lambda e, j=j2, betaA_t=betaA_t: e.scalar_tensor_tensor(
                out=betaA_t[:, j, 1:TW], in0=betaA_t[:, j, 1:TW],
                scalar=t_eu[:, j:j + 1], in1=Aarr[:, j, 0:TW - 1],
                op0=ALU.mult, op1=ALU.add),
                waits=[("s_V", tok_sa), ("s_S", tok_eu)] if j2 == 0 else ())
            vi(lambda e, j=j2, pp=pp, betaA_t=betaA_t: e.scalar_tensor_tensor(
                out=betaA_t[:, j, 0:1], in0=betaA_t[:, j, 0:1],
                scalar=t_eu[:, j:j + 1], in1=t_inc[pp][:, j:j + 1],
                op0=ALU.mult, op1=ALU.add))
        tok_nd = None
        for j2 in range(JC):
            vi(lambda e, j=j2, Barr=Barr: e.scalar_tensor_tensor(
                out=ekA_t[:, j, 1:TW], in0=ekA_t[:, j, 1:TW],
                scalar=t_eu[:, j:j + 1], in1=Barr[:, j, 0:TW - 1],
                op0=ALU.mult, op1=ALU.add))
            tok_nd = vi(lambda e, j=j2, pp=pp: e.scalar_tensor_tensor(
                out=ekA_t[:, j, 0:1], in0=ekA_t[:, j, 0:1],
                scalar=t_eu[:, j:j + 1], in1=t_inc[pp][:, JC + j:JC + j + 1],
                op0=ALU.mult, op1=ALU.add))
        par_tok[1 - par] = [("s_V", tok_nd)]
        stA["arrD"] = [("s_V", tok_nd)]
        ek_flat = ekA_t.rearrange("p a b -> p (a b)")
        tok_rc = vi(lambda e: e.reciprocal(out=ek_flat, in_=ek_flat),
                    waits=[("s_V", tok_nd)])
        tok_y = vi(lambda e, betaA_t=betaA_t: e.tensor_tensor(
            out=betaA_t.rearrange("p a b -> p (a b)"),
            in0=betaA_t.rearrange("p a b -> p (a b)"), in1=ek_flat, op=ALU.mult))
        stA["ek"] = [("s_V", tok_y)]
        drain_tasks(24)
        return dict(pp=pp, par=par, betaA_t=betaA_t, sr_rel=rel)

    def stageA_back(ctx, front_next):
        pp, par, betaA_t = ctx["pp"], ctx["par"], ctx["betaA_t"]
        tok_rwkv = vi(lambda e, betaA_t=betaA_t: e.tensor_tensor(
            out=rwkv_t.rearrange("p a b -> p (a b)"),
            in0=betaA_t.rearrange("p a b -> p (a b)"),
            in1=srA_t.rearrange("p a b -> p (a b)"), op=ALU.mult),
            waits=list(ctx["sr_rel"]) + stA["rwkv"])
        par_tok[par] = [("s_V", tok_rwkv)]
        stA["r"] = [("s_V", tok_rwkv)]
        views, vtoks = wo_res["views"], wo_res["vtoks"]
        mminfo = []
        lastmm = None
        for mc in range(JC):
            ib, bnk, wts = get_bank()
            for kc in range(JC):
                lastmm = ti(lambda e, bnk=bnk, v=views[kc], mc=mc, kc=kc: e.matmul(
                    bnk[:, 0:TW], v[:, mc * 128:(mc + 1) * 128], rwkv_t[:, kc, :],
                    start=(kc == 0), stop=(kc == JC - 1)),
                    waits=(wts + [vtoks[kc], ("s_V", tok_rwkv)]) if kc == 0
                    else ([vtoks[kc]] if vtoks[kc] is not vtoks[kc - 1] else ()))
            mminfo.append((ib, bnk, lastmm))
        stA["rwkv"] = [("s_T", lastmm)]
        wo_res["last"] = lastmm
        ctx_next = front_next() if front_next is not None else None
        tok_x1 = None
        for mc, (ib, bnk, mm) in enumerate(mminfo):
            tok_x1 = vi(lambda e, bnk=bnk, mc=mc: e.tensor_tensor(
                out=xinA_t[:, mc, :], in0=xinA_t[:, mc, :], in1=bnk[:, 0:TW],
                op=ALU.add),
                waits=[("s_T", mm)] + (stA["xinload"] if mc == 0 else []))
            bank_done(ib, [("s_V", tok_x1)])
            drain_tasks(5)
        tok_col = vi(lambda e, pp=pp: e.tensor_copy(t_x1c[pp][:, :],
                                                    xinA_t[:, :, TW - 1:TW]),
                     waits=[("s_V", tok_x1)])
        tok_spx1 = p.dma("scalar", sp_x1[pp], xinA_t, waits=[("s_V", tok_col)])
        stA["xin"] = [tok_spx1]
        return tok_spx1, tok_col, ctx_next

    orderA = [NP - 1] + list(range(NP - 1))
    tok_sp_x1 = {}
    stA["xinload"] = []
    wo_res = {"views": [], "vtoks": [], "last": None, "slots": []}
    _kc = 0
    _slot = 0
    while _kc < JC:
        _kh = min(JC, _kc + WR_SLOT // C)
        wts = list(wslot_tok[_slot]) + [wcast_done["wo"]]
        wslot_tok[_slot] = []
        _v = wslot[_slot][:, 0:(_kh - _kc) * C].rearrange(
            "p (a c) -> p a c", a=_kh - _kc, c=C)
        _tok = p.dma("sync", wslot[_slot][:, 0:(_kh - _kc) * C],
                     sc_w["wo"][:, _kc * C:_kh * C], waits=wts)
        for _i in range(_kh - _kc):
            wo_res["views"].append(_v[:, _i, :])
            wo_res["vtoks"].append(_tok)
        wo_res["slots"].append(_slot)
        _slot += 1
        _kc = _kh
    first_res = not debug
    ctxA = stageA_front(orderA[0], resident=first_res)
    if not first_res:
        stA["xinload"] = [p.dma("scalar", xinA_t, sp_xin[orderA[0]],
                                waits=stA["xin"])]
    for idx, pp in enumerate(orderA):
        nxt = orderA[idx + 1] if idx + 1 < NP else None
        fe = (lambda nxt=nxt: stageA_front(nxt, resident=False)) \
            if nxt is not None else None
        tok_spx1, tok_col, ctxA_next = stageA_back(ctxA, fe)
        tok_sp_x1[pp] = tok_spx1
        if nxt is not None:
            stA["xinload"] = [p.dma("scalar", xinA_t, sp_xin[nxt],
                                    waits=[tok_spx1])]
            ctxA = ctxA_next
        if nxt is None:
            slots_release(wo_res["slots"], [("s_T", wo_res["last"])])
            ring_rr[0] = 0
        if pp == NP - 1:
            tok_pub = p.dma("gpsimd", cc_row_in.ap(), t_x1c[NP - 1][:, :],
                            waits=[("s_V", tok_col)])
            tok_cc2 = p.emit("gpsimd", lambda e: e.collective_compute(
                "AllGather", ALU.bypass, replica_groups=[list(range(NCORES))],
                ins=[cc_row_in.ap().opt()], outs=[cc_row_out.ap().opt()]),
                inc="s_cc", amt=1, waits=[tok_pub])
            rgtoks = []
            for r in range(NCORES):
                rgtoks.append(p.dma("scalar", t_rgath[:, r, :], cc_row_out.ap()[r],
                              waits=[("s_cc", tok_cc2)] if r == 0 else ()))
        drain_tasks(6)
    drain_all_tasks()
    tok_sel = vd(lambda e: e.tensor_scalar(out=t_rowa[:, :], in0=t_rgath[:, 0, :],
                                           scalar1=t_mrow[:, 0:1], scalar2=None,
                                           op0=ALU.mult), waits=[rgtoks[0]])
    for r in range(1, NCORES):
        tok_sel = vd(lambda e, r=r: e.scalar_tensor_tensor(
            out=t_rowa[:, :], in0=t_rgath[:, r, :], scalar=t_mrow[:, r:r + 1],
            in1=t_rowa[:, :], op0=ALU.mult, op1=ALU.add), waits=[rgtoks[r]])
    row_ln(t_rowa[:, :], t_rowa[:, :], g=PV["ln2g"][:, :], b=PV["ln2b"][:, :],
           waits=[("s_V", tok_sel)])
    tok_gcol0 = vd(lambda e: e.tensor_scalar(out=t_gcol[:, :], in0=t_rowa[:, :],
                                             scalar1=t_pvalid[:, :], scalar2=None,
                                             op0=ALU.mult))

    x1B_par = [AR.view(rA0, [JC, TW]), AR.view(rA1, [JC, TW])]
    gB_t = AR.view(rB, [JC, TW])
    d2B_t = AR.view(rD, [JC, TW])
    sigB_t = AR.view(rC, [JC, TW])
    xB_t = AR.view(rE, [JC, TW])
    outB_t = AR.view(rE, [JC, TW])
    relu_st = AR.view(rS3, [TW])
    ffn_st = AR.view(rS3, [TW])
    gk_t = ABF.view(bR1, [JC, TW])
    gr_t = ABF.view(bR1 + TWJC, [JC, TW])
    kf_t = ABF.view(bR2, [JF, TW])
    h2_t = ABF.view(bR2, [JF, TW])
    stg3a = ABF.view(bR2, [JC, TW])
    stg3b = ABF.view(bR2 + TWJC, [JC, TW])
    xf_t = ABF.view(bR5, [JC, TW])
    xa_t = ABF.view(bR4, [TW])
    xa2_t = ABF.view(bR4 + TW, [TW])

    stB = {"x1p0": [], "x1p1": [], "g": [], "d2": [], "sig": [], "x": [],
           "out": [], "gk": [], "kf": [], "xf": [], "bfc": [], "bproj": [],
           "stage": [], "mfree": []}

    def stageB_pass(pp):
        x1B_t = x1B_par[pp % 2]
        gcol = t_gcol if pp == 0 else t_gc[pp - 1]
        gcol_tok = [("s_V", tok_gcol0)] if pp == 0 else stB[f"gcol{pp - 1}"]
        tok_lx1 = p.dma("gpsimd", x1B_t, sp_x1[pp],
                        waits=[tok_sp_x1[pp]] + stB[f"x1p{pp % 2}"])
        tok_rstd, _ = ln_stats(x1B_t, stg3a, stg3b, waits=[tok_lx1],
                               stage_free=stB["stage"], m_free=stB["mfree"])
        tok_g = ln_apply(x1B_t, gB_t, tok_rstd, g=PV["ln2g"], b=PV["ln2b"],
                         waits=stB["g"])
        tok_gc = vi(lambda e, pp=pp: e.tensor_copy(t_gc[pp][:, :],
                                                   gB_t[:, :, TW - 1:TW]),
                    waits=[("s_V", tok_g)])
        stB[f"gcol{pp}"] = [("s_V", tok_gc)]
        if debug:
            p.dma("gpsimd", sp_g[pp], gB_t, waits=[("s_V", tok_gc)])
        for j in range(JC):
            vi(lambda e, j=j: e.tensor_tensor(
                out=d2B_t[:, j, 1:TW], in0=gB_t[:, j, 1:TW], in1=gB_t[:, j, 0:TW - 1],
                op=ALU.subtract), waits=stB["d2"] if j == 0 else ())
            vd(lambda e, j=j, gcol=gcol: e.tensor_tensor(
                out=d2B_t[:, j, 0:1], in0=gB_t[:, j, 0:1], in1=gcol[:, j:j + 1],
                op=ALU.subtract), waits=gcol_tok if j == 0 else ())
        tok_mix = None
        for fm, xo in ((PV["fmk"], gk_t), (PV["fmr"], gr_t)):
            for j in range(JC):
                vi(lambda e, j=j, fm=fm, xo=xo: e.scalar_tensor_tensor(
                    out=xo[:, j, 1:TW], in0=d2B_t[:, j, 1:TW], scalar=fm[:, j:j + 1],
                    in1=gB_t[:, j, 0:TW - 1], op0=ALU.mult, op1=ALU.add),
                    waits=stB["gk"] if (j == 0 and xo is gk_t) else ())
                tok_mix = vd(lambda e, j=j, fm=fm, xo=xo, gcol=gcol:
                             e.scalar_tensor_tensor(
                    out=xo[:, j, 0:1], in0=d2B_t[:, j, 0:1], scalar=fm[:, j:j + 1],
                    in1=gcol[:, j:j + 1], op0=ALU.mult, op1=ALU.add))
        views, vtoks, slots = stream_chunks("fr", 0, JC)
        sig_toks = [None] * JC
        lastmm = None
        for mc in range(JC):
            ib, bnk, wts = get_bank()
            for kc in range(JC):
                lastmm = ti(lambda e, bnk=bnk, v=views[kc], mc=mc, kc=kc: e.matmul(
                    bnk[:, 0:TW], v[:, mc * 128:(mc + 1) * 128], gr_t[:, kc, :],
                    start=(kc == 0), stop=(kc == JC - 1)),
                    waits=(wts + [vtoks[kc], ("s_V", tok_mix)]) if kc == 0
                    else ([vtoks[kc]] if vtoks[kc] is not vtoks[kc - 1] else ()))
            tok_sg = si(lambda e, bnk=bnk, mc=mc: e.activation(
                out=sigB_t[:, mc, :], in_=bnk[:, 0:TW], func=AF.Sigmoid, bias=0.0),
                waits=[("s_T", lastmm)] + (stB["sig"] if mc == 0 else []))
            sig_toks[mc] = tok_sg
            bank_done(ib, [("s_S", tok_sg)])
        slots_release(slots, [("s_T", lastmm)])
        kf_toks = []
        NG = JF // 8
        for g8 in range(NG):
            views, vtoks, slots = stream_cols("fk", g8)
            for jfo in range(8):
                jf = g8 * 8 + jfo
                ib, bnk, wts = get_bank()
                for kc in range(JC):
                    lastmm = ti(lambda e, bnk=bnk, v=views[kc], jfo=jfo, kc=kc:
                                e.matmul(
                        bnk[:, 0:TW], v[:, jfo * 128:(jfo + 1) * 128],
                        gk_t[:, kc, :], start=(kc == 0), stop=(kc == JC - 1)),
                        waits=(wts + [vtoks[kc]]) if (kc == 0 and jfo == 0)
                        else ([vtoks[kc]] if jfo == 0 else ()))
                tok_rl = si(lambda e, bnk=bnk: e.activation(
                    out=relu_st[:, :], in_=bnk[:, 0:TW], func=AF.Relu, bias=0.0),
                    waits=[("s_T", lastmm)] + (stB["kf"] if jf == 0 else []))
                tok_kf = si(lambda e, jf=jf: e.activation(
                    out=kf_t[:, jf, :], in_=relu_st[:, :], func=AF.Square, bias=0.0))
                kf_toks.append(tok_kf)
                bank_done(ib, [("s_S", tok_kf)])
            slots_release(slots, [("s_T", lastmm)])
        tok_after_fk = ("s_T", p.counts["s_T"])
        banks, bwaits = get_all_banks()
        x2_toks = [None] * JC
        for g8 in range(NG):
            views, vtoks, slots = stream_chunks("fv", g8 * 8, g8 * 8 + 8)
            for kci in range(8):
                kc = g8 * 8 + kci
                for mc in range(JC):
                    w8 = []
                    if g8 == 0 and kci == 0 and mc == 0:
                        w8 = bwaits + [vtoks[kci], ("s_S", kf_toks[-1])]
                    elif mc == 0 and (kci == 0 or vtoks[kci] is not vtoks[kci - 1]):
                        w8 = [vtoks[kci]]
                    lastmm = ti(lambda e, v=views[kci], mc=mc, kc=kc: e.matmul(
                        pbank[mc][:, 0:TW], v[:, mc * 128:(mc + 1) * 128],
                        kf_t[:, kc, :], start=(kc == 0), stop=(kc == JF - 1)),
                        waits=w8)
            slots_release(slots, [("s_T", lastmm)])
        for mc in range(JC):
            vi(lambda e, mc=mc: e.tensor_tensor(
                out=sigB_t[:, mc, :], in0=sigB_t[:, mc, :], in1=pbank[mc][:, 0:TW],
                op=ALU.mult), waits=[("s_T", lastmm)] if mc == 0 else ())
            tok_x2 = vi(lambda e, mc=mc: e.tensor_tensor(
                out=x1B_t[:, mc, :], in0=x1B_t[:, mc, :], in1=sigB_t[:, mc, :],
                op=ALU.add))
            x2_toks[mc] = tok_x2
            bank_done(mc, [("s_V", tok_x2)])
        tok_lx = p.dma("gpsimd", xB_t, xin[pp], waits=stB["x"])
        tok_xa = vi(lambda e: e.tensor_tensor(
            out=x1B_t.rearrange("p a b -> p (a b)"),
            in0=x1B_t.rearrange("p a b -> p (a b)"),
            in1=xB_t.rearrange("p a b -> p (a b)"), op=ALU.add),
            waits=[tok_lx, ("s_V", x2_toks[-1])])
        if debug:
            p.dma("gpsimd", sp_xaf[pp], x1B_t, waits=[("s_V", tok_xa)])
            p.dma("gpsimd", sp_sig[pp], sigB_t, waits=[("s_V", x2_toks[-1])])
        tok_rstd, tok_stg3 = ln_stats(x1B_t, stg3a, stg3b,
                                      waits=[("s_V", tok_xa),
                                             ("s_S", kf_toks[-1])])
        tok_xf = ln_apply(x1B_t, xf_t, tok_rstd, waits=stB["xf"], scratch=d2B_t)
        ib, bnk, wts = get_bank()
        for kc in range(JC):
            lastmm = ti(lambda e, bnk=bnk, kc=kc: e.matmul(
                bnk[0:LORA, 0:TW], afcb[:, kc, :], xf_t[:, kc, :],
                start=(kc == 0), stop=(kc == JC - 1)),
                waits=(wts + [("s_V", tok_xf)]) if kc == 0 else ())
        tok_xa1 = si(lambda e, bnk=bnk: e.activation(
            out=xa_t[0:LORA, :], in_=bnk[0:LORA, 0:TW], func=AF.Identity,
            bias=0.0, scale=2.0), waits=[("s_T", lastmm)] + stB["bfc"])
        bank_done(ib, [("s_S", tok_xa1)])
        tok_bfc = p.dma("sync", bfcb[:, :], sc_bfc,
                        waits=[wcast_done["bfc"]] + stB["bfc"])
        h2_toks = []
        for g8 in range(NG):
            views, vtoks, slots = stream_cols("wfc", g8)
            for jfo in range(8):
                jf = g8 * 8 + jfo
                ib, bnk, wts = get_bank()
                for kc in range(JC):
                    ti(lambda e, bnk=bnk, v=views[kc], jfo=jfo, kc=kc: e.matmul(
                        bnk[:, 0:TW], v[:, jfo * 128:(jfo + 1) * 128],
                        xf_t[:, kc, :], start=(kc == 0), stop=False),
                        waits=(wts + [vtoks[kc]]) if (kc == 0 and jfo == 0)
                        else ([vtoks[kc]] if jfo == 0 else ()))
                lastmm = ti(lambda e, bnk=bnk, jf=jf: e.matmul(
                    bnk[:, 0:TW], bfcb[:, jf * 128:(jf + 1) * 128], xa_t[0:LORA, :],
                    start=False, stop=True),
                    waits=[tok_bfc, ("s_S", tok_xa1)]
                    if (g8 == 0 and jfo == 0) else ())
                tok_h2 = si(lambda e, bnk=bnk, jf=jf: e.activation(
                    out=h2_t[:, jf, :], in_=bnk[:, 0:TW], func=AF.Gelu,
                    bias=t_bfcv[:, jf:jf + 1], scale=1.0),
                    waits=[("s_T", lastmm)])
                h2_toks.append(tok_h2)
                bank_done(ib, [("s_S", tok_h2)])
            slots_release(slots, [("s_T", lastmm)])
        tok_after_wfc = ("s_T", p.counts["s_T"])
        if debug:
            p.dma("gpsimd", sp_xf[pp], xf_t, waits=[("s_V", tok_xf)])
            p.dma("gpsimd", sp_h2[pp], h2_t, waits=[("s_S", h2_toks[-1])])
        ib, bnk, wts = get_bank()
        for kc in range(JF):
            lastmm = ti(lambda e, bnk=bnk, kc=kc: e.matmul(
                bnk[0:LORA, 0:TW], aprojb[:, kc, :], h2_t[:, kc, :],
                start=(kc == 0), stop=(kc == JF - 1)),
                waits=(wts + [("s_S", h2_toks[-1])]) if kc == 0 else ())
        tok_xa2 = si(lambda e, bnk=bnk: e.activation(
            out=xa2_t[0:LORA, :], in_=bnk[0:LORA, 0:TW], func=AF.Identity,
            bias=0.0, scale=2.0), waits=[("s_T", lastmm)])
        bank_done(ib, [("s_S", tok_xa2)])
        tok_bproj = p.dma("sync", bprojb[:, :], sc_bproj,
                          waits=[wcast_done["bproj"]] + stB["bproj"])
        banks, bwaits = get_all_banks()
        for g8 in range(NG):
            views, vtoks, slots = stream_chunks("wproj", g8 * 8, g8 * 8 + 8)
            for kci in range(8):
                kc = g8 * 8 + kci
                for mc in range(JC):
                    w8 = []
                    if g8 == 0 and kci == 0 and mc == 0:
                        w8 = bwaits + [vtoks[kci]]
                    elif mc == 0 and (kci == 0 or vtoks[kci] is not vtoks[kci - 1]):
                        w8 = [vtoks[kci]]
                    lastmm = ti(lambda e, v=views[kci], mc=mc, kc=kc: e.matmul(
                        pbank[mc][:, 0:TW], v[:, mc * 128:(mc + 1) * 128],
                        h2_t[:, kc, :], start=(kc == 0), stop=False),
                        waits=w8)
            slots_release(slots, [("s_T", lastmm)])
        for mc in range(JC):
            lastmm = ti(lambda e, mc=mc: e.matmul(
                pbank[mc][:, 0:TW], bprojb[:, mc * 128:(mc + 1) * 128],
                xa2_t[0:LORA, :], start=False, stop=True),
                waits=[tok_bproj, ("s_S", tok_xa2)] if mc == 0 else ())
        tok_out = None
        prev_out = [None, None]
        fst = [ffn_st, AR.view(rS1, [TW])]
        for mc in range(JC):
            fb = fst[mc % 2]
            w9 = [("s_T", lastmm)] + (stB["out"] if mc == 0 else [])
            if prev_out[mc % 2] is not None:
                w9.append(("s_V", prev_out[mc % 2]))
            tok_f = si(lambda e, mc=mc, fb=fb: e.activation(
                out=fb[:, :], in_=pbank[mc][:, 0:TW], func=AF.Identity,
                bias=t_bprojv[:, mc:mc + 1], scale=1.0), waits=w9)
            tok_out = vi(lambda e, mc=mc, fb=fb: e.tensor_tensor(
                out=outB_t[:, mc, :], in0=x1B_t[:, mc, :], in1=fb[:, :],
                op=ALU.add), waits=[("s_S", tok_f)])
            prev_out[mc % 2] = tok_out
            bank_done(mc, [("s_V", tok_out)])
        tok_o = p.dma("sync", out[pp], outB_t, waits=[("s_V", tok_out)])
        stB[f"x1p{pp % 2}"] = [("s_V", tok_out)]
        stB["g"] = [("s_V", tok_mix)]
        stB["d2"] = [("s_V", tok_xf)]
        stB["sig"] = [("s_V", x2_toks[-1])]
        stB["x"] = [("s_V", tok_xa), tok_o]
        stB["out"] = [tok_o]
        stB["gk"] = [tok_after_fk]
        stB["kf"] = [("s_S", h2_toks[-1] if h2_toks else kf_toks[-1])]
        stB["xf"] = [tok_after_wfc]
        stB["bfc"] = [("s_T", p.counts["s_T"])]
        stB["bproj"] = [("s_T", p.counts["s_T"])]
        stB["stage"] = [tok_stg3]
        stB["mfree"] = [("s_V", tok_xf)]
        return tok_o

    vnow = [("s_V", p.counts["s_V"])]
    snow = [("s_S", p.counts["s_S"])]
    tnow = [("s_T", p.counts["s_T"])]
    stB["x1p0"] = vnow
    stB["x1p1"] = vnow

    stB["g"] = vnow + [tok_sp_x1[pp2] for pp2 in range(NP)]
    stB["d2"] = vnow
    stB["sig"] = vnow + snow
    stB["x"] = vnow
    stB["out"] = vnow
    stB["kf"] = vnow
    stB["gk"] = vnow + tnow

    tok_os = []
    for pp in range(NP):
        tok_os.append(stageB_pass(pp))

    p.emit("sync", lambda e: None, waits=tok_os)

    with nc.Block() as block:
        @block.sync
        def _(eng):
            for ev in p.events["sync"]:
                ev(eng)

        @block.tensor
        def _(eng):
            for ev in p.events["tensor"]:
                ev(eng)

        @block.vector
        def _(eng):
            for ev in p.events["vector"]:
                ev(eng)

        @block.scalar
        def _(eng):
            for ev in p.events["scalar"]:
                ev(eng)

        @block.gpsimd
        def _(eng):
            for ev in p.events["gpsimd"]:
                ev(eng)

    es.close()
    return nc


def _featmaj_vec(v, j):
    return np.ascontiguousarray(np.asarray(v, dtype=np.float32).reshape(j, 128).T)


def _prep_inputs(inputs, cfg):
    C, DF, NP, TW, LORA = cfg["C"], cfg["DF"], cfg["NP"], cfg["TW"], cfg["LORA"]
    JC, JF = C // 128, DF // 128
    TC = NP * TW
    QPS = NCORES // 2
    g = lambda k: np.asarray(inputs[k], dtype=np.float32)

    x = g("L0_attn_x")
    B, T, _ = x.shape
    assert T == QPS * TC and B == 2

    shared = {}
    wT = lambda a: np.asarray(a, np.float32).T

    def pmaj(a, kcn, m):
        return np.ascontiguousarray(
            a.reshape(kcn, 128, m).transpose(1, 0, 2).reshape(128, kcn * m))

    def pmaj_grp(a, kcn, m, gw=1024):
        ng = m // gw
        return np.ascontiguousarray(
            a.reshape(kcn, 128, ng, gw).transpose(1, 2, 0, 3).reshape(
                128, ng * kcn * gw))

    shared["wk"] = pmaj(wT(g("L0_attn_Wk")), JC, C)
    shared["wv"] = pmaj(wT(g("L0_attn_Wv")), JC, C)
    shared["wr"] = pmaj(wT(g("L0_attn_Wr")), JC, C)
    shared["wo"] = pmaj(wT(g("L0_attn_Wo")), JC, C)
    shared["fr"] = pmaj(wT(g("L0_attn_Fr")), JC, C)
    shared["fk"] = pmaj_grp(wT(g("L0_attn_Fk")), JC, DF)
    shared["fv"] = pmaj(wT(g("L0_attn_Fv")), JF, C)
    shared["wfc"] = pmaj_grp(wT(g("L0_ffn_W_fc")), JC, DF)
    shared["wproj"] = pmaj(wT(g("L0_ffn_W_proj")), JF, C)
    shared["afc"] = pmaj(wT(g("L0_ffn_A_fc")), JC, LORA)
    shared["aproj"] = pmaj(wT(g("L0_ffn_A_proj")), JF, LORA)
    shared["bfc"] = np.ascontiguousarray(wT(g("L0_ffn_B_fc")))
    shared["bproj"] = np.ascontiguousarray(wT(g("L0_ffn_B_proj")))
    shared["ln1g"] = _featmaj_vec(g("L0_attn_ln1_g"), JC)
    shared["ln1b"] = _featmaj_vec(g("L0_attn_ln1_b"), JC)
    shared["ln2g"] = _featmaj_vec(g("L0_attn_ln2_g"), JC)
    shared["ln2b"] = _featmaj_vec(g("L0_attn_ln2_b"), JC)
    shared["tmk"] = _featmaj_vec(g("L0_attn_tm_k"), JC)
    shared["tmv"] = _featmaj_vec(g("L0_attn_tm_v"), JC)
    shared["tmr"] = _featmaj_vec(g("L0_attn_tm_r"), JC)
    shared["fmk"] = _featmaj_vec(g("L0_attn_fm_k"), JC)
    shared["fmr"] = _featmaj_vec(g("L0_attn_fm_r"), JC)
    shared["tdecay"] = _featmaj_vec(g("L0_attn_time_decay"), JC)
    shared["tfirst"] = _featmaj_vec(g("L0_attn_time_first"), JC)
    shared["bfcv"] = _featmaj_vec(g("L0_ffn_b_fc"), JF)
    shared["bprojv"] = _featmaj_vec(g("L0_ffn_b_proj"), JC)

    in_maps = []
    for core in range(NCORES):
        s, q = core // QPS, core % QPS
        xs = x[s, q * TC:(q + 1) * TC, :]
        xs = np.ascontiguousarray(
            xs.reshape(NP, TW, JC, 128).transpose(0, 3, 2, 1))
        if q > 0:
            xp = np.ascontiguousarray(x[s, q * TC - 1, :].reshape(JC, 128).T)
            pvalid = np.ones((128, 1), np.float32)
        else:
            xp = np.zeros((128, JC), np.float32)
            pvalid = np.zeros((128, 1), np.float32)
        mpre = np.zeros(NCORES, np.float32)
        for r in range(NCORES):
            if r // QPS == s and r % QPS < q:
                mpre[r] = 1.0
        mrow = np.zeros(NCORES, np.float32)
        if q > 0:
            mrow[core - 1] = 1.0
        m = dict(shared)
        m["xin"] = xs
        m["xprev"] = xp
        m["prevvalid"] = pvalid
        m["maskpre"] = np.ascontiguousarray(np.broadcast_to(mpre, (128, NCORES)))
        m["maskbig"] = np.ascontiguousarray(
            np.broadcast_to((mpre - 1.0) * 1e30, (128, NCORES)).astype(np.float32))
        m["maskrow"] = np.ascontiguousarray(np.broadcast_to(mrow, (128, NCORES)))
        in_maps.append(m)
    return in_maps


def run(inputs, cfg=FULL_CFG, debug=False):
    C, NP, TW = cfg["C"], cfg["NP"], cfg["TW"]
    JC = C // 128
    TC = NP * TW
    QPS = NCORES // 2
    nc = build_program(cfg, debug=debug)
    in_maps = _prep_inputs(inputs, cfg)
    res = run_bass_kernel_spmd(nc, in_maps, core_ids=list(range(NCORES)))
    T = QPS * TC
    outx = np.empty((2, T, C), dtype=np.float32)
    for core in range(NCORES):
        s, q = core // QPS, core % QPS
        o = np.asarray(res.results[core]["out"])
        outx[s, q * TC:(q + 1) * TC, :] = o.transpose(0, 3, 2, 1).reshape(TC, C)
    return outx, res


def kernel(**inputs):
    outx, _ = run(inputs, FULL_CFG, debug=False)
    return outx
